# version 11
# speedup vs baseline: 1.0356x; 1.0116x over previous
import os
import numpy as np
import ml_dtypes
from contextlib import ExitStack

import concourse.bass as bass
import concourse.mybir as mybir

F32 = mybir.dt.float32
BF16 = mybir.dt.bfloat16
F16 = mybir.dt.float16
AF = mybir.ActivationFunctionType
ALU = mybir.AluOpType
AX = mybir.AxisListType

NCORES = 8
T = 2048
D = 1024
NCH = 8
NTOK = 16
NSUP = 4
SUP = 512
TILE = 128

ENGINES = ("sync", "tensor", "vector", "scalar", "gpsimd")

_TWOPI = 2.0 * np.pi
_C1V = float(np.float32(6.28125))
_C2V = float(np.float32(_TWOPI - _C1V))
_C3V = float(np.float32(_TWOPI - _C1V - float(np.float32(_TWOPI - _C1V))))
_PI_LO = float(np.nextafter(np.float32(np.pi), np.float32(0)))
_C1H = _C1V / 2.0
_C2H = _C2V / 2.0
_C3H = _C3V / 2.0
PHASES = os.environ.get("KPHASES", "full")


class Item:
    _seq = 0

    def __init__(self, engine, fn, waits, tag=""):
        self.engine = engine
        self.fn = fn
        self.waits = waits
        self.needs_inc = False
        self.inc_val = None
        self.tag = tag
        self.stream = None
        Item._seq += 1
        self.seq = Item._seq


def compact_toks(toks):
    best = {}
    for t in toks:
        if t[0] == "e":
            k, s = ("e", t[1].engine), t[1].seq
        else:
            k, s = ("d", id(t[1])), t[2]
        cur = best.get(k)
        if cur is None or cur[0] < s:
            best[k] = (s, t)
    return [t for (_, t) in best.values()]

class Stream:
    def __init__(self, engine, name):
        self.engine = engine
        self.name = name
        self.sem = None
        self.n = 0

class Prog:
    def __init__(self):
        self.items = []
        self.streams = []

    def stream(self, engine, name):
        s = Stream(engine, name)
        self.streams.append(s)
        return s

    def emit(self, engine, fn, waits=(), tag=""):
        it = Item(engine, fn, list(waits), tag)
        self.items.append(it)
        return ("e", it)

    def build(self, nc, es):
        kstop = int(os.environ.get("KSTOP", "0"))
        if kstop:
            self.items = self.items[:kstop]
        self_sems = {e: es.enter_context(nc.semaphore(f"ss_{e}")) for e in ENGINES}
        for s in self.streams:
            s.sem = es.enter_context(nc.semaphore(f"dm_{s.name}"))
        for it in self.items:
            for tok in it.waits:
                if tok[0] == "e":
                    tok[1].needs_inc = True
        cnt = {e: 0 for e in ENGINES}
        for it in self.items:
            if it.stream is not None:
                continue
            if it.needs_inc:
                cnt[it.engine] += 1
                it.inc_val = cnt[it.engine]
        for e in ENGINES:
            assert cnt[e] < 60000, f"sem overflow {e}: {cnt[e]}"
        per_eng = {e: [it for it in self.items if it.engine == e] for e in ENGINES}
        watermark = {e: {} for e in ENGINES}

        def tok_sem_val(tok):
            if tok[0] == "e":
                it = tok[1]
                assert it.inc_val is not None, f"untracked token for {it.tag}"
                return self_sems[it.engine], it.inc_val
            _, s, n = tok
            return s.sem, 16 * n

        block = es.enter_context(nc.Block())

        def run_engine(eng_obj, ename):
            wm = watermark[ename]
            for it in per_eng[ename]:
                for tok in it.waits:
                    sem, val = tok_sem_val(tok)
                    key = id(sem)
                    if wm.get(key, -1) >= val:
                        continue
                    eng_obj.wait_ge(sem, val)
                    wm[key] = val
                inst = it.fn(eng_obj)
                if it.tag:
                    try:
                        inst.annotate(it.tag)
                    except Exception:
                        pass
                if it.stream is not None:
                    inst.then_inc(it.stream.sem, 16)
                elif it.inc_val is not None:
                    inst.then_inc(self_sems[it.engine], 1)
        block.sync(lambda e: run_engine(e, "sync"))
        block.tensor(lambda e: run_engine(e, "tensor"))
        block.vector(lambda e: run_engine(e, "vector"))
        block.scalar(lambda e: run_engine(e, "scalar"))
        block.gpsimd(lambda e: run_engine(e, "gpsimd"))


class Reg:
    def __init__(self, t, nslots=1):
        self.t = t
        self.w = [[] for _ in range(nslots)]
        self.r = [[] for _ in range(nslots)]

    def _sl(self, sl):
        if sl is None:
            return range(len(self.w))
        if isinstance(sl, int):
            return (sl,)
        return sl

    def rd(self, sl=None):
        out = []
        for s in self._sl(sl):
            out += self.w[s]
        return out

    def wr(self, sl=None):
        out = []
        for s in self._sl(sl):
            out += self.w[s] + self.r[s]
        return out

    @staticmethod
    def _app(lst, tok):
        key = ("e", tok[1].engine) if tok[0] == "e" else ("d", id(tok[1]))
        for i, t in enumerate(lst):
            k2 = ("e", t[1].engine) if t[0] == "e" else ("d", id(t[1]))
            if k2 == key:
                lst[i] = tok
                return
        lst.append(tok)

    def did_read(self, tok, sl=None):
        for s in self._sl(sl):
            self._app(self.r[s], tok)

    def did_write(self, tok, sl=None):
        for s in self._sl(sl):
            self._app(self.w[s], tok)
            self.r[s] = []


class Banks:
    def __init__(self, nc, es):
        self.b = [es.enter_context(nc.psum_tensor(f"pb{i}", [128, 512], F32))
                  for i in range(8)]
        self.reg = [Reg(t) for t in self.b]
        self.rr = 0

    def __post(self):
        pass

    def take(self, n=1):
        if os.environ.get("KBANKLOG"):
            import traceback
            ln = traceback.extract_stack()[-2].lineno
            print("take", n, "rr", self.rr, "line", ln)
        out = []
        while len(out) < n:
            i = self.rr % 8
            self.rr += 1
            if i in getattr(self, "reserved", set()):
                continue
            out.append((self.b[i], self.reg[i]))
        return out if n > 1 else out[0]

    def reserve(self, n=1):
        if not hasattr(self, "reserved"):
            self.reserved = set()
        out = []
        for i in range(8):
            if len(out) >= n:
                break
            if i not in self.reserved:
                self.reserved.add(i)
                out.append((self.b[i], self.reg[i]))
        return out

    def release_all(self):
        self.reserved = set()



def _f32c(a):
    return np.ascontiguousarray(np.asarray(a, dtype=np.float32))

def host_prepare(inputs):
    d = {}
    Wa = _f32c(inputs["L0_attn_W_attn"])
    d["w_l0q"] = _f32c(Wa[:, 0:1024])
    d["w_l0k"] = _f32c(Wa[:, 1024:2048])
    d["w_l0v"] = _f32c(Wa[:, 2048:3072])
    d["w_l0p"] = _f32c(inputs["L0_attn_W_proj"])
    d["w_l0f"] = _f32c(np.asarray(inputs["L0_ffn_W"]).T)
    d["w_l0vr"] = _f32c(np.asarray(inputs["L0_ffn_V_r"]).T)
    d["w_l0c"] = _f32c(np.asarray(inputs["L0_ffn_C"]).T)
    d["b_l0f"] = _f32c(inputs["L0_ffn_b"])
    perm_eo = np.concatenate([np.arange(0, D, 2), np.arange(1, D, 2)])
    d["w_l1q"] = _f32c(np.asarray(inputs["L1_attn_w_q"]).T[:, perm_eo])
    d["w_l1k"] = _f32c(np.asarray(inputs["L1_attn_w_k"]).T[:, perm_eo])
    d["w_l1v"] = _f32c(np.asarray(inputs["L1_attn_w_v"]).T)
    wqkv = np.asarray(inputs["L1_ffn_wqkv"])
    for l in range(2):
        d[f"w_f{l}q"] = _f32c(wqkv[l][:, 0:512])
        d[f"w_f{l}k"] = _f32c(wqkv[l][:, 512:1024])
        d[f"w_f{l}v"] = _f32c(wqkv[l][:, 1024:1536])
        wol = np.asarray(inputs["L1_ffn_wo"])[l]
        d[f"w_o{l}"] = _f32c(np.concatenate(
            [wol[64 * h:64 * (h + 1), :] for h in range(8)], axis=1))
    HID, HIDP = 2730, 2816
    w1 = np.asarray(inputs["L1_ffn_w1"])
    b1 = np.asarray(inputs["L1_ffn_b1"])
    w2 = np.asarray(inputs["L1_ffn_w2"])
    for l in range(2):
        a_w = np.zeros((D, HIDP), np.float32); a_w[:, :HID] = w1[l][:, :HID]
        g_w = np.zeros((D, HIDP), np.float32); g_w[:, :HID] = w1[l][:, HID:]
        inter = np.empty((D, 2 * HIDP), np.float32)
        for j in range(HIDP // 128):
            inter[:, 256 * j:256 * j + 128] = a_w[:, 128 * j:128 * j + 128]
            inter[:, 256 * j + 128:256 * j + 256] = g_w[:, 128 * j:128 * j + 128]
        w1p = np.empty((22 * 128, 2048), np.float32)
        for j in range(22):
            for cch in range(8):
                w1p[128 * j:128 * (j + 1), 256 * cch:256 * (cch + 1)] = \
                    inter[128 * cch:128 * (cch + 1), 256 * j:256 * (j + 1)]
        d[f"w_1_{l}"] = _f32c(w1p)
        a_b = np.zeros((HIDP,), np.float32); a_b[:HID] = b1[l][:HID]
        g_b = np.zeros((HIDP,), np.float32); g_b[:HID] = b1[l][HID:]
        d[f"b_1a_{l}"] = a_b
        d[f"b_1g_{l}"] = g_b
        w2p = np.zeros((HIDP, D), np.float32); w2p[:HID] = w2[l]
        d[f"w_2_{l}"] = _f32c(w2p)
        d[f"b_2_{l}"] = _f32c(np.asarray(inputs["L1_ffn_b2"])[l])
        d[f"ln_g_{l}"] = _f32c(np.asarray(inputs["L1_ffn_ln_g"])[l])
        d[f"ln_b_{l}"] = _f32c(np.asarray(inputs["L1_ffn_ln_b"])[l])
        d[f"qs_{l}"] = _f32c(np.asarray(inputs["L1_ffn_q_scale"])[l])
        d[f"ks_{l}"] = _f32c(np.asarray(inputs["L1_ffn_k_scale"])[l])
    d["c_id_f"] = np.eye(128, dtype=np.float32)
    ff = np.arange(512)[None, :]; pp = np.arange(128)[:, None]
    d["c_trij"] = np.concatenate(
        [(ff >= 128 * j + pp).astype(ml_dtypes.bfloat16) for j in range(4)], axis=1)
    f2 = np.arange(128)[None, :]
    d["c_maskl"] = np.where(f2 <= pp, 0.0, -1e30).astype(np.float32)
    d["c_masku"] = np.where(f2 >= pp, 0.0, -1e30).astype(np.float32)
    d["c_ones_bf"] = np.ones((128, 512), ml_dtypes.bfloat16)
    d["c_ones_f"] = np.ones((128, 128), np.float32)
    twopi = 2.0 * np.pi
    c1 = np.float32(6.28125)
    c2 = np.float32(twopi - float(c1))
    c3 = np.float32(twopi - float(c1) - float(c2))
    pi_lo = np.nextafter(np.float32(np.pi), np.float32(0))
    cc = np.zeros((128, 16), np.float32)
    cc[:, 0] = -c1; cc[:, 1] = -c2; cc[:, 2] = -c3
    cc[:, 3] = pi_lo; cc[:, 4] = -pi_lo
    d["c_cols"] = cc
    d["c_pos"] = np.arange(512, dtype=np.float32)[None, :]
    f32 = np.float32
    _hi = np.arange(0, 64, 2, dtype=f32)
    _invf = (f32(1.0) / np.power(f32(10000.0), _hi / f32(64.0), dtype=f32))
    _tokf = np.arange(2048, dtype=f32)
    _fr = (_tokf[:, None] * _invf[None, :]).astype(f32)
    _cosv = np.cos(_fr)
    _sinv = np.sin(_fr)
    _base = ((_hi + f32(25.6)) / f32(89.6)).astype(f32)
    _pw = ((_tokf - f32(1024.0)) / f32(256.0)).astype(f32)
    _scl = np.power(_base[None, :], _pw[:, None], dtype=f32)
    _lay = lambda a: np.ascontiguousarray(
        a.reshape(16, 128, 32).transpose(1, 0, 2).reshape(128, 512))
    d["c_xtcos"] = _lay(_cosv)
    d["c_xtsin"] = _lay(_sinv)
    d["c_xtscl"] = _lay(_scl)
    gg2 = (np.arange(128)[:, None] + 128 * np.arange(4)[None, :]).astype(np.float64)
    d["c_expthc"] = (-(gg2 - 1.0) / 512.0).astype(np.float32)
    gg = np.arange(512).astype(np.float64)
    d["c_ethrow"] = ((-(gg - 1.0)) / 512.0).astype(np.float32)[None, :]
    xr = np.zeros((1, 128), np.float32)
    ii = np.arange(32).astype(np.float64)
    xr[0, 0:32] = (-ii / 32.0).astype(np.float32)
    xr[0, 32:64] = (10.0 * ii + 128.0).astype(np.float32)
    d["c_xrows"] = xr
    eh = np.zeros((1, 256), np.float32)
    eh[0, 0:64] = 1.0
    eh[0, 128 + 64:128 + 128] = 1.0
    d["c_eh"] = eh
    d["c_p256"] = (np.arange(128) / 256.0).astype(np.float32)[None, :]
    x = np.asarray(inputs["L0_attn_x"], dtype=np.float32)
    xs = [np.ascontiguousarray(x[b].T) for b in range(NCORES)]
    return d, xs



ARENA_BASE = 16512

class Mem:
    def __init__(self, nc):
        self.nc = nc
        self.limit = 229344
        self.ranges = []

    def at(self, name, off, shape, dtype, phases=("all",), nslots=1):
        nbytes = int(np.prod(shape[1:])) * mybir.dt.size(dtype)
        end = off + nbytes
        assert end <= self.limit, f"{name} overflows SBUF: {off}+{nbytes}={end}"
        for (o2, e2, n2, ph2) in self.ranges:
            if off < e2 and o2 < end:
                if "all" in phases or "all" in ph2 or set(phases) & set(ph2):
                    raise AssertionError(f"overlap {name}{phases} with {n2}{ph2}")
        self.ranges.append((off, end, name, phases))
        t = self.nc.alloc_sbuf_tensor_at(name, list(shape), dtype, offset=off)
        return Reg(t, nslots)


def build_program():
    nc = bass.Bass("TRN2", target_bir_lowering=False)
    p = Prog()
    es = ExitStack()

    def din(name, shape, dtype=F32):
        return nc.dram_tensor(name, list(shape), dtype, kind="ExternalInput").ap()

    x_in = din("x", [D, T])
    out_t = nc.dram_tensor("out", [D, T], F32, kind="ExternalOutput").ap()
    hbm_x1 = nc.dram_tensor("hx1", [D, T], F32, kind="Internal").ap()
    hbm_g = nc.dram_tensor("hg", [D, T], F32, kind="Internal").ap()
    if os.environ.get("KDBG"):
        dbg_mid = [nc.dram_tensor(f"dbg_mid{l}", [D, T], F32,
                                  kind="Internal").ap() for l in range(2)]
        dbg_out = [nc.dram_tensor(f"dbg_out{l}", [D, T], F32,
                                  kind="Internal").ap() for l in range(2)]
        dbg_hn = nc.dram_tensor("dbg_hn", [128, 8 * 512], BF16,
                                kind="Internal").ap()
        dbg_h2 = nc.dram_tensor("dbg_h2", [128, 22 * 512], BF16,
                                kind="Internal").ap()
        dbg_xn1 = nc.dram_tensor("dbg_xn1", [128, 8 * T], BF16,
                                 kind="Internal").ap()
        dbg_vtb = nc.dram_tensor("dbg_vtb", [128, 16 * D], BF16,
                                 kind="Internal").ap()
        dbg_kr = nc.dram_tensor("dbg_kr", [128, 8 * T], BF16,
                                kind="Internal").ap()
        dbg_qr = nc.dram_tensor("dbg_qr", [128, 4096], BF16,
                                kind="Internal").ap()

    w_l0q = din("w_l0q", [D, D]); w_l0k = din("w_l0k", [D, D])
    w_l0v = din("w_l0v", [D, D]); w_l0p = din("w_l0p", [D, D])
    w_l0f = din("w_l0f", [D, D]); w_l0vr = din("w_l0vr", [32, D])
    w_l0c = din("w_l0c", [32, D]); b_l0f = din("b_l0f", [D])
    w_l1q = din("w_l1q", [D, D]); w_l1k = din("w_l1k", [D, D])
    w_l1v = din("w_l1v", [D, D])
    w_fq = [din(f"w_f{l}q", [D, 512]) for l in range(2)]
    w_fk = [din(f"w_f{l}k", [D, 512]) for l in range(2)]
    w_fv = [din(f"w_f{l}v", [D, 512]) for l in range(2)]
    w_o = [din(f"w_o{l}", [64, 8 * D]) for l in range(2)]
    w_1 = [din(f"w_1_{l}", [22 * 128, 2048]) for l in range(2)]
    w_2 = [din(f"w_2_{l}", [2816, D]) for l in range(2)]
    b_1a = [din(f"b_1a_{l}", [2816]) for l in range(2)]
    b_1g = [din(f"b_1g_{l}", [2816]) for l in range(2)]
    b_2 = [din(f"b_2_{l}", [D]) for l in range(2)]
    ln_g = [din(f"ln_g_{l}", [D]) for l in range(2)]
    ln_b = [din(f"ln_b_{l}", [D]) for l in range(2)]
    qs_in = [din(f"qs_{l}", [64]) for l in range(2)]
    ks_in = [din(f"ks_{l}", [64]) for l in range(2)]
    c_id_f = din("c_id_f", [128, 128], F32)
    c_trij = din("c_trij", [128, 2048], BF16)
    c_maskl = din("c_maskl", [128, 128]); c_masku = din("c_masku", [128, 128])
    c_ones_bf = din("c_ones_bf", [128, 512], BF16)
    c_ones_f = din("c_ones_f", [128, 128])
    c_cols = din("c_cols", [128, 16])
    c_xtcos = din("c_xtcos", [128, 512])
    c_xtsin = din("c_xtsin", [128, 512])
    c_xtscl = din("c_xtscl", [128, 512])
    c_pos = din("c_pos", [1, 512])
    c_ethrow = din("c_ethrow", [1, 512])
    c_expthc = din("c_expthc", [128, 4])
    c_xrows = din("c_xrows", [1, 128]); c_eh = din("c_eh", [1, 256])
    c_p256 = din("c_p256", [1, 128])

    mem = Mem(nc)
    o = ARENA_BASE
    def nxt(n):
        nonlocal o
        r = o; o = (o + n + 31) & ~31
        return r
    ID_F = mem.at("id_f", nxt(512), [128, 128], F32)
    MASKL = mem.at("maskl", nxt(512), [128, 128], F32)
    MASKU = mem.at("masku", nxt(512), [128, 128], F32)
    ONES_BF = mem.at("ones_bf", nxt(1024), [128, 512], BF16)
    ONES_F = mem.at("ones_f", nxt(512), [128, 128], F32)
    CC = mem.at("ccols", nxt(64), [128, 16], F32)
    DC = mem.at("dcon", nxt(64), [128, 16], F32)
    VEC = mem.at("vecs", nxt(256), [128, 64], F32)
    B1V = mem.at("b1v", nxt(384), [128, 96], F32)
    POSR = mem.at("posr", nxt(2048), [1, 512], F32)
    ETHC = mem.at("ethc", nxt(64), [128, 4], F32)
    ETHR = mem.at("ethr", nxt(2048), [1, 512], F32)
    XROW = mem.at("xrow", nxt(512), [1, 128], F32)
    EHR = mem.at("ehr", nxt(1024), [1, 256], F32)
    P256 = mem.at("p256", nxt(512), [1, 128], F32)
    RROW = mem.at("rrow", nxt(2048), [1, 512], F32)

    oXN = nxt(32768); oKR = nxt(32768); oVR = nxt(32768)
    oWR = nxt(40960)
    oWS = nxt(16384)
    oFA = nxt(45072)
    assert o <= mem.limit, f"arena overflow: {o} > {mem.limit}"

    XN = mem.at("l0_xn", oXN, [128, 8 * T], BF16, ("l0",), nslots=NSUP)
    KFM = mem.at("l0_k", oKR, [128, 8 * T], BF16, ("l0",), nslots=NTOK)
    VTM = mem.at("l0_v", oVR, [128, NTOK * D], BF16, ("l0",), nslots=NTOK)
    WE0 = mem.at("l0_we", oWR, [128, 8 * D], BF16, ("l0",))
    QY = mem.at("l0_qy", oWR + 16384, [128, 8 * 512], BF16, ("l0",), nslots=8)
    WKV0 = mem.at("l0_wkv", oWR + 24576, [128, 8 * D], BF16, ("l0kv",))
    WQR = mem.at("l0_wqres", oWR + 24576, [128, 8 * D], BF16, ("l0qq",))
    WSA = mem.at("ws_a", oWS, [128, 2048], BF16, ("all",), nslots=2)
    SPFX = mem.at("l0_spfx", oWS + 4096, [1, 1024], BF16, ("l0",))
    TRIJ = mem.at("l0_trij", oWS + 6144, [128, 2048], BF16, ("l0",))
    PB16 = mem.at("l0_pb", oWS + 10240, [128, 1024], BF16, ("l0",), nslots=2)
    PB16B = mem.at("l0_pbb", oWS + 12288, [128, 1024], BF16, ("l0",), nslots=2)
    XSUP = mem.at("l0_xsup", oFA, [128, 8 * 512], F32, ("l0",))
    SSUP = mem.at("l0_ssup", oFA, [128, 8 * 512], F32, ("l0s",))
    GSUP = mem.at("l0_gsup", oFA + 16384, [128, 8 * 512], F16, ("l0",))
    XG = mem.at("l0_xg", oFA + 24576, [128, 4096], F32, ("l0",))
    STAT = mem.at("stat", oFA + 40960, [128, 1024], F32, ("all",))
    PR = mem.at("l0_pr", oWS + 14336, [128, 512], F32, ("all",))
    WTS = mem.at("l0_wts", oFA, [128, 2048], F32, ("l0pre",), nslots=2)
    VRC = mem.at("l0_vrc", oFA + 8192, [32, 2048], BF16, ("l0pre",))
    SQT = mem.at("l0_sqt", oFA + 24576, [128, 1024], F32, ("l0ln",), nslots=2)

    banks = Banks(nc, es)
    HX1 = Reg(None, nslots=NSUP)
    HG = Reg(None, nslots=NSUP)
    sCONST = p.stream("sync", "cst")
    sXS = p.stream("sync", "xs")
    sXG = p.stream("sync", "xg")
    sWT = [p.stream("sync", f"wt{i}") for i in range(2)]
    sOUT = p.stream("sync", "out")
    sDBG = p.stream("sync", "dbg")
    sWV = p.stream("gpsimd", "wv")
    sVRC = p.stream("gpsimd", "vrc")
    sWSA = [p.stream("gpsimd", f"wsa{i}") for i in range(2)]

    def seal(reg, sl, stream):
        tok = ("d", stream, stream.n)
        for s2 in reg._sl(sl):
            reg.w[s2] = [tok]

    def op(engine, fn, rd=(), wr=(), extra=(), tag=""):
        waits = list(extra)
        for (rg, sl) in rd:
            waits += rg.rd(sl)
        for (rg, sl) in wr:
            waits += rg.wr(sl)
        tok = p.emit(engine, fn, waits, tag)
        for (rg, sl) in rd:
            rg.did_read(tok, sl)
        for (rg, sl) in wr:
            rg.did_write(tok, sl)
        return tok

    def dma(stream, out_ap, in_ap, rd=(), wr=(), extra=(), tag="", slow=False):
        waits = list(extra)
        for (rg, sl) in rd:
            waits += rg.rd(sl)
        for (rg, sl) in wr:
            waits += rg.wr(sl)
        if stream.n > 0:
            waits.append(("d", stream, stream.n))
        def fn(eng, _o=out_ap, _i=in_ap, _sl=slow):
            return eng.dma_start(out=_o, in_=_i, allow_slow_non_contiguous=_sl)
        it = Item(stream.engine, fn, waits, tag)
        it.stream = stream
        p.items.append(it)
        stream.n += 1
        tok = ("d", stream, stream.n)
        for (rg, sl) in rd:
            rg.did_read(tok, sl)
        for (rg, sl) in wr:
            rg.did_write(tok, sl)
        return tok

    def mm(bank, out_ap, lhsT_ap, rhs_ap, start, stop, rd=(), tile_position=None,
           ldw=False, tag="", kwargs_record_write=False, skip_gc=False, extra=()):
        if ldw:
            op("tensor", lambda e, _l=lhsT_ap, _tp=tile_position:
               e.ldweights(_l, tile_position=_tp), rd=list(rd), tag=tag + ".ldw")
        waits = list(extra)
        if start:
            waits += bank[1].wr()
        span = 1
        for st_, ct_ in out_ap.ap[1:]:
            span *= ct_
        if out_ap.ap[0][1] < 128 or out_ap.offset != 0 or span != 512:
            skip_gc = True
        def fn(e, _o=out_ap, _l=lhsT_ap, _r=rhs_ap, _s=start, _p=stop,
               _tp=tile_position, _sg=skip_gc):
            return e.matmul(_o, _l, _r, start=_s, stop=_p, tile_position=_tp,
                            skip_group_check=_sg)
        for (rg, sl) in rd:
            waits += rg.rd(sl)
        tok = p.emit("tensor", fn, waits, tag)
        for (rg, sl) in rd:
            rg.did_read(tok, sl)
        if stop or kwargs_record_write:
            bank[1].did_write(tok)
        return tok

    def alias_from(dst, *srcs):
        toks = []
        for s in srcs:
            for sl in range(len(s.w)):
                toks += s.w[sl] + s.r[sl]
        for sl in range(len(dst.w)):
            dst.w[sl] = compact_toks(dst.w[sl] + dst.r[sl] + toks)
            dst.r[sl] = []

    dma(sCONST, ID_F.t[:], c_id_f, wr=[(ID_F, None)])
    dma(sCONST, TRIJ.t[:], c_trij, wr=[(TRIJ, None)])
    dma(sCONST, MASKL.t[:], c_maskl, wr=[(MASKL, None)])
    dma(sCONST, MASKU.t[:], c_masku, wr=[(MASKU, None)])
    dma(sCONST, ONES_BF.t[:], c_ones_bf, wr=[(ONES_BF, None)])
    dma(sCONST, ONES_F.t[:], c_ones_f, wr=[(ONES_F, None)])
    dma(sCONST, CC.t[:], c_cols, wr=[(CC, None)])
    op("vector", lambda e: e.memset(CC.t[:, 5:6], 0.5), wr=[(CC, None)])
    op("vector", lambda e: e.memset(CC.t[:, 6:7], 1.0), wr=[(CC, None)])
    dma(sCONST, POSR.t[:], c_pos, wr=[(POSR, None)])
    dma(sCONST, ETHR.t[:], c_ethrow, wr=[(ETHR, None)])
    dma(sCONST, ETHC.t[:], c_expthc, wr=[(ETHC, None)])
    dma(sCONST, XROW.t[:], c_xrows, wr=[(XROW, None)])
    dma(sCONST, EHR.t[:], c_eh, wr=[(EHR, None)])
    dma(sCONST, P256.t[:], c_p256, wr=[(P256, None)])
    dma(sCONST, VEC.t[:, 0:8], bass.AP(b_l0f.tensor, 0, [[1, 128], [128, 8]]),
        wr=[(VEC, None)], slow=True)
    for l in range(2):
        dma(sCONST, VEC.t[:, 8 + 16 * l:16 + 16 * l],
            bass.AP(ln_g[l].tensor, 0, [[1, 128], [128, 8]]), wr=[(VEC, None)], slow=True)
        dma(sCONST, VEC.t[:, 16 + 16 * l:24 + 16 * l],
            bass.AP(ln_b[l].tensor, 0, [[1, 128], [128, 8]]), wr=[(VEC, None)], slow=True)
        dma(sCONST, VEC.t[:, 40 + 8 * l:48 + 8 * l],
            bass.AP(b_2[l].tensor, 0, [[1, 128], [128, 8]]), wr=[(VEC, None)], slow=True)
        dma(sCONST, B1V.t[:, 44 * l:44 * l + 22],
            bass.AP(b_1a[l].tensor, 0, [[1, 128], [128, 22]]), wr=[(B1V, None)], slow=True)
        dma(sCONST, B1V.t[:, 44 * l + 22:44 * l + 44],
            bass.AP(b_1g[l].tensor, 0, [[1, 128], [128, 22]]), wr=[(B1V, None)], slow=True)
        dma(sCONST, RROW.t[0:1, 128 * l:128 * l + 64],
            bass.AP(qs_in[l].tensor, 0, [[64, 1], [1, 64]]), wr=[(RROW, None)])
        dma(sCONST, RROW.t[0:1, 128 * l + 64:128 * l + 128],
            bass.AP(ks_in[l].tensor, 0, [[64, 1], [1, 64]]), wr=[(RROW, None)])

    for _rg in (ID_F, TRIJ, MASKL, MASKU, ONES_BF, ONES_F, CC, POSR,
                ETHC, ETHR, XROW, EHR, P256, VEC, B1V):
        seal(_rg, None, sCONST)

    t_dc = DC.t
    op("vector", lambda e: e.memset(t_dc[:, 15:16], -100.0), wr=[(DC, None)])
    op("vector", lambda e: e.reciprocal(t_dc[:, 0:1], t_dc[:, 15:16]),
       rd=[(DC, None)], wr=[(DC, None)])
    op("vector", lambda e: e.memset(t_dc[:, 14:15], 1000.0), wr=[(DC, None)])
    op("vector", lambda e: e.reciprocal(t_dc[:, 1:2], t_dc[:, 14:15]),
       rd=[(DC, None)], wr=[(DC, None)])
    op("vector", lambda e: e.tensor_tensor(t_dc[:, 13:14], CC.t[:, 0:1],
       CC.t[:, 1:2], ALU.add), rd=[(CC, None)], wr=[(DC, None)])
    op("vector", lambda e: e.tensor_scalar_mul(t_dc[:, 12:13], t_dc[:, 13:14], -1.0),
       rd=[(DC, None)], wr=[(DC, None)])
    op("vector", lambda e: e.reciprocal(t_dc[:, 2:3], t_dc[:, 12:13]),
       rd=[(DC, None)], wr=[(DC, None)])
    op("vector", lambda e: e.tensor_scalar_mul(t_dc[:, 4:5], t_dc[:, 2:3], 2.0),
       rd=[(DC, None)], wr=[(DC, None)])
    op("vector", lambda e: e.memset(t_dc[:, 11:12], 10000.0), wr=[(DC, None)])
    op("vector", lambda e: e.memset(t_dc[:, 10:11], 1e-5), wr=[(DC, None)])
    op("scalar", lambda e: e.activation(t_dc[:, 3:4], t_dc[:, 11:12], AF.Ln),
       rd=[(DC, None)], wr=[(DC, None)])

    dma(sVRC, VRC.t[0:32, 0:1024], w_l0vr, wr=[(VRC, None)], tag="vrt")
    dma(sVRC, VRC.t[0:32, 1024:2048], w_l0c, wr=[(VRC, None)], tag="ct")
    seal(VRC, None, sVRC)
    for c in range(NCH):
        sl = c % 2
        dma(sWT[sl], WTS.t[:, sl * 1024:(sl + 1) * 1024],
            w_l0f[128 * c:128 * (c + 1), :], wr=[(WTS, sl)], tag="wt")
        for h in range(2):
            bk = banks.take()
            mm(bk, bk[0][:, :], VRC.t[0:32, c * 128:(c + 1) * 128],
               VRC.t[0:32, 1024 + 512 * h:1024 + 512 * (h + 1)],
               start=True, stop=True, rd=[(VRC, None)], tag="weff.mm")
            tok = op("vector", lambda e, _c=c, _h=h, _sl=sl, _b=bk[0]:
                     e.tensor_tensor(
                         WE0.t[:, _c * D + 512 * _h:_c * D + 512 * (_h + 1)],
                         WTS.t[:, _sl * 1024 + 512 * _h:_sl * 1024 + 512 * (_h + 1)],
                         _b[:, :], ALU.add),
                     rd=[(WTS, sl)], wr=[(WE0, None)], extra=bk[1].rd(),
                     tag="weff.add")
            bk[1].did_read(tok)

    def ln_stats_apply(x_rd, x_chunk_ap, dst_wr, dst_ap, sq, sq_ap, tag="ln"):
        bkA = banks.take(); bkB = banks.take()
        for c in range(NCH):
            sl = c % 2
            op("scalar", lambda e, _s=sq_ap(sl), _x=x_chunk_ap(c): e.activation(
                _s, _x, AF.Square), rd=[x_rd(c)], wr=[(sq, sl)], tag=tag + ".sq")
            mm(bkA, bkA[0][0:1, :], ONES_F.t[:, 0:1], x_chunk_ap(c),
               start=(c == 0), stop=(c == NCH - 1), rd=[x_rd(c)], tag=tag + ".sx")
            mm(bkB, bkB[0][0:1, :], ONES_F.t[:, 0:1], sq_ap(sl),
               start=(c == 0), stop=(c == NCH - 1), rd=[(sq, sl)], tag=tag + ".sx2")
        A = STAT.t[0:1, 0:512]
        B = STAT.t[0:1, 512:1024]
        tmp = sq.t[0:1, 0:512]
        t1 = op("vector", lambda e, _b=bkA[0]: e.tensor_scalar_mul(
                A, _b[0:1, :], -1.0 / 1024.0), wr=[(STAT, None)],
                extra=bkA[1].rd(), tag=tag + ".m")
        bkA[1].did_read(t1)
        t2 = op("vector", lambda e, _b=bkB[0]: e.tensor_scalar_mul(
                B, _b[0:1, :], 1.0 / 1024.0), wr=[(STAT, None)],
                extra=bkB[1].rd(), tag=tag + ".e2")
        bkB[1].did_read(t2)
        op("vector", lambda e: e.tensor_tensor(tmp, A, A, ALU.mult),
           rd=[(STAT, None)], wr=[(sq, 0)], tag=tag + ".msq")
        op("vector", lambda e: e.tensor_tensor(B, B, tmp, ALU.subtract),
           rd=[(STAT, None), (sq, 0)], wr=[(STAT, None)], tag=tag + ".var")
        op("scalar", lambda e: e.activation(B, B, AF.Sqrt,
           bias=DC.t[0:1, 10:11]),
           rd=[(STAT, None), (DC, None)], wr=[(STAT, None)], tag=tag + ".sd")
        op("vector", lambda e: e.reciprocal(B, B),
           rd=[(STAT, None)], wr=[(STAT, None)], tag=tag + ".rstd")
        bkR = banks.take(); bkM = banks.take()
        mm(bkR, bkR[0][:, :], ONES_F.t[0:1, 0:128], B, start=True, stop=True,
           rd=[(STAT, None)], tag=tag + ".bcr")
        mm(bkM, bkM[0][:, :], ONES_F.t[0:1, 0:128], A, start=True, stop=True,
           rd=[(STAT, None)], tag=tag + ".bcm")
        for c in range(NCH):
            sl = c % 2
            tk1 = op("vector", lambda e, _t=sq_ap(sl), _x=x_chunk_ap(c), _b=bkM[0]:
                     e.tensor_tensor(_t, _x, _b[:, :], ALU.add),
                     rd=[x_rd(c)], wr=[(sq, sl)], extra=bkM[1].rd(), tag=tag + ".a1")
            bkM[1].did_read(tk1)
            tk2 = op("vector", lambda e, _t=sq_ap(sl), _d=dst_ap(c), _b=bkR[0]:
                     e.tensor_tensor(_d, _t, _b[:, :], ALU.mult),
                     rd=[(sq, sl)], wr=[dst_wr(c)], extra=bkR[1].rd(),
                     tag=tag + ".a2")
            bkR[1].did_read(tk2)

    alias_from(XSUP, WTS, VRC)
    for s in range(NSUP):
        for c in range(NCH):
            dma(sXS, XSUP.t[:, c * 512:(c + 1) * 512],
                x_in[128 * c:128 * (c + 1), SUP * s:SUP * (s + 1)],
                wr=[(XSUP, 0)], tag="x.ld")
        seal(XSUP, None, sXS)
        ln_stats_apply(
            x_rd=lambda c, _s=s: (XSUP, 0),
            x_chunk_ap=lambda c, _s=s: XSUP.t[:, c * 512:(c + 1) * 512],
            dst_wr=lambda c, _s=s: (XN, _s),
            dst_ap=lambda c, _s=s: XN.t[:, c * T + SUP * _s:c * T + SUP * (_s + 1)],
            sq=SQT, sq_ap=lambda sl: SQT.t[:, sl * 512:(sl + 1) * 512],
            tag=f"ln0.{s}")

    for c in range(NCH):
        dma(sWV, WKV0.t[:, c * D:(c + 1) * D], w_l0k[128 * c:128 * (c + 1), :],
            wr=[(WKV0, None)], tag="wk0")
    seal(WKV0, None, sWV)
    for s in range(NSUP):
        for m in range(NCH):
            bk = banks.take()
            for c in range(NCH):
                mm(bk, bk[0][:, :],
                   WKV0.t[:, c * D + 128 * m:c * D + 128 * (m + 1)],
                   XN.t[:, c * T + SUP * s:c * T + SUP * (s + 1)],
                   start=(c == 0), stop=(c == NCH - 1),
                   rd=[(WKV0, None), (XN, s)], tag="k.mm")
            kt = op("vector", lambda e, _b=bk[0], _m=m, _s=s: e.tensor_copy(
                KFM.t[:, _m * T + SUP * _s:_m * T + SUP * (_s + 1)], _b[:, :]),
                wr=[(KFM, [4 * s + j for j in range(4)])], extra=bk[1].rd(),
                tag="k.ev")
            bk[1].did_read(kt)
    for c in range(NCH):
        dma(sWV, WKV0.t[:, c * D:(c + 1) * D], w_l0v[128 * c:128 * (c + 1), :],
            wr=[(WKV0, None)], tag="wv0")
    seal(WKV0, None, sWV)
    for tt in range(NTOK):
        s = tt // 4
        for h in range(2):
            bk = banks.take()
            for c in range(NCH):
                mm(bk, bk[0][:, :],
                   XN.t[:, c * T + TILE * tt:c * T + TILE * (tt + 1)],
                   WKV0.t[:, c * D + 512 * h:c * D + 512 * (h + 1)],
                   start=(c == 0), stop=(c == NCH - 1),
                   rd=[(WKV0, None), (XN, s)], tag="v.mm")
            vt = op("vector", lambda e, _b=bk[0], _t=tt, _h=h: e.tensor_scalar(
                VTM.t[:, _t * D + 512 * _h:_t * D + 512 * (_h + 1)], _b[:, :],
                DC.t[:, 1:2], None, ALU.mult),
                rd=[(DC, None)], wr=[(VTM, tt)], extra=bk[1].rd(), tag="v.ev")
            bk[1].did_read(vt)

    alias_from(WQR, WKV0)
    alias_from(XG, SQT)
    alias_from(SSUP, XSUP)
    for c in range(NCH):
        dma(sWV, WQR.t[:, c * D:(c + 1) * D], w_l0q[128 * c:128 * (c + 1), :],
            wr=[(WQR, None)], tag="wq.res")
    seal(WQR, None, sWV)
    cs_banks = banks.reserve(2)
    cs_started = [False, False]
    for s in range(NSUP):
        for mh in range(2):
            bks = banks.take(4)
            for c in range(NCH):
                for mi in range(4):
                    m = mh * 4 + mi
                    mm(bks[mi], bks[mi][0][:, :],
                       WQR.t[:, c * D + 128 * m:c * D + 128 * (m + 1)],
                       XN.t[:, c * T + SUP * s:c * T + SUP * (s + 1)],
                       start=(c == 0), stop=(c == NCH - 1),
                       rd=[(WQR, None), (XN, s)], tag="q.mm")
            for mi in range(4):
                m = mh * 4 + mi
                tk = op("vector", lambda e, _b=bks[mi][0], _m=m: e.tensor_scalar_mul(
                    QY.t[:, _m * 512:(_m + 1) * 512], _b[:, :], 0.125),
                    wr=[(QY, m)], extra=bks[mi][1].rd(), tag="q.ev")
                bks[mi][1].did_read(tk)
        for m in range(NCH):
            bk = banks.take()
            for c in range(NCH):
                mm(bk, bk[0][:, :], WE0.t[:, c * D + 128 * m:c * D + 128 * (m + 1)],
                   XN.t[:, c * T + SUP * s:c * T + SUP * (s + 1)],
                   start=(c == 0), stop=(c == NCH - 1),
                   rd=[(WE0, None), (XN, s)], tag="g.mm")
            gt = op("scalar", lambda e, _b=bk[0], _m=m, _s=s: e.activation(
                GSUP.t[:, _m * 512:(_m + 1) * 512], _b[:, :], AF.Silu,
                bias=VEC.t[:, _m:_m + 1]),
                rd=[(VEC, None)], wr=[(GSUP, 0)], extra=bk[1].rd(), tag="g.silu")
            bk[1].did_read(gt)
        if s > 0:
            for kt in range(4 * (s - 1), 4 * s):
                for hp in range(NCH):
                    bi = hp // 4
                    ex = list(cs_banks[bi][1].r[0]) if kt == 4 * (s - 1) else []
                    mm(cs_banks[bi],
                       cs_banks[bi][0][0:1, 128 * (hp % 4):128 * (hp % 4 + 1)],
                       ONES_BF.t[:, 0:1],
                       VTM.t[:, kt * D + 128 * hp:kt * D + 128 * (hp + 1)],
                       start=(not cs_started[bi]), stop=False,
                       rd=[(VTM, kt), (ONES_BF, None)], tag="cs.mm",
                       kwargs_record_write=True, skip_gc=True, extra=ex)
                    cs_started[bi] = True
            for bi in range(2):
                tk = op("vector", lambda e, _b=cs_banks[bi][0], _bi=bi:
                        e.tensor_copy(SPFX.t[0:1, 512 * _bi:512 * (_bi + 1)],
                                      _b[0:1, :]),
                        wr=[(SPFX, None)], extra=cs_banks[bi][1].rd(), tag="cs.ev")
                cs_banks[bi][1].did_read(tk)
        for hp in range(NCH):
            ybks = banks.take(2)
            ystarted = set()
            def ymm(h, lhsT_ap, rhs_ap, rd, tag, stop=False, ldw=True):
                ybk = ybks[h]
                start = h not in ystarted
                oy = ybk[0][64 * h:64 * (h + 1), :]
                mm(ybk, oy, lhsT_ap, rhs_ap, start=start, stop=stop, rd=rd,
                   tag=tag, kwargs_record_write=stop,
                   tile_position=(0, 64 * h), ldw=ldw)
                ystarted.add(h)
            if s > 0:
                for h in range(2):
                    ymm(h, SPFX.t[0:1, 128 * hp + 64 * h:128 * hp + 64 * (h + 1)],
                        ONES_BF.t[0:1, 0:512],
                        [(SPFX, None), (ONES_BF, None)], "y.ones")
            sc_pair = [banks.take(2), banks.take(2)]
            nkt = 4 * s + 4

            def scores(kt, _s=s, _hp=hp, _sp=sc_pair):
                pr = _sp[kt % 2]
                for h in range(2):
                    mm(pr[h], pr[h][0][:, :],
                       KFM.t[64 * h:64 * (h + 1),
                             _hp * T + 128 * kt:_hp * T + 128 * (kt + 1)],
                       QY.t[64 * h:64 * (h + 1), _hp * 512:(_hp + 1) * 512],
                       start=True, stop=True, rd=[(KFM, kt), (QY, _hp)],
                       tile_position=(64 * h, 0), ldw=True, tag="sc.mm")

            def poly_av(kt, _s=s, _hp=hp, _sp=sc_pair, _ybk=None, _fm=None):
                pr = _sp[kt % 2]
                j = kt - 4 * _s
                last = (kt == 4 * _s + 3)
                for h in range(2):
                    sb, sreg = pr[h]
                    r2_ap = PB16.t[:, h * 512:(h + 1) * 512]
                    u_ap = PB16B.t[:, h * 512:(h + 1) * 512]
                    t1 = op("scalar", lambda e, _sb=sb: e.activation(
                        PR.t[:], _sb[:, :], AF.Relu), wr=[(PR, None)],
                        extra=sreg.rd(), tag="poly.r")
                    sreg.did_read(t1)
                    op("scalar", lambda e, _d=r2_ap: e.activation(
                        _d, PR.t[:], AF.Square), rd=[(PR, None)],
                        wr=[(PB16, h)], tag="poly.r2")
                    t2 = op("scalar", lambda e, _sb=sb, _d=u_ap: e.activation(
                        _d, _sb[:, :], AF.Relu, scale=DC.t[:, 0:1]),
                        rd=[(DC, None)], wr=[(PB16B, h)], extra=sreg.rd(),
                        tag="poly.u")
                    sreg.did_read(t2)
                    if j >= 0:
                        op("vector", lambda e, _d=r2_ap, _j=j: e.tensor_tensor(
                            _d, _d, TRIJ.t[:, _j * 512:(_j + 1) * 512], ALU.mult),
                            rd=[(PB16, h), (TRIJ, None)], wr=[(PB16, h)],
                            tag="poly.mr2")
                        op("vector", lambda e, _d=u_ap, _j=j: e.tensor_tensor(
                            _d, _d, TRIJ.t[:, _j * 512:(_j + 1) * 512], ALU.mult),
                            rd=[(PB16B, h), (TRIJ, None)], wr=[(PB16B, h)],
                            tag="poly.mu")
                    vh_ap = VTM.t[:, kt * D + 128 * _hp + 64 * h:
                                  kt * D + 128 * _hp + 64 * (h + 1)]
                    ymm(h, vh_ap, r2_ap, [(VTM, kt), (PB16, h)], "y.r2")
                    ymm(h, vh_ap, u_ap, [(VTM, kt), (PB16B, h)], "y.u")
                if j >= 0:
                    for h in range(2):
                        vh_ap = VTM.t[:, kt * D + 128 * _hp + 64 * h:
                                      kt * D + 128 * _hp + 64 * (h + 1)]
                        ymm(h, vh_ap, TRIJ.t[:, j * 512:(j + 1) * 512],
                            [(VTM, kt), (TRIJ, None)], "y.tri", stop=last)


            scores(0)
            for kt in range(nkt):
                if kt + 1 < nkt:
                    scores(kt + 1)
                poly_av(kt)
            for h in range(2):
                tk = op("vector", lambda e, _hp=hp, _h=h, _b=ybks[h][0]:
                        e.tensor_copy(
                            QY.t[64 * _h:64 * (_h + 1),
                                 _hp * 512:(_hp + 1) * 512],
                            _b[64 * _h:64 * (_h + 1), :]),
                        wr=[(QY, hp)], extra=ybks[h][1].rd(), tag="y.ev")
                ybks[h][1].did_read(tk)
        for mh in range(2):
            bks = banks.take(4)
            for c in range(NCH):
                sl = c % 2
                dma(sWSA[sl], WSA.t[:, sl * 1024:(sl + 1) * 1024],
                    w_l0p[128 * c:128 * (c + 1), :], wr=[(WSA, sl)], tag="wp.st")
                for mi in range(4):
                    m = mh * 4 + mi
                    mm(bks[mi], bks[mi][0][:, :],
                       WSA.t[:, sl * 1024 + 128 * m:sl * 1024 + 128 * (m + 1)],
                       QY.t[:, c * 512:(c + 1) * 512],
                       start=(c == 0), stop=(c == NCH - 1),
                       rd=[(WSA, sl), (QY, c)], tag="s.mm")
            for mi in range(4):
                m = mh * 4 + mi
                tk = op("vector", lambda e, _b=bks[mi][0], _m=m: e.tensor_copy(
                    SSUP.t[:, _m * 512:(_m + 1) * 512], _b[:, :]),
                    wr=[(SSUP, 0)], extra=bks[mi][1].rd(), tag="s.ev")
                bks[mi][1].did_read(tk)
        for c in range(NCH):
            dma(sXG, XG.t[:, c * 512:(c + 1) * 512],
                x_in[128 * c:128 * (c + 1), SUP * s:SUP * (s + 1)],
                wr=[(XG, 0)], tag="xg.ld")
        seal(XG, None, sXG)
        for c in range(NCH):
            op("vector", lambda e, _c=c: e.tensor_tensor(
                PR.t[:], SSUP.t[:, _c * 512:(_c + 1) * 512],
                GSUP.t[:, _c * 512:(_c + 1) * 512], ALU.mult),
                rd=[(SSUP, 0), (GSUP, 0)], wr=[(PR, None)], tag="gate.sg")
            op("vector", lambda e, _c=c: e.tensor_tensor(
                SSUP.t[:, _c * 512:(_c + 1) * 512], PR.t[:],
                XG.t[:, _c * 512:(_c + 1) * 512], ALU.add),
                rd=[(PR, None), (XG, 0)], wr=[(SSUP, 0)], tag="gate.add")
            tgt = out_t if PHASES == "l0" else hbm_x1
            dma(sOUT, tgt[128 * c:128 * (c + 1), SUP * s:SUP * (s + 1)],
                SSUP.t[:, c * 512:(c + 1) * 512], rd=[(SSUP, 0)],
                wr=[(HX1, s)], tag="x1.st")


    banks.release_all()

    if PHASES != "l0":
        sX1 = p.stream("sync", "x1l")
        sG = p.stream("sync", "g")
        sQKV = [p.stream("gpsimd", f"qkv{i}") for i in range(2)]
        sW1 = [p.stream("gpsimd", f"w1_{i}") for i in range(2)]
        sW2b = [p.stream("gpsimd", f"w2_{i}") for i in range(2)]
        sWO = [p.stream("gpsimd", f"wo{i}") for i in range(2)]
        sWB = p.stream("gpsimd", "wb")

        XF = mem.at("l1_xf", oXN, [128, 16384], F32, ("l1f",), nslots=NTOK)
        VL = mem.at("l1_vl", oVR, [128, 8 * 512], BF16, ("l1a",), nslots=8)
        HBR = mem.at("l1_hb", oVR + 8192, [128, 8 * 512], BF16, ("l1a",))
        PREQ = mem.at("l1_preq", oVR + 16384, [128, 4 * 512], BF16, ("l1a",), nslots=4)
        PREK = mem.at("l1_prek", oVR + 20480, [128, 4 * 512], BF16, ("l1a",), nslots=4)
        QTOK = mem.at("l1_qtok", oVR + 24576, [128, 512], BF16, ("l1a",))
        KTOK = mem.at("l1_ktok", oVR + 25600, [128, 512], BF16, ("l1a",))
        OF64 = mem.at("l1_of64", oVR + 16384, [64, 8 * 512], BF16, ("l1ab",))
        STR = mem.at("l1_str", oVR + 8192, [128, 8 * 512], BF16, ("l1ab",))
        R2S = mem.at("l1_r2s", oVR + 30720, [128, 512], F32, ("l1a",))
        QP = mem.at("l1_qp", oWR, [128, 8 * 512], BF16, ("l1a",), nslots=8)
        KP = mem.at("l1_kp", oWR + 8192, [128, 8 * 512], BF16, ("l1a",), nslots=8)
        WQKVS = mem.at("l1_wqkvs", oWR + 16384, [128, 2 * 1536], BF16, ("l1a",), nslots=2)
        WOS = mem.at("l1_wos", oWR + 22528, [128, 2 * 1024], BF16, ("l1a",), nslots=2)
        W1S = mem.at("l1_w1s", oWR + 26624, [128, 2 * 2048], BF16, ("l1a",), nslots=2)
        W2S = mem.at("l1_w2s", oWR + 34816, [128, 2 * 1024], BF16, ("l1a",), nslots=2)
        XTB = mem.at("l1_xtb", oFA, [128, 2048], F32, ("l1a",))
        ABT = mem.at("l1_abt", oFA + 8192, [128, 4 * 1024], F16, ("l1a",), nslots=4)
        H2R = mem.at("l1_h2", oFA + 16384, [128, 22 * 512], BF16, ("l1a",))
        A16 = mem.at("l1_a16", oFA + 38912, [128, 512], F16, ("l1a",))
        G16 = mem.at("l1_g16", oFA + 39936, [128, 512], F16, ("l1a",))
        ROT1 = mem.at("l1_rot1", oWS + 4096, [128, 512], F32, ("l1a",))
        ROT2 = mem.at("l1_rot2", oWS + 6144, [128, 512], F32, ("l1a",))
        ROT3 = mem.at("l1_rot3", oWS + 8192, [128, 512], F32, ("l1a",))
        RSR = mem.at("l1_rs", oWS + 10240, [1, 1024], F32, ("l1a",))

        alias_from(XF, XN, KFM)
        alias_from(VL, VTM)
        alias_from(HBR, VTM)
        alias_from(PREQ, VTM)
        alias_from(PREK, VTM)
        alias_from(QTOK, VTM)
        alias_from(KTOK, VTM)
        alias_from(OF64, VTM)
        alias_from(STR, VTM)
        alias_from(R2S, VTM)
        alias_from(QP, WE0)
        alias_from(KP, WE0, QY)
        alias_from(WQKVS, QY)
        alias_from(WOS, QY, WQR)
        alias_from(W1S, WQR)
        alias_from(W2S, WQR)
        alias_from(XTB, XSUP, SSUP)
        alias_from(ABT, XSUP, SSUP)
        alias_from(H2R, GSUP, XG)
        alias_from(A16, XG)
        alias_from(G16, XG)
        alias_from(ROT1, SPFX)
        alias_from(ROT2, TRIJ)
        alias_from(ROT3, TRIJ)
        alias_from(RSR, PB16, PB16B)

        st_dc2 = op("vector", lambda e: e.memset(t_dc[:, 8:9], -50.0), wr=[(DC, None)])
        op("vector", lambda e: e.memset(t_dc[:, 9:10], -20.0), wr=[(DC, None)])

        X1L = mem.at("l1_x1l", oFA + 16384, [128, 8 * 512], F32, ("l1ln",))
        alias_from(X1L, GSUP, XG)
        SQ2 = mem.at("l1_sq2", oFA, [128, 1024], F32, ("l1ln2",), nslots=2)
        alias_from(SQ2, XSUP, SSUP)

        for s in range(NSUP):
            for c in range(NCH):
                dma(sX1, X1L.t[:, c * 512:(c + 1) * 512],
                    hbm_x1[128 * c:128 * (c + 1), SUP * s:SUP * (s + 1)],
                    rd=[(HX1, s)], wr=[(X1L, 0)], tag="x1.ld")
            seal(X1L, None, sX1)
            ln_stats_apply(
                x_rd=lambda c, _s=s: (X1L, 0),
                x_chunk_ap=lambda c, _s=s: X1L.t[:, c * 512:(c + 1) * 512],
                dst_wr=lambda c, _s=s: (XF, [4 * _s + j for j in range(4)]),
                dst_ap=lambda c, _s=s: XF.t[:, c * T + SUP * _s:c * T + SUP * (_s + 1)],
                sq=SQ2, sq_ap=lambda sl: SQ2.t[:, sl * 512:(sl + 1) * 512],
                tag=f"ln1.{s}")
        alias_from(XTB, SQ2)
        alias_from(ABT, SQ2)
        alias_from(H2R, X1L)

        cosb_ap = XTB.t[:, 0:512]; sinb_ap = XTB.t[:, 512:1024]
        scrb_ap = XTB.t[:, 1024:1536]; xscr_ap = XTB.t[:, 1536:2048]
        r_invf = RSR.t[0:1, 0:32]
        r_tmp = RSR.t[0:1, 32:64]
        r_lnb = RSR.t[0:1, 64:96]
        r_qs = RSR.t[0:1, 96:160]
        r_ks = RSR.t[0:1, 160:224]
        op("scalar", lambda e: e.activation(r_invf, XROW.t[0:1, 0:32], AF.Exp,
           scale=DC.t[0:1, 3:4]), rd=[(XROW, None), (DC, None)], wr=[(RSR, None)],
           tag="xt.invf")
        op("vector", lambda e: e.memset(t_dc[:, 7:8], 448.0), wr=[(DC, None)])
        op("vector", lambda e: e.reciprocal(t_dc[:, 6:7], t_dc[:, 7:8]),
           rd=[(DC, None)], wr=[(DC, None)])
        op("scalar", lambda e: e.activation(r_lnb, XROW.t[0:1, 32:64], AF.Ln,
           scale=DC.t[0:1, 6:7]), rd=[(XROW, None), (DC, None)], wr=[(RSR, None)],
           tag="xt.lnb")

        def cw_sincos(bank_ang, breg, dst_sin, dst_cos, wr_reg, tag):
            def one_pass(bias_col, kq, dst, ptag):
                t1 = op("scalar", lambda e, _bc=bias_col: e.activation(
                        ROT1.t[:], bank_ang[:, :], AF.Identity,
                        scale=DC.t[:, 4:5], bias=_bc),
                        rd=[(DC, None), (CC, None)], wr=[(ROT1, None)],
                        extra=breg.rd(), tag=ptag + ".y")
                breg.did_read(t1)
                op("vector", lambda e: e.tensor_copy(ROT2.t[:], ROT1.t[:]),
                   rd=[(ROT1, None)], wr=[(ROT2, None)], tag=ptag + ".fc")
                for _b in (512.0, 256.0, 128.0, 64.0, 32.0, 16.0, 8.0, 4.0,
                           2.0, 1.0):
                    op("vector", lambda e, _bb=_b: e.tensor_scalar(
                        ROT3.t[:], ROT2.t[:], _bb, None, ALU.is_ge),
                        rd=[(ROT2, None)], wr=[(ROT3, None)], tag=ptag + ".fg")
                    op("vector", lambda e, _bb=_b: e.tensor_scalar_mul(
                        PR.t[:], ROT3.t[:], -_bb),
                        rd=[(ROT3, None)], wr=[(PR, None)], tag=ptag + ".fm")
                    op("vector", lambda e: e.tensor_tensor(
                        ROT2.t[:], ROT2.t[:], PR.t[:], ALU.add),
                        rd=[(ROT2, None), (PR, None)], wr=[(ROT2, None)],
                        tag=ptag + ".fs")
                op("vector", lambda e: e.tensor_scalar_mul(
                    PR.t[:], ROT3.t[:], -2.0),
                    rd=[(ROT3, None)], wr=[(PR, None)], tag=ptag + ".sg1")
                op("vector", lambda e: e.tensor_scalar_add(
                    PR.t[:], PR.t[:], 1.0),
                    rd=[(PR, None)], wr=[(PR, None)], tag=ptag + ".sg2")
                op("vector", lambda e: e.tensor_tensor(ROT1.t[:], ROT1.t[:],
                   ROT2.t[:], ALU.subtract),
                   rd=[(ROT1, None), (ROT2, None)], wr=[(ROT1, None)],
                   tag=ptag + ".k")
                if kq:
                    op("vector", lambda e: e.tensor_scalar_add(
                        ROT1.t[:], ROT1.t[:], kq),
                        rd=[(ROT1, None)], wr=[(ROT1, None)], tag=ptag + ".kq")
                op("vector", lambda e: e.tensor_scalar_mul(ROT3.t[:], ROT1.t[:],
                   -_C1H), rd=[(ROT1, None)], wr=[(ROT3, None)],
                   tag=ptag + ".r1m")
                t2 = op("vector", lambda e: e.tensor_tensor(ROT2.t[:],
                        ROT3.t[:], bank_ang[:, :], ALU.add),
                        rd=[(ROT3, None)], wr=[(ROT2, None)],
                        extra=breg.rd(), tag=ptag + ".r1")
                breg.did_read(t2)
                for cv, nm in ((_C2H, ".r2"), (_C3H, ".r3")):
                    op("vector", lambda e, _cv=cv: e.tensor_scalar_mul(
                        ROT3.t[:], ROT1.t[:], -_cv),
                        rd=[(ROT1, None)], wr=[(ROT3, None)], tag=ptag + nm + "m")
                    op("vector", lambda e: e.tensor_tensor(ROT2.t[:], ROT2.t[:],
                       ROT3.t[:], ALU.add),
                       rd=[(ROT2, None), (ROT3, None)], wr=[(ROT2, None)],
                       tag=ptag + nm)
                op("vector", lambda e: e.tensor_single_scalar(ROT2.t[:],
                   ROT2.t[:], _PI_LO, ALU.min),
                   rd=[(ROT2, None)], wr=[(ROT2, None)], tag=ptag + ".clA")
                op("vector", lambda e: e.tensor_single_scalar(ROT2.t[:],
                   ROT2.t[:], -_PI_LO, ALU.max),
                   rd=[(ROT2, None)], wr=[(ROT2, None)], tag=ptag + ".cl")
                op("scalar", lambda e: e.activation(ROT2.t[:], ROT2.t[:],
                   AF.Sin), rd=[(ROT2, None)], wr=[(ROT2, None)],
                   tag=ptag + ".sin")
                op("vector", lambda e, _d=dst: e.tensor_tensor(_d, ROT2.t[:],
                   PR.t[:], ALU.mult),
                   rd=[(ROT2, None), (PR, None)], wr=[wr_reg],
                   tag=ptag + ".ap")
            one_pass(CC.t[:, 5:6], None, dst_sin, tag + ".s")
            one_pass(CC.t[:, 6:7], -0.5, dst_cos, tag + ".c")

        dma(sCONST, cosb_ap, c_xtcos, wr=[(XTB, None)], tag="xt.ldc")
        dma(sCONST, sinb_ap, c_xtsin, wr=[(XTB, None)], tag="xt.lds")
        dma(sCONST, scrb_ap, c_xtscl, wr=[(XTB, None)], tag="xt.ldr")
        seal(XTB, None, sCONST)

        def build_ab(l, kind, a_slot, b_slot, tag):
            if kind == "q":
                op("vector", lambda e: e.tensor_tensor(xscr_ap, cosb_ap, scrb_ap,
                   ALU.mult), rd=[(XTB, None)], wr=[(XTB, None)], tag=tag + ".cq")
            else:
                op("vector", lambda e: e.tensor_tensor(ROT1.t[:], scrb_ap, scrb_ap,
                   ALU.bypass), rd=[(XTB, None)], wr=[(ROT1, None)], tag=tag + ".cp")
                op("vector", lambda e: e.reciprocal(ROT1.t[:], ROT1.t[:]),
                   rd=[(ROT1, None)], wr=[(ROT1, None)], tag=tag + ".ci")
                op("vector", lambda e: e.tensor_tensor(xscr_ap, cosb_ap, ROT1.t[:],
                   ALU.mult), rd=[(XTB, None), (ROT1, None)], wr=[(XTB, None)],
                   tag=tag + ".ck")
            qrow = RROW.t[0:1, 128 * l + (0 if kind == "q" else 64):
                          128 * l + (64 if kind == "q" else 128)]
            qsc = 8.0 if kind == "q" else 1.0
            bkq = banks.take()
            opx = op("vector", lambda e, _q=qrow, _s=qsc: e.tensor_scalar_mul(
                RSR.t[0:1, 224:288], _q, _s), rd=[(RROW, None)], wr=[(RSR, None)],
                tag=tag + ".qs")
            mm(bkq, bkq[0][:, 0:64], ONES_F.t[0:1, 0:128], RSR.t[0:1, 224:288],
               start=True, stop=True, rd=[(RSR, None), (ONES_F, None)],
               tag=tag + ".qb")
            xrep = bass.AP(XTB.t, 1536, [[2048, 128], [32, 16], [0, 2], [1, 32]])
            qrep = None
            def fa(e, _sl=a_slot, _x=xrep, _b=bkq[0]):
                qr = bass.AP(_b, 0, [[512, 128], [0, 16], [1, 64]])
                return e.tensor_tensor(ABT.t[:, _sl * 1024:(_sl + 1) * 1024],
                                       _x, qr, ALU.mult)
            ta = op("vector", fa, rd=[(XTB, None)], wr=[(ABT, a_slot)],
                    extra=bkq[1].rd(), tag=tag + ".A")
            bkq[1].did_read(ta)
            if kind == "q":
                op("vector", lambda e: e.tensor_tensor(xscr_ap, sinb_ap, scrb_ap,
                   ALU.mult), rd=[(XTB, None)], wr=[(XTB, None)], tag=tag + ".sq")
            else:
                op("vector", lambda e: e.tensor_tensor(xscr_ap, sinb_ap, ROT1.t[:],
                   ALU.mult), rd=[(XTB, None), (ROT1, None)], wr=[(XTB, None)],
                   tag=tag + ".sk")
            def fb(e, _sl=b_slot, _x=xrep, _b=bkq[0]):
                qr = bass.AP(_b, 0, [[512, 128], [0, 16], [1, 64]])
                return e.tensor_tensor(ABT.t[:, _sl * 1024:(_sl + 1) * 1024],
                                       _x, qr, ALU.mult)
            tb = op("vector", fb, rd=[(XTB, None)], wr=[(ABT, b_slot)],
                    extra=bkq[1].rd(), tag=tag + ".B")
            bkq[1].did_read(tb)

        for l in range(2):
            build_ab(l, "q", 0, 1, f"ab{l}q")
            build_ab(l, "k", 2, 3, f"ab{l}k")
            sqr = [ROT1, ROT2]

            for s in range(NSUP):
                alias_from(HBR, STR)
                alias_from(PREQ, OF64)
                alias_from(PREK, OF64)
                xsl = [4 * s + j for j in range(4)]
                for c in range(NCH):
                    if c < 4:
                        op("vector", lambda e, _c=c, _s=s: e.tensor_copy(
                            HBR.t[:, _c * 512:(_c + 1) * 512],
                            XF.t[:, _c * T + SUP * _s:_c * T + SUP * (_s + 1)]),
                            rd=[(XF, xsl)], wr=[(HBR, None)], tag="hb.c")
                    elif s > 0:
                        op("vector", lambda e, _c=c: e.tensor_copy(
                            HBR.t[:, _c * 512:_c * 512 + 1],
                            R2S.t[:, _c - 4:_c - 3]),
                            rd=[(R2S, None)], wr=[(HBR, None)], tag="hb.sb")
                        op("vector", lambda e, _c=c, _s=s: e.tensor_copy(
                            HBR.t[:, _c * 512 + 1:(_c + 1) * 512],
                            XF.t[:, _c * T + SUP * _s:_c * T + SUP * (_s + 1) - 1]),
                            rd=[(XF, xsl)], wr=[(HBR, None)], tag="hb.s")
                    else:
                        op("vector", lambda e, _c=c: e.memset(
                            HBR.t[:, _c * 512:_c * 512 + 1], 0.0),
                            wr=[(HBR, None)], tag="hb.z")
                        op("vector", lambda e, _c=c: e.tensor_copy(
                            HBR.t[:, _c * 512 + 1:(_c + 1) * 512],
                            XF.t[:, _c * T:_c * T + 511]),
                            rd=[(XF, xsl)], wr=[(HBR, None)], tag="hb.s0")
                for ki, (wsrc, dst, dsl) in enumerate(
                        ((w_fq[l], PREQ, None), (w_fk[l], PREK, None),
                         (w_fv[l], VL, None))):
                    bks = banks.take(4)
                    for c in range(NCH):
                        sl = c % 2
                        dma(sQKV[sl], WQKVS.t[:, sl * 1536 + 512 * 0:sl * 1536 + 512],
                            wsrc[128 * c:128 * (c + 1), :], wr=[(WQKVS, sl)],
                            tag="wqkv.ld")
                        for t in range(4):
                            mm(bks[t], bks[t][0][:, :],
                               HBR.t[:, c * 512 + 128 * t:c * 512 + 128 * (t + 1)],
                               WQKVS.t[:, sl * 1536:sl * 1536 + 512],
                               start=(c == 0), stop=(c == NCH - 1),
                               rd=[(WQKVS, sl), (HBR, None)], tag=f"qkv{ki}.mm")
                    for t in range(4):
                        tt = 4 * s + t
                        if ki == 2:
                            tk = op("vector", lambda e, _b=bks[t][0], _tt=tt:
                                    e.tensor_copy(
                                        VL.t[:, (_tt % 8) * 512:(_tt % 8 + 1) * 512],
                                        _b[:, :]),
                                    wr=[(VL, tt % 8)], extra=bks[t][1].rd(),
                                    tag="v.ev")
                        else:
                            dstr = PREQ if ki == 0 else PREK
                            tk = op("vector", lambda e, _b=bks[t][0], _t=t, _d=dstr:
                                    e.tensor_copy(
                                        _d.t[:, _t * 512:(_t + 1) * 512], _b[:, :]),
                                    wr=[(dstr, t)], extra=bks[t][1].rd(),
                                    tag="pre.ev")
                        bks[t][1].did_read(tk)
                for ki in range(2):
                    pre = PREQ if ki == 0 else PREK
                    tokd = QTOK if ki == 0 else KTOK
                    ring = QP if ki == 0 else KP
                    asl, bsl = (0, 1) if ki == 0 else (2, 3)
                    for t in range(4):
                        tt = 4 * s + t
                        pap = pre.t[:, t * 512:(t + 1) * 512]
                        op("scalar", lambda e, _p=pap: e.activation(
                            PR.t[:], _p, AF.Square), rd=[(pre, t)],
                            wr=[(PR, None)], tag="l2.sq")
                        for h in range(8):
                            op("vector", lambda e, _h=h: e.reduce_sum(
                                STAT.t[:, 1008 + _h:1009 + _h],
                                PR.t[:, 64 * _h:64 * (_h + 1)], AX.X),
                                rd=[(PR, None)], wr=[(STAT, None)], tag="l2.red")
                        op("scalar", lambda e: e.activation(
                            STAT.t[:, 1000:1008], STAT.t[:, 1008:1016], AF.Sqrt),
                            rd=[(STAT, None)], wr=[(STAT, None)], tag="l2.sd")
                        op("vector", lambda e: e.tensor_scalar_max(
                            STAT.t[:, 1000:1008], STAT.t[:, 1000:1008], 1e-12),
                            rd=[(STAT, None)], wr=[(STAT, None)], tag="l2.mx")
                        op("vector", lambda e: e.reciprocal(
                            STAT.t[:, 992:1000], STAT.t[:, 1000:1008]),
                            rd=[(STAT, None)], wr=[(STAT, None)], tag="l2.rc")
                        def f_qr(e, _p=pap):
                            rr = bass.AP(STAT.t, 992, [[1024, 128], [1, 8], [0, 64]])
                            return e.tensor_tensor(ROT1.t[:], _p, rr, ALU.mult)
                        op("vector", f_qr, rd=[(pre, t), (STAT, None)],
                           wr=[(ROT1, None)], tag="rot.qr")
                        def f_t1(e, _sl=asl, _t=t, _s=s):
                            arep = bass.AP(ABT.t, _sl * 1024 + (4 * _s + _t) * 64,
                                           [[4096, 128], [0, 8], [1, 64]])
                            return e.tensor_tensor(ROT2.t[:], ROT1.t[:], arep,
                                                   ALU.mult)
                        op("vector", f_t1, rd=[(ROT1, None), (ABT, asl)],
                           wr=[(ROT2, None)], tag="rot.t1")
                        def f_t2(e, _sl=bsl, _t=t, _s=s):
                            brep = bass.AP(ABT.t, _sl * 1024 + (4 * _s + _t) * 64,
                                           [[4096, 128], [0, 8], [1, 64]])
                            return e.tensor_tensor(ROT3.t[:], ROT1.t[:], brep,
                                                   ALU.mult)
                        op("vector", f_t2, rd=[(ROT1, None), (ABT, bsl)],
                           wr=[(ROT3, None)], tag="rot.t2")
                        def f_lo(e):
                            i0 = bass.AP(ROT2.t, 0, [[512, 128], [64, 8], [1, 32]])
                            i1 = bass.AP(ROT3.t, 32, [[512, 128], [64, 8], [1, 32]])
                            o0 = bass.AP(ROT1.t, 0, [[512, 128], [64, 8], [1, 32]])
                            return e.tensor_tensor(o0, i0, i1, ALU.subtract)
                        op("vector", f_lo, rd=[(ROT2, None), (ROT3, None)],
                           wr=[(ROT1, None)], tag="rot.lo")
                        def f_hi(e):
                            i0 = bass.AP(ROT2.t, 32, [[512, 128], [64, 8], [1, 32]])
                            i1 = bass.AP(ROT3.t, 0, [[512, 128], [64, 8], [1, 32]])
                            o0 = bass.AP(ROT1.t, 32, [[512, 128], [64, 8], [1, 32]])
                            return e.tensor_tensor(o0, i0, i1, ALU.add)
                        op("vector", f_hi, rd=[(ROT2, None), (ROT3, None)],
                           wr=[(ROT1, None)], tag="rot.hi")
                        bkt = banks.take()
                        pw = bkt[1].wr()
                        bkt[1].w[0] = []; bkt[1].r[0] = []
                        for cq in range(4):
                            mm_t = op("tensor", lambda e, _b=bkt[0], _c=cq:
                                      e.transpose(_b[:, _c * 128:(_c + 1) * 128],
                                                  ROT1.t[:, _c * 128:(_c + 1) * 128],
                                                  ID_F.t[:]),
                                      rd=[(ROT1, None), (ID_F, None)],
                                      extra=(pw if cq == 0 else ()),
                                      tag="rot.tr")
                            bkt[1].did_write(mm_t)
                        tk = op("vector", lambda e, _b=bkt[0], _sl=tt % 8, _r=ring:
                                e.tensor_copy(
                                    _r.t[:, _sl * 512:(_sl + 1) * 512], _b[:, :]),
                                wr=[(ring, tt % 8)], extra=bkt[1].rd(),
                                tag="rot.ev")
                        bkt[1].did_read(tk)
                op("vector", lambda e, _s=s: e.tensor_copy(
                    R2S.t[:, 0:4],
                    bass.AP(XF.t, 4 * T + SUP * (_s + 1) - 1,
                            [[16384, 128], [T, 4]])),
                    rd=[(XF, 4 * s + 3)], wr=[(R2S, None)], tag="sv1")

                w = s
                alias_from(OF64, PREQ, PREK)
                alias_from(STR, HBR)
                for h in range(8):
                    for j in range(4):
                        if w > 0:
                            runs = [list(range(4 * (w - 1) + j, 4 * w)),
                                    list(range(4 * w, 4 * w + j + 1))]
                        else:
                            runs = [list(range(0, j + 1))]
                        nr = len(runs)
                        lhsq = QP.t[64 * (h % 2):64 * (h % 2) + 64,
                                    ((4 * w + j) % 8) * 512 + (h // 2) * 128:
                                    ((4 * w + j) % 8) * 512 + (h // 2) * 128 + 128]
                        scbs = []
                        for ri, run in enumerate(runs):
                            scb = banks.take()
                            scbs.append(scb)
                            krh = bass.AP(
                                KP.t, 64 * (h % 2) * 4096 +
                                (run[0] % 8) * 512 + (h // 2) * 128,
                                [[4096, 64], [512, len(run)], [1, 128]])
                            mm(scb, scb[0][:, 0:128 * len(run)], lhsq, krh,
                               start=True, stop=True,
                               rd=[(KP, [kt % 8 for kt in run]),
                                   (QP, (4 * w + j) % 8)],
                               tile_position=(64 * (h % 2), 0), ldw=True,
                               tag="la.sc")

                        def addmask(scb, col, mreg):
                            tmk = op("vector", lambda e, _b=scb[0], _c=col,
                                     _m=mreg: e.tensor_tensor(
                                         _b[:, _c * 128:(_c + 1) * 128],
                                         _b[:, _c * 128:(_c + 1) * 128],
                                         _m.t[:], ALU.add),
                                     rd=[(mreg, None)], extra=scb[1].rd(),
                                     tag="la.mask")
                            scb[1].did_read(tmk)
                            scb[1].did_write(tmk)
                        if w > 0:
                            addmask(scbs[0], 0, MASKU)
                        addmask(scbs[-1], len(runs[-1]) - 1, MASKL)
                        for ri, run in enumerate(runs):
                            t_r = op("vector", lambda e, _b=scbs[ri][0], _ri=ri,
                                     _n=128 * len(run): e.reduce_max(
                                         STAT.t[:, 944 + _ri:945 + _ri],
                                         _b[:, 0:_n], AX.X),
                                     wr=[(STAT, None)], extra=scbs[ri][1].rd(),
                                     tag="la.max")
                            scbs[ri][1].did_read(t_r)
                        if nr == 2:
                            op("vector", lambda e: e.tensor_tensor(
                                STAT.t[:, 944:945], STAT.t[:, 944:945],
                                STAT.t[:, 945:946], ALU.max),
                                rd=[(STAT, None)], wr=[(STAT, None)],
                                tag="la.max2")
                        op("vector", lambda e: e.tensor_scalar_mul(
                            STAT.t[:, 946:947], STAT.t[:, 944:945], -1.0),
                            rd=[(STAT, None)], wr=[(STAT, None)], tag="la.neg")
                        for ri, run in enumerate(runs):
                            phd = ROT1 if ri == 0 else ROT2
                            tex = op("scalar", lambda e, _b=scbs[ri][0], _d=phd,
                                     _n=128 * len(run), _ri=ri: e.activation(
                                         _d.t[:, 0:_n], _b[:, 0:_n], AF.Exp,
                                         bias=STAT.t[:, 946:947],
                                         accum_out=STAT.t[:, 948 + _ri:
                                                          949 + _ri]),
                                     rd=[(STAT, None)],
                                     wr=[(phd, None), (STAT, None)],
                                     extra=scbs[ri][1].rd(), tag="la.exp")
                            scbs[ri][1].did_read(tex)
                        if nr == 2:
                            op("vector", lambda e: e.tensor_tensor(
                                STAT.t[:, 948:949], STAT.t[:, 948:949],
                                STAT.t[:, 949:950], ALU.add),
                                rd=[(STAT, None)], wr=[(STAT, None)],
                                tag="la.rs2")
                        op("vector", lambda e: e.reciprocal(
                            STAT.t[:, 950:951], STAT.t[:, 948:949]),
                            rd=[(STAT, None)], wr=[(STAT, None)], tag="la.rc")
                        for ri, run in enumerate(runs):
                            phd = ROT1 if ri == 0 else ROT2
                            op("scalar", lambda e, _d=phd, _n=128 * len(run):
                               e.activation(_d.t[:, 0:_n], _d.t[:, 0:_n],
                                            AF.Identity,
                                            scale=STAT.t[:, 950:951]),
                               rd=[(phd, None), (STAT, None)], wr=[(phd, None)],
                               tag="la.nrm")
                            bkt = banks.take()
                            pw = bkt[1].wr()
                            bkt[1].w[0] = []
                            bkt[1].r[0] = []
                            for ii in range(len(run)):
                                tmt = op("tensor", lambda e, _b=bkt[0], _i=ii,
                                         _d=phd: e.transpose(
                                             _b[:, _i * 128:(_i + 1) * 128],
                                             _d.t[:, _i * 128:(_i + 1) * 128],
                                             ID_F.t[:]),
                                         rd=[(phd, None), (ID_F, None)],
                                         extra=(pw if ii == 0 else ()),
                                         tag="la.tr")
                                bkt[1].did_write(tmt)
                            tev = op("vector", lambda e, _b=bkt[0],
                                     _n=len(run), _k0=run[0], _j=j:
                                     e.tensor_copy(
                                         bass.AP(STR.t,
                                                 (_k0 % 8) * 512 + _j * 128,
                                                 [[4096, 128], [512, _n],
                                                  [1, 128]]),
                                         _b[:, 0:128 * _n]),
                                     wr=[(STR, None)], extra=bkt[1].rd(),
                                     tag="la.trev")
                            bkt[1].did_read(tev)
                    ybk = banks.take()
                    kts_av = list(range(4 * w, 4 * w + 4))
                    if w > 0:
                        kts_av += list(range(4 * w - 4, 4 * w))
                    for ki, kt in enumerate(kts_av):
                        if kt < 4 * w:
                            pp = kt - (4 * w - 4)
                            q0, q1 = 0, 128 * (pp + 1)
                        else:
                            ko = kt - 4 * w
                            q0, q1 = 128 * ko, 512
                        mm(ybk, ybk[0][0:64, q0:q1],
                           VL.t[:, (kt % 8) * 512 + 64 * h:
                                (kt % 8) * 512 + 64 * h + 64],
                           bass.AP(STR.t, (kt % 8) * 512 + q0,
                                   [[4096, 128], [1, q1 - q0]]),
                           start=(ki == 0), stop=(ki == len(kts_av) - 1),
                           rd=[(VL, kt % 8), (STR, None)],
                           tile_position=(0, 0), ldw=True,
                           kwargs_record_write=(ki == len(kts_av) - 1),
                           tag="la.av")
                    tof = op("vector", lambda e, _b=ybk[0], _h=h:
                             e.tensor_copy(
                                 OF64.t[0:64, _h * 512:(_h + 1) * 512],
                                 _b[0:64, :]),
                             wr=[(OF64, None)], extra=ybk[1].rd(), tag="la.oev")
                    ybk[1].did_read(tof)
                bks8 = banks.take(8)
                for h in range(8):
                    sl = h % 2
                    dma(sWO[sl], WOS.t[0:64, sl * 1024:(sl + 1) * 1024],
                        w_o[l][0:64, D * h:D * (h + 1)], wr=[(WOS, sl)],
                        tag="wo.ld")
                    for m in range(NCH):
                        mm(bks8[m], bks8[m][0][:, :],
                           WOS.t[0:64, sl * 1024 + 128 * m:
                                 sl * 1024 + 128 * (m + 1)],
                           OF64.t[0:64, h * 512:(h + 1) * 512],
                           start=(h == 0), stop=(h == 7),
                           rd=[(WOS, sl), (OF64, None)],
                           skip_gc=(h not in (0, 7)),
                           kwargs_record_write=(h == 7), tag="wo.mm")
                for m in range(NCH):
                    wsl = [4 * w + jj for jj in range(4)]
                    tup = op("vector", lambda e, _b=bks8[m][0], _m=m, _w=w:
                             e.tensor_tensor(
                                 XF.t[:, _m * T + SUP * _w:
                                      _m * T + SUP * (_w + 1)],
                                 XF.t[:, _m * T + SUP * _w:
                                      _m * T + SUP * (_w + 1)],
                                 _b[:, :], ALU.add),
                             rd=[(XF, wsl)], wr=[(XF, wsl)],
                             extra=bks8[m][1].rd(), tag="wo.up")
                    bks8[m][1].did_read(tup)

                op("vector", lambda e, _s=s, _d=4 + 4 * (s % 2): e.tensor_copy(
                    R2S.t[:, _d:_d + 4],
                    bass.AP(XF.t, 4 * T + SUP * (_s + 1) - 1,
                            [[16384, 128], [T, 4]])),
                    rd=[(XF, 4 * s + 3)], wr=[(R2S, None)], tag="sv2")
                if os.environ.get("KDBG"):
                    for c in range(NCH):
                        dma(sDBG, dbg_mid[l][128 * c:128 * (c + 1),
                                             SUP * s:SUP * (s + 1)],
                            XF.t[:, c * T + SUP * s:c * T + SUP * (s + 1)],
                            rd=[(XF, [4 * s + jj for jj in range(4)])],
                            tag="dbg.mid")

                alias_from(HBR, STR)
                xsl = [4 * s + j for j in range(4)]
                sv2rd = 4 + 4 * ((s + 1) % 2)
                bkA = banks.take(); bkB = banks.take()
                for c in range(NCH):
                    sl = c % 2
                    bsrc = None
                    if c < 4:
                        xap = XF.t[:, c * T + SUP * s:c * T + SUP * (s + 1)]
                        sqs, sqn = 0, 512
                    elif s > 0:
                        xap = XF.t[:, c * T + SUP * s:c * T + SUP * (s + 1) - 1]
                        sqs, sqn = 1, 511
                        bsrc = R2S.t[:, sv2rd + c - 4:sv2rd + c - 3]
                    else:
                        xap = XF.t[:, c * T:c * T + 511]
                        sqs, sqn = 1, 511
                    op("scalar", lambda e, _x=xap, _sl=sl, _s0=sqs, _n=sqn:
                       e.activation(sqr[_sl].t[:, _s0:_s0 + _n],
                                    _x, AF.Square),
                       rd=[(XF, xsl)], wr=[(sqr[sl], None)], tag="fh.sq")
                    if bsrc is not None:
                        op("scalar", lambda e, _b=bsrc, _sl=sl: e.activation(
                            sqr[_sl].t[:, 0:1], _b, AF.Square),
                            rd=[(R2S, None)], wr=[(sqr[sl], None)], tag="fh.sqb")
                        mm(bkA, bkA[0][0:1, 0:1], ONES_F.t[:, 0:1], bsrc,
                           start=False, stop=False, rd=[(R2S, None)],
                           skip_gc=True, tag="fh.sxb")
                    mm(bkA, bkA[0][0:1, sqs:sqs + sqn], ONES_F.t[:, 0:1], xap,
                       start=(c == 0), stop=(c == NCH - 1), rd=[(XF, xsl)],
                       skip_gc=(c not in (0, NCH - 1)),
                       tag="fh.sx")
                    sq0, sqN = (1, 511) if (c >= 4 and s == 0) else (0, 512)
                    mm(bkB, bkB[0][0:1, sq0:sq0 + sqN], ONES_F.t[:, 0:1],
                       sqr[sl].t[:, sq0:sq0 + sqN],
                       start=(c == 0), stop=(c == NCH - 1), rd=[(sqr[sl], None)],
                       skip_gc=(c not in (0, NCH - 1)),
                       tag="fh.sx2")
                A = STAT.t[0:1, 0:512]
                B = STAT.t[0:1, 512:1024]
                tmp = sqr[0].t[0:1, 0:512]
                t1 = op("vector", lambda e, _b=bkA[0]: e.tensor_scalar_mul(
                        A, _b[0:1, :], -1.0 / 1024.0), wr=[(STAT, None)],
                        extra=bkA[1].rd(), tag="fh.m")
                bkA[1].did_read(t1)
                t2 = op("vector", lambda e, _b=bkB[0]: e.tensor_scalar_mul(
                        B, _b[0:1, :], 1.0 / 1024.0), wr=[(STAT, None)],
                        extra=bkB[1].rd(), tag="fh.e2")
                bkB[1].did_read(t2)
                op("vector", lambda e: e.tensor_tensor(tmp, A, A,
                   ALU.mult), rd=[(STAT, None)], wr=[(sqr[0], None)],
                   tag="fh.msq")
                op("vector", lambda e: e.tensor_tensor(B, B, tmp,
                   ALU.subtract), rd=[(STAT, None), (sqr[0], None)],
                   wr=[(STAT, None)], tag="fh.var")
                op("scalar", lambda e: e.activation(B, B, AF.Sqrt,
                   bias=DC.t[0:1, 10:11]), rd=[(STAT, None), (DC, None)],
                   wr=[(STAT, None)], tag="fh.sd")
                op("vector", lambda e: e.reciprocal(B, B),
                   rd=[(STAT, None)], wr=[(STAT, None)], tag="fh.rstd")
                bkR = banks.take(); bkM = banks.take()
                mm(bkR, bkR[0][:, :], ONES_F.t[0:1, 0:128], B,
                   start=True, stop=True, rd=[(STAT, None)], tag="fh.bcr")
                mm(bkM, bkM[0][:, :], ONES_F.t[0:1, 0:128], A,
                   start=True, stop=True, rd=[(STAT, None)], tag="fh.bcm")
                for c in range(NCH):
                    sl = c % 2
                    bsrc = None
                    if c < 4:
                        hap = XF.t[:, c * T + SUP * s:c * T + SUP * (s + 1)]
                        c0 = 0
                    elif s > 0:
                        hap = XF.t[:, c * T + SUP * s:c * T + SUP * (s + 1) - 1]
                        c0 = 1
                        bsrc = R2S.t[:, sv2rd + c - 4:sv2rd + c - 3]
                    else:
                        hap = XF.t[:, c * T:c * T + 511]
                        c0 = 1
                    n = 512 - c0
                    tk1 = op("vector", lambda e, _h=hap, _sl=sl, _c0=c0, _n=n,
                             _b=bkM[0]: e.tensor_tensor(
                                 sqr[_sl].t[:, _c0:_c0 + _n],
                                 _h, _b[:, _c0:_c0 + _n], ALU.add),
                             rd=[(XF, xsl)], wr=[(sqr[sl], None)],
                             extra=bkM[1].rd(), tag="fh.a1")
                    bkM[1].did_read(tk1)
                    if bsrc is not None:
                        tkb = op("vector", lambda e, _b2=bsrc, _sl=sl,
                                 _b=bkM[0]: e.tensor_tensor(
                                     sqr[_sl].t[:, 0:1], _b2,
                                     _b[:, 0:1], ALU.add),
                                 rd=[(R2S, None)], wr=[(sqr[sl], None)],
                                 extra=bkM[1].rd(), tag="fh.a1b")
                        bkM[1].did_read(tkb)
                    elif c0 == 1:
                        tkz = op("vector", lambda e, _sl=sl, _b=bkM[0]:
                                 e.tensor_copy(sqr[_sl].t[:, 0:1], _b[:, 0:1]),
                                 wr=[(sqr[sl], None)], extra=bkM[1].rd(),
                                 tag="fh.a1z")
                        bkM[1].did_read(tkz)
                    tk2 = op("vector", lambda e, _sl=sl, _b=bkR[0]:
                             e.tensor_tensor(
                                 sqr[_sl].t[:, :],
                                 sqr[_sl].t[:, :],
                                 _b[:, :], ALU.mult),
                             rd=[(sqr[sl], None)], wr=[(sqr[sl], None)],
                             extra=bkR[1].rd(), tag="fh.a2")
                    bkR[1].did_read(tk2)
                    op("scalar", lambda e, _c=c, _sl=sl, _l=l:
                       e.activation(
                           HBR.t[:, _c * 512:(_c + 1) * 512],
                           sqr[_sl].t[:, :], AF.Identity,
                           scale=VEC.t[:, 8 + 16 * _l + _c:9 + 16 * _l + _c],
                           bias=VEC.t[:, 16 + 16 * _l + _c:17 + 16 * _l + _c]),
                       rd=[(sqr[sl], None), (VEC, None)], wr=[(HBR, None)],
                       tag="fh.a3")
                if os.environ.get("KDBG") and s == 0 and l == 0:
                    dma(sDBG, dbg_hn[:, :], HBR.t[:, :], rd=[(HBR, None)],
                        tag="dbg.hn")
                for j in range(22):
                    sl = j % 2
                    dma(sW1[sl], W1S.t[:, sl * 2048:(sl + 1) * 2048],
                        w_1[l][128 * j:128 * (j + 1), :],
                        wr=[(W1S, sl)], tag="w1.ld")
                    bkA2 = banks.take(); bkG2 = banks.take()
                    for c in range(NCH):
                        mm(bkA2, bkA2[0][:, :],
                           W1S.t[:, sl * 2048 + c * 256:sl * 2048 + c * 256 + 128],
                           HBR.t[:, c * 512:(c + 1) * 512],
                           start=(c == 0), stop=(c == NCH - 1),
                           rd=[(W1S, sl), (HBR, None)], tag="w1.a")
                        mm(bkG2, bkG2[0][:, :],
                           W1S.t[:, sl * 2048 + c * 256 + 128:
                                 sl * 2048 + c * 256 + 256],
                           HBR.t[:, c * 512:(c + 1) * 512],
                           start=(c == 0), stop=(c == NCH - 1),
                           rd=[(W1S, sl), (HBR, None)], tag="w1.g")
                    ta = op("scalar", lambda e, _b=bkA2[0], _j=j, _l=l:
                        e.activation(
                            A16.t[:], _b[:, :], AF.Identity,
                            bias=B1V.t[:, 44 * _l + _j:44 * _l + _j + 1]),
                        rd=[(B1V, None)], wr=[(A16, None)],
                        extra=bkA2[1].rd(), tag="w1.aev")
                    bkA2[1].did_read(ta)
                    tg = op("scalar", lambda e, _b=bkG2[0], _j=j, _l=l:
                        e.activation(
                            G16.t[:], _b[:, :], AF.Gelu,
                            bias=B1V.t[:, 44 * _l + 22 + _j:44 * _l + 23 + _j]),
                        rd=[(B1V, None)], wr=[(G16, None)],
                        extra=bkG2[1].rd(), tag="w1.gev")
                    bkG2[1].did_read(tg)
                    op("vector", lambda e, _j=j: e.tensor_tensor(
                        H2R.t[:, _j * 512:(_j + 1) * 512], A16.t[:], G16.t[:],
                        ALU.mult), rd=[(A16, None), (G16, None)],
                        wr=[(H2R, None)], tag="w1.h2")
                if os.environ.get("KDBG") and s == 0 and l == 0:
                    dma(sDBG, dbg_h2[:, :], H2R.t[:, :], rd=[(H2R, None)],
                        tag="dbg.h2")
                bks8b = banks.take(8)
                for j in range(22):
                    sl = j % 2
                    dma(sW2b[sl], W2S.t[:, sl * 1024:(sl + 1) * 1024],
                        w_2[l][128 * j:128 * (j + 1), :], wr=[(W2S, sl)],
                        tag="w2.ld")
                    for m in range(NCH):
                        mm(bks8b[m], bks8b[m][0][:, :],
                           W2S.t[:, sl * 1024 + 128 * m:sl * 1024 + 128 * (m + 1)],
                           H2R.t[:, j * 512:(j + 1) * 512],
                           start=(j == 0), stop=(j == 21),
                           rd=[(W2S, sl), (H2R, None)], tag="w2.mm")
                for m in range(NCH):
                    ssl = [4 * s + jj for jj in range(4)]
                    tpp = op("scalar", lambda e, _b=bks8b[m][0], _m=m, _l=l:
                             e.activation(
                                 PR.t[:], _b[:, :], AF.Identity,
                                 bias=VEC.t[:, 40 + 8 * _l + _m:
                                            41 + 8 * _l + _m]),
                             rd=[(VEC, None)], wr=[(PR, None)],
                             extra=bks8b[m][1].rd(), tag="w2.upb")
                    bks8b[m][1].did_read(tpp)
                    op("vector", lambda e, _m=m, _s=s: e.tensor_tensor(
                        XF.t[:, _m * T + SUP * _s:_m * T + SUP * (_s + 1)],
                        XF.t[:, _m * T + SUP * _s:_m * T + SUP * (_s + 1)],
                        PR.t[:], ALU.add),
                        rd=[(XF, ssl), (PR, None)], wr=[(XF, ssl)],
                        tag="w2.up")
                if os.environ.get("KDBG"):
                    for c in range(NCH):
                        dma(sDBG, dbg_out[l][128 * c:128 * (c + 1),
                                             SUP * s:SUP * (s + 1)],
                            XF.t[:, c * T + SUP * s:c * T + SUP * (s + 1)],
                            rd=[(XF, [4 * s + jj for jj in range(4)])],
                            tag="dbg.out")

        sGL = p.stream("sync", "gl")
        sX1B = p.stream("sync", "x1b")
        XN1 = mem.at("b_xn1", oXN, [128, 8 * T], BF16, ("l1b",), nslots=NSUP)
        KROT = mem.at("b_krot", oKR, [128, 8 * T], BF16, ("l1b",), nslots=NSUP)
        VTB = mem.at("b_vtb", oVR, [128, NTOK * D], BF16, ("l1b",), nslots=NTOK)
        WVR = mem.at("b_wvr", oWR, [128, 8 * D], BF16, ("l1bw",))
        WQR1 = mem.at("b_wqr", oWR + 16384, [128, 8 * D], BF16, ("l1b",))
        PT1 = mem.at("b_pt", oWR + 32768, [128, 16 * 256], BF16, ("l1b",))
        X1S2 = mem.at("b_x1s", oFA, [128, 8 * 512], F32, ("l1bln",))
        QROT = mem.at("b_qrot", oFA, [128, 4096], BF16, ("l1batt",))
        PRE1 = mem.at("b_pre", oFA + 8192, [128, 4096], BF16, ("l1bq",))
        SQB = mem.at("b_sqb", oFA + 32768, [128, 1024], F32, ("l1bln2",), nslots=2)
        PHAT = mem.at("b_phat", oFA + 8192, [128, 2048], BF16, ("l1batt2",))
        GRB = mem.at("b_grb", oFA + 12288, [128, 512], F32, ("l1batt2",))
        X1RB = mem.at("b_x1rb", oFA + 14336, [128, 512], F32, ("l1batt2",))
        COSB1 = mem.at("b_cosb", oFA + 16384, [128, 2048], F32, ("l1b",))
        SINB1 = mem.at("b_sinb", oFA + 24576, [128, 2048], F32, ("l1b",))
        CS1 = mem.at("b_cs", oFA + 32768, [128, 2048], F16, ("l1b",))
        SS1 = mem.at("b_ss", oFA + 36864, [128, 2048], F16, ("l1b",))
        ROTA = mem.at("b_rota", oWS + 8192, [128, 512], F16, ("l1b",))
        ROTB = mem.at("b_rotb", oWS + 9216, [128, 512], F16, ("l1b",))
        THR = mem.at("b_thr", oWS + 12288, [1, 512], F32, ("l1b",))
        OUTS = mem.at("b_outs", oWS + 10240, [128, 512], F32, ("l1b",))

        alias_from(XN1, XF)
        alias_from(KROT, XF)
        alias_from(VTB, VL, HBR, PREQ, PREK, QTOK, KTOK, OF64, STR, R2S)
        alias_from(WVR, QP, KP, WQKVS)
        alias_from(WQR1, WQKVS, WOS, W1S)
        alias_from(PT1, W1S, W2S)
        alias_from(X1S2, XTB, ABT, SQ2)
        alias_from(PRE1, ABT, H2R)
        alias_from(SQB, ABT, H2R)
        alias_from(GRB, H2R)
        alias_from(X1RB, H2R)
        alias_from(COSB1, H2R)
        alias_from(SINB1, H2R, A16)
        alias_from(CS1, G16)
        alias_from(SS1, G16)
        alias_from(ROTA, ROT3)
        alias_from(ROTB, ROT3)
        alias_from(THR, RSR)
        alias_from(OUTS, RSR)

        for s in range(NSUP):
            for c in range(NCH):
                tg = op("scalar", lambda e, _c=c, _s=s: e.activation(
                    OUTS.t[:], XF.t[:, _c * T + SUP * _s:_c * T + SUP * (_s + 1)],
                    AF.Silu), rd=[(XF, [4 * s + j for j in range(4)])],
                    wr=[(OUTS, None)], tag="g.silu")
                dma(sG, hbm_g[128 * c:128 * (c + 1), SUP * s:SUP * (s + 1)],
                    OUTS.t[:], rd=[(OUTS, None)], wr=[(HG, s)], tag="g.st")

        op("scalar", lambda e: e.activation(THR.t[:], ETHR.t[:], AF.Exp,
           scale=DC.t[0:1, 3:4]), rd=[(ETHR, None), (DC, None)], wr=[(THR, None)],
           tag="rb.th")
        for c in range(4):
            bka = banks.take()
            mm(bka, bka[0][:, :], THR.t[0:1, 128 * c:128 * (c + 1)],
               POSR.t[0:1, :], start=True, stop=True,
               rd=[(THR, None), (POSR, None)], tag="rb.ang")
            cw_sincos(bka[0], bka[1],
                      SINB1.t[:, 512 * c:512 * (c + 1)],
                      COSB1.t[:, 512 * c:512 * (c + 1)],
                      (SINB1, None) if True else None, f"rb.{c}")
            COSB1.did_write(SINB1.w[0][-1])

        def cw_col(src_ap, dst_sin_ap, dst_cos_ap, tag):
            A1 = STAT.t[:, 980:984]; A2 = STAT.t[:, 984:988]
            G = STAT.t[:, 944:948]; J = STAT.t[:, 948:952]
            def one_pass(bias_col, kq, dst, ptag):
                op("scalar", lambda e, _bc=bias_col: e.activation(
                   A1, src_ap, AF.Identity, scale=DC.t[:, 4:5], bias=_bc),
                   rd=[(STAT, None), (DC, None), (CC, None)],
                   wr=[(STAT, None)], tag=ptag + ".y")
                op("vector", lambda e: e.tensor_copy(A2, A1),
                   rd=[(STAT, None)], wr=[(STAT, None)], tag=ptag + ".fc")
                for _b in (512.0, 256.0, 128.0, 64.0, 32.0, 16.0, 8.0, 4.0,
                           2.0, 1.0):
                    op("vector", lambda e, _bb=_b: e.tensor_scalar(
                        G, A2, _bb, None, ALU.is_ge),
                        rd=[(STAT, None)], wr=[(STAT, None)], tag=ptag + ".fg")
                    op("vector", lambda e, _bb=_b: e.tensor_scalar_mul(
                        J, G, -_bb),
                        rd=[(STAT, None)], wr=[(STAT, None)], tag=ptag + ".fm")
                    op("vector", lambda e: e.tensor_tensor(A2, A2, J, ALU.add),
                        rd=[(STAT, None)], wr=[(STAT, None)], tag=ptag + ".fs")
                op("vector", lambda e: e.tensor_scalar_mul(J, G, -2.0),
                   rd=[(STAT, None)], wr=[(STAT, None)], tag=ptag + ".sg1")
                op("vector", lambda e: e.tensor_scalar_add(J, J, 1.0),
                   rd=[(STAT, None)], wr=[(STAT, None)], tag=ptag + ".sg2")
                op("vector", lambda e: e.tensor_tensor(A1, A1, A2, ALU.subtract),
                   rd=[(STAT, None)], wr=[(STAT, None)], tag=ptag + ".k")
                if kq:
                    op("vector", lambda e: e.tensor_scalar_add(A1, A1, kq),
                       rd=[(STAT, None)], wr=[(STAT, None)], tag=ptag + ".kq")
                op("vector", lambda e: e.tensor_scalar_mul(G, A1, -_C1H),
                   rd=[(STAT, None)], wr=[(STAT, None)], tag=ptag + ".r1m")
                op("vector", lambda e: e.tensor_tensor(A2, src_ap, G, ALU.add),
                   rd=[(STAT, None)], wr=[(STAT, None)], tag=ptag + ".r1")
                for cv, nm in ((_C2H, ".r2"), (_C3H, ".r3")):
                    op("vector", lambda e, _cv=cv: e.tensor_scalar_mul(
                        G, A1, -_cv),
                        rd=[(STAT, None)], wr=[(STAT, None)], tag=ptag + nm + "m")
                    op("vector", lambda e: e.tensor_tensor(A2, A2, G, ALU.add),
                       rd=[(STAT, None)], wr=[(STAT, None)], tag=ptag + nm)
                op("vector", lambda e: e.tensor_single_scalar(A2, A2, _PI_LO,
                   ALU.min), rd=[(STAT, None)], wr=[(STAT, None)],
                   tag=ptag + ".clA")
                op("vector", lambda e: e.tensor_single_scalar(A2, A2, -_PI_LO,
                   ALU.max), rd=[(STAT, None)], wr=[(STAT, None)],
                   tag=ptag + ".cl")
                op("scalar", lambda e: e.activation(A2, A2, AF.Sin),
                   rd=[(STAT, None)], wr=[(STAT, None)], tag=ptag + ".sin")
                op("vector", lambda e, _d=dst: e.tensor_tensor(_d, A2, J,
                   ALU.mult), rd=[(STAT, None)], wr=[(STAT, None)],
                   tag=ptag + ".ap")
            one_pass(CC.t[:, 5:6], None, dst_sin_ap, tag + ".s")
            one_pass(CC.t[:, 6:7], -0.5, dst_cos_ap, tag + ".c")

        for s in range(NSUP):
            for c in range(NCH):
                dma(sX1B, X1S2.t[:, c * 512:(c + 1) * 512],
                    hbm_x1[128 * c:128 * (c + 1), SUP * s:SUP * (s + 1)],
                    rd=[(HX1, s)], wr=[(X1S2, 0)], tag="b.x1ld")
            seal(X1S2, None, sX1B)
            ln_stats_apply(
                x_rd=lambda c, _s=s: (X1S2, 0),
                x_chunk_ap=lambda c, _s=s: X1S2.t[:, c * 512:(c + 1) * 512],
                dst_wr=lambda c, _s=s: (XN1, _s),
                dst_ap=lambda c, _s=s: XN1.t[:, c * T + SUP * _s:
                                            c * T + SUP * (_s + 1)],
                sq=SQB, sq_ap=lambda sl: SQB.t[:, sl * 512:(sl + 1) * 512],
                tag=f"bln.{s}")

        if os.environ.get("KDBG"):
            dma(sDBG, dbg_xn1[:, :], XN1.t[:, :],
                rd=[(XN1, list(range(NSUP)))], tag="dbg.xn1")
        alias_from(ROTA, ROT3)
        alias_from(ROTB, ROT3)
        op("scalar", lambda e: e.activation(STAT.t[:, 976:980], ETHC.t[:], AF.Exp,
           scale=DC.t[:, 3:4]), rd=[(ETHC, None), (DC, None)], wr=[(STAT, None)],
           tag="rb.thc")
        for c in range(NCH):
            dma(sWB, WVR.t[:, c * D:(c + 1) * D], w_l1v[128 * c:128 * (c + 1), :],
                wr=[(WVR, None)], tag="b.wv")
        seal(WVR, None, sWB)
        for tt in range(NTOK):
            s = tt // 4
            for h in range(2):
                bk = banks.take()
                for c in range(NCH):
                    mm(bk, bk[0][:, :],
                       XN1.t[:, c * T + TILE * tt:c * T + TILE * (tt + 1)],
                       WVR.t[:, c * D + 512 * h:c * D + 512 * (h + 1)],
                       start=(c == 0), stop=(c == NCH - 1),
                       rd=[(WVR, None), (XN1, s)], tag="b.vmm")
                vt = op("vector", lambda e, _b=bk[0], _t=tt, _h=h: e.tensor_copy(
                    VTB.t[:, _t * D + 512 * _h:_t * D + 512 * (_h + 1)], _b[:, :]),
                    wr=[(VTB, tt)], extra=bk[1].rd(), tag="b.vev")
                bk[1].did_read(vt)
        if os.environ.get("KDBG"):
            dma(sDBG, dbg_vtb[:, :], VTB.t[:, :],
                rd=[(VTB, list(range(NTOK)))], tag="dbg.vtb")
        alias_from(CS1, SQB, G16)
        for c in range(NCH):
            dma(sWB, WVR.t[:, c * D:(c + 1) * D], w_l1k[128 * c:128 * (c + 1), :],
                wr=[(WVR, None)], tag="b.wk")
        seal(WVR, None, sWB)
        for s in range(NSUP):
            op("vector", lambda e, _s=s: e.tensor_scalar_mul(
                STAT.t[:, 988:992], STAT.t[:, 976:980], float(512 * _s)),
                rd=[(STAT, None)], wr=[(STAT, None)], tag="b.angs")
            cw_col(STAT.t[:, 988:992], STAT.t[:, 960:964], STAT.t[:, 964:968],
                   f"b.cw{s}")
            for c in range(4):
                op("vector", lambda e, _c=c: e.tensor_scalar_mul(
                    ROT1.t[:], SINB1.t[:, 512 * _c:512 * (_c + 1)],
                    STAT.t[:, 960 + _c:961 + _c]),
                    rd=[(SINB1, None), (STAT, None)],
                    wr=[(ROT1, None)], tag="b.ts")
                op("scalar", lambda e, _c=c: e.activation(
                    ROT2.t[:], COSB1.t[:, 512 * _c:512 * (_c + 1)], AF.Identity,
                    scale=STAT.t[:, 964 + _c:965 + _c]),
                    rd=[(COSB1, None), (STAT, None)], wr=[(ROT2, None)],
                    tag="b.csa")
                op("vector", lambda e, _c=c: e.tensor_tensor(
                    CS1.t[:, 512 * _c:512 * (_c + 1)],
                    ROT2.t[:], ROT1.t[:], ALU.subtract),
                    rd=[(ROT2, None), (ROT1, None)],
                    wr=[(CS1, None)], tag="b.cs")
                op("vector", lambda e, _c=c: e.tensor_scalar_mul(
                    ROT1.t[:], COSB1.t[:, 512 * _c:512 * (_c + 1)],
                    STAT.t[:, 960 + _c:961 + _c]),
                    rd=[(COSB1, None), (STAT, None)],
                    wr=[(ROT1, None)], tag="b.tc")
                op("scalar", lambda e, _c=c: e.activation(
                    ROT2.t[:], SINB1.t[:, 512 * _c:512 * (_c + 1)], AF.Identity,
                    scale=STAT.t[:, 964 + _c:965 + _c]),
                    rd=[(SINB1, None), (STAT, None)], wr=[(ROT2, None)],
                    tag="b.ssa")
                op("vector", lambda e, _c=c: e.tensor_tensor(
                    SS1.t[:, 512 * _c:512 * (_c + 1)],
                    ROT2.t[:], ROT1.t[:], ALU.add),
                    rd=[(ROT2, None), (ROT1, None)],
                    wr=[(SS1, None)], tag="b.ss")
            for m in range(NCH):
                bk = banks.take()
                for c in range(NCH):
                    mm(bk, bk[0][:, :],
                       WVR.t[:, c * D + 128 * m:c * D + 128 * (m + 1)],
                       XN1.t[:, c * T + SUP * s:c * T + SUP * (s + 1)],
                       start=(c == 0), stop=(c == NCH - 1),
                       rd=[(WVR, None), (XN1, s)], tag="b.kmm")
                tk = op("vector", lambda e, _b=bk[0], _m=m: e.tensor_copy(
                    PRE1.t[:, _m * 512:(_m + 1) * 512], _b[:, :]),
                    wr=[(PRE1, None)], extra=bk[1].rd(), tag="b.kev")
                bk[1].did_read(tk)
            for c in range(4):
                pe = PRE1.t[:, c * 512:(c + 1) * 512]
                po = PRE1.t[:, (c + 4) * 512:(c + 5) * 512]
                cs_ap = CS1.t[:, 512 * c:512 * (c + 1)]
                ss_ap = SS1.t[:, 512 * c:512 * (c + 1)]
                op("vector", lambda e, _pe=pe, _cs=cs_ap: e.tensor_tensor(
                    ROTA.t[:], _pe, _cs, ALU.mult),
                    rd=[(PRE1, None), (CS1, None)], wr=[(ROTA, None)], tag="b.k1")
                op("vector", lambda e, _po=po, _ss=ss_ap: e.tensor_tensor(
                    ROTB.t[:], _po, _ss, ALU.mult),
                    rd=[(PRE1, None), (SS1, None)], wr=[(ROTB, None)], tag="b.k2")
                op("vector", lambda e, _c=c, _s=s: e.tensor_tensor(
                    KROT.t[:, _c * T + SUP * _s:_c * T + SUP * (_s + 1)],
                    ROTA.t[:], ROTB.t[:], ALU.add),
                    rd=[(ROTA, None), (ROTB, None)], wr=[(KROT, s)], tag="b.k3")
                op("vector", lambda e, _po=po, _cs=cs_ap: e.tensor_tensor(
                    ROTA.t[:], _po, _cs, ALU.mult),
                    rd=[(PRE1, None), (CS1, None)], wr=[(ROTA, None)], tag="b.k4")
                op("vector", lambda e, _pe=pe, _ss=ss_ap: e.tensor_tensor(
                    ROTB.t[:], _pe, _ss, ALU.mult),
                    rd=[(PRE1, None), (SS1, None)], wr=[(ROTB, None)], tag="b.k5")
                op("vector", lambda e, _c=c, _s=s: e.tensor_tensor(
                    KROT.t[:, (_c + 4) * T + SUP * _s:(_c + 4) * T + SUP * (_s + 1)],
                    ROTA.t[:], ROTB.t[:], ALU.subtract),
                    rd=[(ROTA, None), (ROTB, None)], wr=[(KROT, s)], tag="b.k6")
        if os.environ.get("KDBG"):
            dma(sDBG, dbg_kr[:, :], KROT.t[:, :],
                rd=[(KROT, list(range(NSUP)))], tag="dbg.kr")
        for c in range(NCH):
            dma(sWB, WQR1.t[:, c * D:(c + 1) * D], w_l1q[128 * c:128 * (c + 1), :],
                wr=[(WQR1, None)], tag="b.wq")
        seal(WQR1, None, sWB)
        alias_from(GRB, X1S2, H2R)
        alias_from(X1RB, X1S2, H2R)
        alias_from(QROT, X1S2, XTB)
        for s in range(NSUP):
            op("vector", lambda e, _s=s: e.tensor_scalar_mul(
                STAT.t[:, 988:992], STAT.t[:, 976:980], float(512 * _s)),
                rd=[(STAT, None)], wr=[(STAT, None)], tag="b.angs2")
            cw_col(STAT.t[:, 988:992], STAT.t[:, 960:964], STAT.t[:, 964:968],
                   f"b.cw2{s}")
            for c in range(4):
                op("vector", lambda e, _c=c: e.tensor_scalar_mul(
                    ROT1.t[:], SINB1.t[:, 512 * _c:512 * (_c + 1)],
                    STAT.t[:, 960 + _c:961 + _c]),
                    rd=[(SINB1, None), (STAT, None)],
                    wr=[(ROT1, None)], tag="b.ts2")
                op("scalar", lambda e, _c=c: e.activation(
                    ROT2.t[:], COSB1.t[:, 512 * _c:512 * (_c + 1)], AF.Identity,
                    scale=STAT.t[:, 964 + _c:965 + _c]),
                    rd=[(COSB1, None), (STAT, None)], wr=[(ROT2, None)],
                    tag="b.cs2a")
                op("vector", lambda e, _c=c: e.tensor_tensor(
                    CS1.t[:, 512 * _c:512 * (_c + 1)],
                    ROT2.t[:], ROT1.t[:], ALU.subtract),
                    rd=[(ROT2, None), (ROT1, None)],
                    wr=[(CS1, None)], tag="b.cs2")
                op("vector", lambda e, _c=c: e.tensor_scalar_mul(
                    ROT1.t[:], COSB1.t[:, 512 * _c:512 * (_c + 1)],
                    STAT.t[:, 960 + _c:961 + _c]),
                    rd=[(COSB1, None), (STAT, None)],
                    wr=[(ROT1, None)], tag="b.tc2")
                op("scalar", lambda e, _c=c: e.activation(
                    ROT2.t[:], SINB1.t[:, 512 * _c:512 * (_c + 1)], AF.Identity,
                    scale=STAT.t[:, 964 + _c:965 + _c]),
                    rd=[(SINB1, None), (STAT, None)], wr=[(ROT2, None)],
                    tag="b.ss2a")
                op("vector", lambda e, _c=c: e.tensor_tensor(
                    SS1.t[:, 512 * _c:512 * (_c + 1)],
                    ROT2.t[:], ROT1.t[:], ALU.add),
                    rd=[(ROT2, None), (ROT1, None)],
                    wr=[(SS1, None)], tag="b.ss2")
            alias_from(PRE1, PHAT, GRB, X1RB)
            for mh in range(2):
                bks = banks.take(4)
                for c in range(NCH):
                    for mi in range(4):
                        m = mh * 4 + mi
                        mm(bks[mi], bks[mi][0][:, :],
                           WQR1.t[:, c * D + 128 * m:c * D + 128 * (m + 1)],
                           XN1.t[:, c * T + SUP * s:c * T + SUP * (s + 1)],
                           start=(c == 0), stop=(c == NCH - 1),
                           rd=[(WQR1, None), (XN1, s)], tag="b.qmm")
                for mi in range(4):
                    m = mh * 4 + mi
                    tk = op("vector", lambda e, _b=bks[mi][0], _m=m:
                            e.tensor_scalar_mul(
                                PRE1.t[:, _m * 512:(_m + 1) * 512], _b[:, :],
                                0.03125),
                            wr=[(PRE1, None)], extra=bks[mi][1].rd(), tag="b.qev")
                    bks[mi][1].did_read(tk)
            for c in range(4):
                pe = PRE1.t[:, c * 512:(c + 1) * 512]
                po = PRE1.t[:, (c + 4) * 512:(c + 5) * 512]
                cs_ap = CS1.t[:, 512 * c:512 * (c + 1)]
                ss_ap = SS1.t[:, 512 * c:512 * (c + 1)]
                op("vector", lambda e, _pe=pe, _cs=cs_ap: e.tensor_tensor(
                    ROTA.t[:], _pe, _cs, ALU.mult),
                    rd=[(PRE1, None), (CS1, None)], wr=[(ROTA, None)], tag="b.q1")
                op("vector", lambda e, _po=po, _ss=ss_ap: e.tensor_tensor(
                    ROTB.t[:], _po, _ss, ALU.mult),
                    rd=[(PRE1, None), (SS1, None)], wr=[(ROTB, None)], tag="b.q2")
                op("vector", lambda e, _c=c: e.tensor_tensor(
                    QROT.t[:, _c * 512:(_c + 1) * 512],
                    ROTA.t[:], ROTB.t[:], ALU.add),
                    rd=[(ROTA, None), (ROTB, None)], wr=[(QROT, None)], tag="b.q3")
                op("vector", lambda e, _po=po, _cs=cs_ap: e.tensor_tensor(
                    ROTA.t[:], _po, _cs, ALU.mult),
                    rd=[(PRE1, None), (CS1, None)], wr=[(ROTA, None)], tag="b.q4")
                op("vector", lambda e, _pe=pe, _ss=ss_ap: e.tensor_tensor(
                    ROTB.t[:], _pe, _ss, ALU.mult),
                    rd=[(PRE1, None), (SS1, None)], wr=[(ROTB, None)], tag="b.q5")
                op("vector", lambda e, _c=c: e.tensor_tensor(
                    QROT.t[:, (_c + 4) * 512:(_c + 5) * 512],
                    ROTA.t[:], ROTB.t[:], ALU.subtract),
                    rd=[(ROTA, None), (ROTB, None)], wr=[(QROT, None)], tag="b.q6")
            if os.environ.get("KDBG") and s == 0:
                dma(sDBG, dbg_qr[:, :], QROT.t[:, :],
                    rd=[(QROT, None)], tag="dbg.qr")
            alias_from(PHAT, PRE1)
            alias_from(GRB, PRE1)
            alias_from(X1RB, PRE1)
            for h2 in range(2):
                for qt in range(2):
                    t_glob = 4 * s + 2 * h2 + qt
                    nkt = t_glob + 1
                    nkb = (nkt + 3) // 4
                    scbs = banks.take(nkb)
                    scbs = [scbs] if nkb == 1 else scbs
                    for kb in range(nkb):
                        n_in_kb = min(512, nkt * 128 - kb * 512)
                        for c in range(NCH):
                            mm(scbs[kb], scbs[kb][0][:, 0:n_in_kb],
                               QROT.t[:, c * 512 + 128 * (2 * h2 + qt):
                                      c * 512 + 128 * (2 * h2 + qt + 1)],
                               KROT.t[:, c * T + 512 * kb:c * T + 512 * kb + n_in_kb],
                               start=(c == 0), stop=(c == NCH - 1),
                               rd=[(QROT, None),
                                   (KROT, [min(3, (512 * kb) // 512 + jj)
                                           for jj in range(2)] if False else
                                    list(range(NSUP)))],
                               tag="b.sc")
                        if kb == nkb - 1:
                            d0 = nkt * 128 - 128 - kb * 512
                            tmk = op("vector", lambda e, _b=scbs[kb][0], _d0=d0:
                                     e.tensor_tensor(_b[:, _d0:_d0 + 128],
                                                     _b[:, _d0:_d0 + 128],
                                                     MASKL.t[:], ALU.add),
                                     rd=[(MASKL, None)], extra=scbs[kb][1].rd(),
                                     tag="b.mask")
                            scbs[kb][1].did_read(tmk)
                            scbs[kb][1].did_write(tmk)
                        n_in_kb2 = min(512, nkt * 128 - kb * 512)
                        tex = op("scalar", lambda e, _b=scbs[kb][0], _kb=kb,
                                 _n=n_in_kb2: e.activation(
                                     PHAT.t[:, _kb * 512:_kb * 512 + _n],
                                     _b[:, 0:_n], AF.Exp, bias=DC.t[:, 9:10],
                                     accum_out=STAT.t[:, 968 + _kb:969 + _kb]),
                                 rd=[(DC, None)], wr=[(PHAT, None), (STAT, None)],
                                 extra=scbs[kb][1].rd(), tag="b.exp")
                        scbs[kb][1].did_read(tex)
                    for kb in range(1, nkb):
                        op("vector", lambda e, _kb=kb: e.tensor_tensor(
                            STAT.t[:, 968:969], STAT.t[:, 968:969],
                            STAT.t[:, 968 + _kb:969 + _kb], ALU.add),
                            rd=[(STAT, None)], wr=[(STAT, None)], tag="b.rs")
                    op("vector", lambda e: e.reciprocal(STAT.t[:, 972:973],
                       STAT.t[:, 968:969]), rd=[(STAT, None)], wr=[(STAT, None)],
                       tag="b.rc")
                    for kb in range(nkb):
                        n_in_kb2 = min(512, nkt * 128 - kb * 512)
                        op("scalar", lambda e, _kb=kb, _n=n_in_kb2:
                           e.activation(
                               ROT1.t[:, 0:_n],
                               PHAT.t[:, _kb * 512:_kb * 512 + _n],
                               AF.Identity,
                               scale=STAT.t[:, 972:973]),
                           rd=[(PHAT, None), (STAT, None)], wr=[(ROT1, None)],
                           tag="b.nrm")
                        ntile = min(4, nkt - kb * 4)
                        bkt = banks.take()
                        pw = bkt[1].wr()
                        bkt[1].w[0] = []; bkt[1].r[0] = []
                        for i in range(ntile):
                            tmt = op("tensor", lambda e, _b=bkt[0], _i=i:
                                     e.transpose(_b[:, _i * 128:(_i + 1) * 128],
                                                 ROT1.t[:, _i * 128:(_i + 1) * 128],
                                                 ID_F.t[:]),
                                     rd=[(ROT1, None), (ID_F, None)],
                                     extra=(pw if i == 0 else ()), tag="b.tr")
                            bkt[1].did_write(tmt)
                        tev = op("vector", lambda e, _b=bkt[0], _kb=kb, _nt=ntile,
                                 _qt=qt: e.tensor_copy(
                                     bass.AP(PT1.t, (_kb * 4) * 256 + _qt * 128,
                                             [[4096, 128], [256, _nt], [1, 128]]),
                                     _b[:, 0:_nt * 128]),
                                 wr=[(PT1, None)], extra=bkt[1].rd(), tag="b.trev")
                        bkt[1].did_read(tev)
                _ktl = 4 * s + 2 * h2 + 1
                op("vector", lambda e, _k=_ktl: e.memset(
                    PT1.t[:, _k * 256:_k * 256 + 128], 0.0),
                    wr=[(PT1, None)], tag="b.ptz")
                for m in range(NCH):
                    dma(sGL, GRB.t[:], hbm_g[128 * m:128 * (m + 1),
                                            SUP * s:SUP * (s + 1)],
                        rd=[(HG, s)], wr=[(GRB, None)], tag="b.gld")
                    dma(sX1B, X1RB.t[:], hbm_x1[128 * m:128 * (m + 1),
                                               SUP * s:SUP * (s + 1)],
                        rd=[(HX1, s)], wr=[(X1RB, None)], tag="b.x1ld2")
                    ybk = banks.take()
                    nkt_half = 4 * s + 2 * h2 + 2
                    for kt in range(nkt_half):
                        mm(ybk, ybk[0][:, 0:256],
                           VTB.t[:, kt * D + 128 * m:kt * D + 128 * (m + 1)],
                           PT1.t[:, kt * 256:(kt + 1) * 256],
                           start=(kt == 0), stop=(kt == nkt_half - 1),
                           rd=[(VTB, kt), (PT1, None)], tag="b.av")
                    t1g = op("vector", lambda e, _b=ybk[0], _h2=h2:
                             e.tensor_tensor(PR.t[:, 0:256], _b[:, 0:256],
                                             GRB.t[:, 256 * _h2:256 * (_h2 + 1)],
                                             ALU.mult),
                             rd=[(GRB, None)], wr=[(PR, None)],
                             extra=ybk[1].rd(), tag="b.g1")
                    ybk[1].did_read(t1g)
                    op("vector", lambda e, _h2=h2: e.tensor_tensor(
                        OUTS.t[:, 0:256], PR.t[:, 0:256],
                        X1RB.t[:, 256 * _h2:256 * (_h2 + 1)], ALU.add),
                        rd=[(PR, None), (X1RB, None)], wr=[(OUTS, None)],
                        tag="b.g2")
                    dma(sOUT, out_t[128 * m:128 * (m + 1),
                                    SUP * s + 256 * h2:SUP * s + 256 * (h2 + 1)],
                        OUTS.t[:, 0:256], rd=[(OUTS, None)], tag="b.out")

    final_toks = [("d", sOUT, sOUT.n)]
    p.emit("vector", lambda e: e.engine_nop(), waits=final_toks, tag="fin")
    p.build(nc, es)
    es.close()
    return nc


def kernel(**inputs):
    from concourse.bass_utils import run_bass_kernel_spmd
    shared, xs = host_prepare(inputs)
    nc = build_program()
    in_maps = []
    for b in range(NCORES):
        m = dict(shared)
        m["x"] = xs[b]
        in_maps.append(m)
    res = run_bass_kernel_spmd(nc, in_maps, core_ids=list(range(NCORES)))
    out = np.stack([np.asarray(r["out"], dtype=np.float32).T
                    for r in res.results], axis=0)
    return out


# version 12 (final)
# speedup vs baseline: 1.0442x; 1.0083x over previous
import os
import numpy as np
import ml_dtypes
from contextlib import ExitStack

import concourse.bass as bass
import concourse.mybir as mybir

F32 = mybir.dt.float32
BF16 = mybir.dt.bfloat16
F16 = mybir.dt.float16
AF = mybir.ActivationFunctionType
ALU = mybir.AluOpType
AX = mybir.AxisListType

NCORES = 8
T = 2048
D = 1024
NCH = 8
NTOK = 16
NSUP = 4
SUP = 512
TILE = 128

ENGINES = ("sync", "tensor", "vector", "scalar", "gpsimd")

_TWOPI = 2.0 * np.pi
_C1V = float(np.float32(6.28125))
_C2V = float(np.float32(_TWOPI - _C1V))
_C3V = float(np.float32(_TWOPI - _C1V - float(np.float32(_TWOPI - _C1V))))
_PI_LO = float(np.nextafter(np.float32(np.pi), np.float32(0)))
_C1H = _C1V / 2.0
_C2H = _C2V / 2.0
_C3H = _C3V / 2.0
PHASES = os.environ.get("KPHASES", "full")


class Item:
    _seq = 0

    def __init__(self, engine, fn, waits, tag=""):
        self.engine = engine
        self.fn = fn
        self.waits = waits
        self.needs_inc = False
        self.inc_val = None
        self.tag = tag
        self.stream = None
        Item._seq += 1
        self.seq = Item._seq


def compact_toks(toks):
    best = {}
    for t in toks:
        if t[0] == "e":
            k, s = ("e", t[1].engine), t[1].seq
        else:
            k, s = ("d", id(t[1])), t[2]
        cur = best.get(k)
        if cur is None or cur[0] < s:
            best[k] = (s, t)
    return [t for (_, t) in best.values()]

class Stream:
    def __init__(self, engine, name):
        self.engine = engine
        self.name = name
        self.sem = None
        self.n = 0

class Prog:
    def __init__(self):
        self.items = []
        self.streams = []

    def stream(self, engine, name):
        s = Stream(engine, name)
        self.streams.append(s)
        return s

    def emit(self, engine, fn, waits=(), tag=""):
        it = Item(engine, fn, list(waits), tag)
        self.items.append(it)
        return ("e", it)

    def build(self, nc, es):
        kstop = int(os.environ.get("KSTOP", "0"))
        if kstop:
            self.items = self.items[:kstop]
        self_sems = {e: es.enter_context(nc.semaphore(f"ss_{e}")) for e in ENGINES}
        for s in self.streams:
            s.sem = es.enter_context(nc.semaphore(f"dm_{s.name}"))
        for it in self.items:
            for tok in it.waits:
                if tok[0] == "e":
                    tok[1].needs_inc = True
        cnt = {e: 0 for e in ENGINES}
        for it in self.items:
            if it.stream is not None:
                continue
            if it.needs_inc:
                cnt[it.engine] += 1
                it.inc_val = cnt[it.engine]
        for e in ENGINES:
            assert cnt[e] < 60000, f"sem overflow {e}: {cnt[e]}"
        per_eng = {e: [it for it in self.items if it.engine == e] for e in ENGINES}
        watermark = {e: {} for e in ENGINES}

        def tok_sem_val(tok):
            if tok[0] == "e":
                it = tok[1]
                assert it.inc_val is not None, f"untracked token for {it.tag}"
                return self_sems[it.engine], it.inc_val
            _, s, n = tok
            return s.sem, 16 * n

        block = es.enter_context(nc.Block())

        def run_engine(eng_obj, ename):
            wm = watermark[ename]
            for it in per_eng[ename]:
                for tok in it.waits:
                    sem, val = tok_sem_val(tok)
                    key = id(sem)
                    if wm.get(key, -1) >= val:
                        continue
                    eng_obj.wait_ge(sem, val)
                    wm[key] = val
                inst = it.fn(eng_obj)
                if it.tag:
                    try:
                        inst.annotate(it.tag)
                    except Exception:
                        pass
                if it.stream is not None:
                    inst.then_inc(it.stream.sem, 16)
                elif it.inc_val is not None:
                    inst.then_inc(self_sems[it.engine], 1)
        block.sync(lambda e: run_engine(e, "sync"))
        block.tensor(lambda e: run_engine(e, "tensor"))
        block.vector(lambda e: run_engine(e, "vector"))
        block.scalar(lambda e: run_engine(e, "scalar"))
        block.gpsimd(lambda e: run_engine(e, "gpsimd"))


class Reg:
    def __init__(self, t, nslots=1):
        self.t = t
        self.w = [[] for _ in range(nslots)]
        self.r = [[] for _ in range(nslots)]

    def _sl(self, sl):
        if sl is None:
            return range(len(self.w))
        if isinstance(sl, int):
            return (sl,)
        return sl

    def rd(self, sl=None):
        out = []
        for s in self._sl(sl):
            out += self.w[s]
        return out

    def wr(self, sl=None):
        out = []
        for s in self._sl(sl):
            out += self.w[s] + self.r[s]
        return out

    @staticmethod
    def _app(lst, tok):
        key = ("e", tok[1].engine) if tok[0] == "e" else ("d", id(tok[1]))
        for i, t in enumerate(lst):
            k2 = ("e", t[1].engine) if t[0] == "e" else ("d", id(t[1]))
            if k2 == key:
                lst[i] = tok
                return
        lst.append(tok)

    def did_read(self, tok, sl=None):
        for s in self._sl(sl):
            self._app(self.r[s], tok)

    def did_write(self, tok, sl=None):
        for s in self._sl(sl):
            self._app(self.w[s], tok)
            self.r[s] = []


class Banks:
    def __init__(self, nc, es):
        self.b = [es.enter_context(nc.psum_tensor(f"pb{i}", [128, 512], F32))
                  for i in range(8)]
        self.reg = [Reg(t) for t in self.b]
        self.rr = 0

    def __post(self):
        pass

    def take(self, n=1):
        if os.environ.get("KBANKLOG"):
            import traceback
            ln = traceback.extract_stack()[-2].lineno
            print("take", n, "rr", self.rr, "line", ln)
        out = []
        while len(out) < n:
            i = self.rr % 8
            self.rr += 1
            if i in getattr(self, "reserved", set()):
                continue
            out.append((self.b[i], self.reg[i]))
        return out if n > 1 else out[0]

    def reserve(self, n=1):
        if not hasattr(self, "reserved"):
            self.reserved = set()
        out = []
        for i in range(8):
            if len(out) >= n:
                break
            if i not in self.reserved:
                self.reserved.add(i)
                out.append((self.b[i], self.reg[i]))
        return out

    def release_all(self):
        self.reserved = set()



def _f32c(a):
    return np.ascontiguousarray(np.asarray(a, dtype=np.float32))

def host_prepare(inputs):
    d = {}
    Wa = _f32c(inputs["L0_attn_W_attn"])
    d["w_l0q"] = _f32c(Wa[:, 0:1024])
    d["w_l0k"] = _f32c(Wa[:, 1024:2048])
    d["w_l0v"] = _f32c(Wa[:, 2048:3072])
    d["w_l0p"] = _f32c(inputs["L0_attn_W_proj"])
    d["w_l0f"] = _f32c(np.asarray(inputs["L0_ffn_W"]).T)
    d["w_l0vr"] = _f32c(np.asarray(inputs["L0_ffn_V_r"]).T)
    d["w_l0c"] = _f32c(np.asarray(inputs["L0_ffn_C"]).T)
    d["b_l0f"] = _f32c(inputs["L0_ffn_b"])
    perm_eo = np.concatenate([np.arange(0, D, 2), np.arange(1, D, 2)])
    d["w_l1q"] = _f32c(np.asarray(inputs["L1_attn_w_q"]).T[:, perm_eo])
    d["w_l1k"] = _f32c(np.asarray(inputs["L1_attn_w_k"]).T[:, perm_eo])
    d["w_l1v"] = _f32c(np.asarray(inputs["L1_attn_w_v"]).T)
    wqkv = np.asarray(inputs["L1_ffn_wqkv"])
    for l in range(2):
        d[f"w_f{l}q"] = _f32c(wqkv[l][:, 0:512])
        d[f"w_f{l}k"] = _f32c(wqkv[l][:, 512:1024])
        d[f"w_f{l}v"] = _f32c(wqkv[l][:, 1024:1536])
        wol = np.asarray(inputs["L1_ffn_wo"])[l]
        d[f"w_o{l}"] = _f32c(np.concatenate(
            [wol[64 * h:64 * (h + 1), :] for h in range(8)], axis=1))
    HID, HIDP = 2730, 2816
    w1 = np.asarray(inputs["L1_ffn_w1"])
    b1 = np.asarray(inputs["L1_ffn_b1"])
    w2 = np.asarray(inputs["L1_ffn_w2"])
    for l in range(2):
        a_w = np.zeros((D, HIDP), np.float32); a_w[:, :HID] = w1[l][:, :HID]
        g_w = np.zeros((D, HIDP), np.float32); g_w[:, :HID] = w1[l][:, HID:]
        inter = np.empty((D, 2 * HIDP), np.float32)
        for j in range(HIDP // 128):
            inter[:, 256 * j:256 * j + 128] = a_w[:, 128 * j:128 * j + 128]
            inter[:, 256 * j + 128:256 * j + 256] = g_w[:, 128 * j:128 * j + 128]
        w1p = np.empty((22 * 128, 2048), np.float32)
        for j in range(22):
            for cch in range(8):
                w1p[128 * j:128 * (j + 1), 256 * cch:256 * (cch + 1)] = \
                    inter[128 * cch:128 * (cch + 1), 256 * j:256 * (j + 1)]
        d[f"w_1_{l}"] = _f32c(w1p)
        a_b = np.zeros((HIDP,), np.float32); a_b[:HID] = b1[l][:HID]
        g_b = np.zeros((HIDP,), np.float32); g_b[:HID] = b1[l][HID:]
        d[f"b_1a_{l}"] = a_b
        d[f"b_1g_{l}"] = g_b
        w2p = np.zeros((HIDP, D), np.float32); w2p[:HID] = w2[l]
        d[f"w_2_{l}"] = _f32c(w2p)
        d[f"b_2_{l}"] = _f32c(np.asarray(inputs["L1_ffn_b2"])[l])
        d[f"ln_g_{l}"] = _f32c(np.asarray(inputs["L1_ffn_ln_g"])[l])
        d[f"ln_b_{l}"] = _f32c(np.asarray(inputs["L1_ffn_ln_b"])[l])
        d[f"qs_{l}"] = _f32c(np.asarray(inputs["L1_ffn_q_scale"])[l])
        d[f"ks_{l}"] = _f32c(np.asarray(inputs["L1_ffn_k_scale"])[l])
    d["c_id_f"] = np.eye(128, dtype=np.float32)
    ff = np.arange(512)[None, :]; pp = np.arange(128)[:, None]
    d["c_trij"] = np.concatenate(
        [(ff >= 128 * j + pp).astype(ml_dtypes.bfloat16) for j in range(4)], axis=1)
    f2 = np.arange(128)[None, :]
    d["c_maskl"] = np.where(f2 <= pp, 0.0, -1e30).astype(np.float32)
    d["c_masku"] = np.where(f2 >= pp, 0.0, -1e30).astype(np.float32)
    d["c_ones_bf"] = np.ones((128, 512), ml_dtypes.bfloat16)
    d["c_ones_f"] = np.ones((128, 128), np.float32)
    twopi = 2.0 * np.pi
    c1 = np.float32(6.28125)
    c2 = np.float32(twopi - float(c1))
    c3 = np.float32(twopi - float(c1) - float(c2))
    pi_lo = np.nextafter(np.float32(np.pi), np.float32(0))
    cc = np.zeros((128, 16), np.float32)
    cc[:, 0] = -c1; cc[:, 1] = -c2; cc[:, 2] = -c3
    cc[:, 3] = pi_lo; cc[:, 4] = -pi_lo
    d["c_cols"] = cc
    d["c_pos"] = np.arange(512, dtype=np.float32)[None, :]
    f32 = np.float32
    _hi = np.arange(0, 64, 2, dtype=f32)
    _invf = (f32(1.0) / np.power(f32(10000.0), _hi / f32(64.0), dtype=f32))
    _tokf = np.arange(2048, dtype=f32)
    _fr = (_tokf[:, None] * _invf[None, :]).astype(f32)
    _cosv = np.cos(_fr)
    _sinv = np.sin(_fr)
    _base = ((_hi + f32(25.6)) / f32(89.6)).astype(f32)
    _pw = ((_tokf - f32(1024.0)) / f32(256.0)).astype(f32)
    _scl = np.power(_base[None, :], _pw[:, None], dtype=f32)
    _lay = lambda a: np.ascontiguousarray(
        a.reshape(16, 128, 32).transpose(1, 0, 2).reshape(128, 512))
    d["c_xtcos"] = _lay(_cosv)
    d["c_xtsin"] = _lay(_sinv)
    d["c_xtscl"] = _lay(_scl)
    gg2 = (np.arange(128)[:, None] + 128 * np.arange(4)[None, :]).astype(np.float64)
    d["c_expthc"] = (-(gg2 - 1.0) / 512.0).astype(np.float32)
    gg = np.arange(512).astype(np.float64)
    d["c_ethrow"] = ((-(gg - 1.0)) / 512.0).astype(np.float32)[None, :]
    xr = np.zeros((1, 128), np.float32)
    ii = np.arange(32).astype(np.float64)
    xr[0, 0:32] = (-ii / 32.0).astype(np.float32)
    xr[0, 32:64] = (10.0 * ii + 128.0).astype(np.float32)
    d["c_xrows"] = xr
    eh = np.zeros((1, 256), np.float32)
    eh[0, 0:64] = 1.0
    eh[0, 128 + 64:128 + 128] = 1.0
    d["c_eh"] = eh
    d["c_p256"] = (np.arange(128) / 256.0).astype(np.float32)[None, :]
    x = np.asarray(inputs["L0_attn_x"], dtype=np.float32)
    xs = [np.ascontiguousarray(x[b].T) for b in range(NCORES)]
    return d, xs



ARENA_BASE = 16512

class Mem:
    def __init__(self, nc):
        self.nc = nc
        self.limit = 229344
        self.ranges = []

    def at(self, name, off, shape, dtype, phases=("all",), nslots=1):
        nbytes = int(np.prod(shape[1:])) * mybir.dt.size(dtype)
        end = off + nbytes
        assert end <= self.limit, f"{name} overflows SBUF: {off}+{nbytes}={end}"
        for (o2, e2, n2, ph2) in self.ranges:
            if off < e2 and o2 < end:
                if "all" in phases or "all" in ph2 or set(phases) & set(ph2):
                    raise AssertionError(f"overlap {name}{phases} with {n2}{ph2}")
        self.ranges.append((off, end, name, phases))
        t = self.nc.alloc_sbuf_tensor_at(name, list(shape), dtype, offset=off)
        return Reg(t, nslots)


def build_program():
    nc = bass.Bass("TRN2", target_bir_lowering=False)
    p = Prog()
    es = ExitStack()

    def din(name, shape, dtype=F32):
        return nc.dram_tensor(name, list(shape), dtype, kind="ExternalInput").ap()

    x_in = din("x", [D, T])
    out_t = nc.dram_tensor("out", [D, T], F32, kind="ExternalOutput").ap()
    hbm_x1 = nc.dram_tensor("hx1", [D, T], F32, kind="Internal").ap()
    hbm_g = nc.dram_tensor("hg", [D, T], F32, kind="Internal").ap()
    if os.environ.get("KDBG"):
        dbg_mid = [nc.dram_tensor(f"dbg_mid{l}", [D, T], F32,
                                  kind="Internal").ap() for l in range(2)]
        dbg_out = [nc.dram_tensor(f"dbg_out{l}", [D, T], F32,
                                  kind="Internal").ap() for l in range(2)]
        dbg_hn = nc.dram_tensor("dbg_hn", [128, 8 * 512], BF16,
                                kind="Internal").ap()
        dbg_h2 = nc.dram_tensor("dbg_h2", [128, 22 * 512], BF16,
                                kind="Internal").ap()
        dbg_xn1 = nc.dram_tensor("dbg_xn1", [128, 8 * T], BF16,
                                 kind="Internal").ap()
        dbg_vtb = nc.dram_tensor("dbg_vtb", [128, 16 * D], BF16,
                                 kind="Internal").ap()
        dbg_kr = nc.dram_tensor("dbg_kr", [128, 8 * T], BF16,
                                kind="Internal").ap()
        dbg_qr = nc.dram_tensor("dbg_qr", [128, 4096], BF16,
                                kind="Internal").ap()

    w_l0q = din("w_l0q", [D, D]); w_l0k = din("w_l0k", [D, D])
    w_l0v = din("w_l0v", [D, D]); w_l0p = din("w_l0p", [D, D])
    w_l0f = din("w_l0f", [D, D]); w_l0vr = din("w_l0vr", [32, D])
    w_l0c = din("w_l0c", [32, D]); b_l0f = din("b_l0f", [D])
    w_l1q = din("w_l1q", [D, D]); w_l1k = din("w_l1k", [D, D])
    w_l1v = din("w_l1v", [D, D])
    w_fq = [din(f"w_f{l}q", [D, 512]) for l in range(2)]
    w_fk = [din(f"w_f{l}k", [D, 512]) for l in range(2)]
    w_fv = [din(f"w_f{l}v", [D, 512]) for l in range(2)]
    w_o = [din(f"w_o{l}", [64, 8 * D]) for l in range(2)]
    w_1 = [din(f"w_1_{l}", [22 * 128, 2048]) for l in range(2)]
    w_2 = [din(f"w_2_{l}", [2816, D]) for l in range(2)]
    b_1a = [din(f"b_1a_{l}", [2816]) for l in range(2)]
    b_1g = [din(f"b_1g_{l}", [2816]) for l in range(2)]
    b_2 = [din(f"b_2_{l}", [D]) for l in range(2)]
    ln_g = [din(f"ln_g_{l}", [D]) for l in range(2)]
    ln_b = [din(f"ln_b_{l}", [D]) for l in range(2)]
    qs_in = [din(f"qs_{l}", [64]) for l in range(2)]
    ks_in = [din(f"ks_{l}", [64]) for l in range(2)]
    c_id_f = din("c_id_f", [128, 128], F32)
    c_trij = din("c_trij", [128, 2048], BF16)
    c_maskl = din("c_maskl", [128, 128]); c_masku = din("c_masku", [128, 128])
    c_ones_bf = din("c_ones_bf", [128, 512], BF16)
    c_ones_f = din("c_ones_f", [128, 128])
    c_cols = din("c_cols", [128, 16])
    c_xtcos = din("c_xtcos", [128, 512])
    c_xtsin = din("c_xtsin", [128, 512])
    c_xtscl = din("c_xtscl", [128, 512])
    c_pos = din("c_pos", [1, 512])
    c_ethrow = din("c_ethrow", [1, 512])
    c_expthc = din("c_expthc", [128, 4])
    c_xrows = din("c_xrows", [1, 128]); c_eh = din("c_eh", [1, 256])
    c_p256 = din("c_p256", [1, 128])

    mem = Mem(nc)
    o = ARENA_BASE
    def nxt(n):
        nonlocal o
        r = o; o = (o + n + 31) & ~31
        return r
    ID_F = mem.at("id_f", nxt(512), [128, 128], F32)
    MASKL = mem.at("maskl", nxt(512), [128, 128], F32)
    MASKU = mem.at("masku", nxt(512), [128, 128], F32)
    ONES_BF = mem.at("ones_bf", nxt(1024), [128, 512], BF16)
    ONES_F = mem.at("ones_f", nxt(512), [128, 128], F32)
    CC = mem.at("ccols", nxt(64), [128, 16], F32)
    DC = mem.at("dcon", nxt(64), [128, 16], F32)
    VEC = mem.at("vecs", nxt(256), [128, 64], F32)
    B1V = mem.at("b1v", nxt(384), [128, 96], F32)
    POSR = mem.at("posr", nxt(2048), [1, 512], F32)
    ETHC = mem.at("ethc", nxt(64), [128, 4], F32)
    ETHR = mem.at("ethr", nxt(2048), [1, 512], F32)
    XROW = mem.at("xrow", nxt(512), [1, 128], F32)
    EHR = mem.at("ehr", nxt(1024), [1, 256], F32)
    P256 = mem.at("p256", nxt(512), [1, 128], F32)
    RROW = mem.at("rrow", nxt(2048), [1, 512], F32)

    oXN = nxt(32768); oKR = nxt(32768); oVR = nxt(32768)
    oWR = nxt(40960)
    oWS = nxt(16384)
    oFA = nxt(45072)
    assert o <= mem.limit, f"arena overflow: {o} > {mem.limit}"

    XN = mem.at("l0_xn", oXN, [128, 8 * T], BF16, ("l0",), nslots=NSUP)
    KFM = mem.at("l0_k", oKR, [128, 8 * T], BF16, ("l0",), nslots=NTOK)
    VTM = mem.at("l0_v", oVR, [128, NTOK * D], BF16, ("l0",), nslots=NTOK)
    WE0 = mem.at("l0_we", oWR, [128, 8 * D], BF16, ("l0",))
    QY = mem.at("l0_qy", oWR + 16384, [128, 8 * 512], BF16, ("l0",), nslots=8)
    WKV0 = mem.at("l0_wkv", oWR + 24576, [128, 8 * D], BF16, ("l0kv",))
    WQR = mem.at("l0_wqres", oWR + 24576, [128, 8 * D], BF16, ("l0qq",))
    WSA = mem.at("ws_a", oWS, [128, 2048], BF16, ("all",), nslots=2)
    SPFX = mem.at("l0_spfx", oWS + 4096, [1, 1024], BF16, ("l0",))
    TRIJ = mem.at("l0_trij", oWS + 6144, [128, 2048], BF16, ("l0",))
    PB16 = mem.at("l0_pb", oWS + 10240, [128, 1024], BF16, ("l0",), nslots=2)
    PB16B = mem.at("l0_pbb", oWS + 12288, [128, 1024], BF16, ("l0",), nslots=2)
    XSUP = mem.at("l0_xsup", oFA, [128, 8 * 512], F32, ("l0",))
    SSUP = mem.at("l0_ssup", oFA, [128, 8 * 512], F32, ("l0s",))
    GSUP = mem.at("l0_gsup", oFA + 16384, [128, 8 * 512], F16, ("l0",))
    XG = mem.at("l0_xg", oFA + 24576, [128, 4096], F32, ("l0",))
    STAT = mem.at("stat", oFA + 40960, [128, 1024], F32, ("all",))
    PR = mem.at("l0_pr", oWS + 14336, [128, 512], F32, ("all",))
    WTS = mem.at("l0_wts", oFA, [128, 2048], F32, ("l0pre",), nslots=2)
    VRC = mem.at("l0_vrc", oFA + 8192, [32, 2048], BF16, ("l0pre",))
    SQT = mem.at("l0_sqt", oFA + 24576, [128, 1024], F32, ("l0ln",), nslots=2)

    banks = Banks(nc, es)
    HX1 = Reg(None, nslots=NSUP)
    HG = Reg(None, nslots=NSUP)
    sCONST = p.stream("sync", "cst")
    sXS = p.stream("sync", "xs")
    sXG = p.stream("sync", "xg")
    sWT = [p.stream("sync", f"wt{i}") for i in range(2)]
    sOUT = p.stream("sync", "out")
    sDBG = p.stream("sync", "dbg")
    sWV = p.stream("gpsimd", "wv")
    sVRC = p.stream("gpsimd", "vrc")
    sWSA = [p.stream("gpsimd", f"wsa{i}") for i in range(2)]

    def seal(reg, sl, stream):
        tok = ("d", stream, stream.n)
        for s2 in reg._sl(sl):
            reg.w[s2] = [tok]

    def op(engine, fn, rd=(), wr=(), extra=(), tag=""):
        waits = list(extra)
        for (rg, sl) in rd:
            waits += rg.rd(sl)
        for (rg, sl) in wr:
            waits += rg.wr(sl)
        tok = p.emit(engine, fn, waits, tag)
        for (rg, sl) in rd:
            rg.did_read(tok, sl)
        for (rg, sl) in wr:
            rg.did_write(tok, sl)
        return tok

    def dma(stream, out_ap, in_ap, rd=(), wr=(), extra=(), tag="", slow=False):
        waits = list(extra)
        for (rg, sl) in rd:
            waits += rg.rd(sl)
        for (rg, sl) in wr:
            waits += rg.wr(sl)
        if stream.n > 0:
            waits.append(("d", stream, stream.n))
        def fn(eng, _o=out_ap, _i=in_ap, _sl=slow):
            return eng.dma_start(out=_o, in_=_i, allow_slow_non_contiguous=_sl)
        it = Item(stream.engine, fn, waits, tag)
        it.stream = stream
        p.items.append(it)
        stream.n += 1
        tok = ("d", stream, stream.n)
        for (rg, sl) in rd:
            rg.did_read(tok, sl)
        for (rg, sl) in wr:
            rg.did_write(tok, sl)
        return tok

    def mm(bank, out_ap, lhsT_ap, rhs_ap, start, stop, rd=(), tile_position=None,
           ldw=False, tag="", kwargs_record_write=False, skip_gc=False, extra=()):
        if ldw:
            op("tensor", lambda e, _l=lhsT_ap, _tp=tile_position:
               e.ldweights(_l, tile_position=_tp), rd=list(rd), tag=tag + ".ldw")
        waits = list(extra)
        if start:
            waits += bank[1].wr()
        span = 1
        for st_, ct_ in out_ap.ap[1:]:
            span *= ct_
        if out_ap.ap[0][1] < 128 or out_ap.offset != 0 or span != 512:
            skip_gc = True
        def fn(e, _o=out_ap, _l=lhsT_ap, _r=rhs_ap, _s=start, _p=stop,
               _tp=tile_position, _sg=skip_gc):
            return e.matmul(_o, _l, _r, start=_s, stop=_p, tile_position=_tp,
                            skip_group_check=_sg)
        for (rg, sl) in rd:
            waits += rg.rd(sl)
        tok = p.emit("tensor", fn, waits, tag)
        for (rg, sl) in rd:
            rg.did_read(tok, sl)
        if stop or kwargs_record_write:
            bank[1].did_write(tok)
        return tok

    def alias_from(dst, *srcs):
        toks = []
        for s in srcs:
            for sl in range(len(s.w)):
                toks += s.w[sl] + s.r[sl]
        for sl in range(len(dst.w)):
            dst.w[sl] = compact_toks(dst.w[sl] + dst.r[sl] + toks)
            dst.r[sl] = []

    dma(sCONST, ID_F.t[:], c_id_f, wr=[(ID_F, None)])
    dma(sCONST, TRIJ.t[:], c_trij, wr=[(TRIJ, None)])
    dma(sCONST, MASKL.t[:], c_maskl, wr=[(MASKL, None)])
    dma(sCONST, MASKU.t[:], c_masku, wr=[(MASKU, None)])
    dma(sCONST, ONES_BF.t[:], c_ones_bf, wr=[(ONES_BF, None)])
    dma(sCONST, ONES_F.t[:], c_ones_f, wr=[(ONES_F, None)])
    dma(sCONST, CC.t[:], c_cols, wr=[(CC, None)])
    op("vector", lambda e: e.memset(CC.t[:, 5:6], 0.5), wr=[(CC, None)])
    op("vector", lambda e: e.memset(CC.t[:, 6:7], 1.0), wr=[(CC, None)])
    dma(sCONST, POSR.t[:], c_pos, wr=[(POSR, None)])
    dma(sCONST, ETHR.t[:], c_ethrow, wr=[(ETHR, None)])
    dma(sCONST, ETHC.t[:], c_expthc, wr=[(ETHC, None)])
    dma(sCONST, XROW.t[:], c_xrows, wr=[(XROW, None)])
    dma(sCONST, EHR.t[:], c_eh, wr=[(EHR, None)])
    dma(sCONST, P256.t[:], c_p256, wr=[(P256, None)])
    dma(sCONST, VEC.t[:, 0:8], bass.AP(b_l0f.tensor, 0, [[1, 128], [128, 8]]),
        wr=[(VEC, None)], slow=True)
    for l in range(2):
        dma(sCONST, VEC.t[:, 8 + 16 * l:16 + 16 * l],
            bass.AP(ln_g[l].tensor, 0, [[1, 128], [128, 8]]), wr=[(VEC, None)], slow=True)
        dma(sCONST, VEC.t[:, 16 + 16 * l:24 + 16 * l],
            bass.AP(ln_b[l].tensor, 0, [[1, 128], [128, 8]]), wr=[(VEC, None)], slow=True)
        dma(sCONST, VEC.t[:, 40 + 8 * l:48 + 8 * l],
            bass.AP(b_2[l].tensor, 0, [[1, 128], [128, 8]]), wr=[(VEC, None)], slow=True)
        dma(sCONST, B1V.t[:, 44 * l:44 * l + 22],
            bass.AP(b_1a[l].tensor, 0, [[1, 128], [128, 22]]), wr=[(B1V, None)], slow=True)
        dma(sCONST, B1V.t[:, 44 * l + 22:44 * l + 44],
            bass.AP(b_1g[l].tensor, 0, [[1, 128], [128, 22]]), wr=[(B1V, None)], slow=True)
        dma(sCONST, RROW.t[0:1, 128 * l:128 * l + 64],
            bass.AP(qs_in[l].tensor, 0, [[64, 1], [1, 64]]), wr=[(RROW, None)])
        dma(sCONST, RROW.t[0:1, 128 * l + 64:128 * l + 128],
            bass.AP(ks_in[l].tensor, 0, [[64, 1], [1, 64]]), wr=[(RROW, None)])

    for _rg in (ID_F, TRIJ, MASKL, MASKU, ONES_BF, ONES_F, CC, POSR,
                ETHC, ETHR, XROW, EHR, P256, VEC, B1V):
        seal(_rg, None, sCONST)

    t_dc = DC.t
    op("vector", lambda e: e.memset(t_dc[:, 15:16], -100.0), wr=[(DC, None)])
    op("vector", lambda e: e.reciprocal(t_dc[:, 0:1], t_dc[:, 15:16]),
       rd=[(DC, None)], wr=[(DC, None)])
    op("vector", lambda e: e.memset(t_dc[:, 14:15], 1000.0), wr=[(DC, None)])
    op("vector", lambda e: e.reciprocal(t_dc[:, 1:2], t_dc[:, 14:15]),
       rd=[(DC, None)], wr=[(DC, None)])
    op("vector", lambda e: e.tensor_tensor(t_dc[:, 13:14], CC.t[:, 0:1],
       CC.t[:, 1:2], ALU.add), rd=[(CC, None)], wr=[(DC, None)])
    op("vector", lambda e: e.tensor_scalar_mul(t_dc[:, 12:13], t_dc[:, 13:14], -1.0),
       rd=[(DC, None)], wr=[(DC, None)])
    op("vector", lambda e: e.reciprocal(t_dc[:, 2:3], t_dc[:, 12:13]),
       rd=[(DC, None)], wr=[(DC, None)])
    op("vector", lambda e: e.tensor_scalar_mul(t_dc[:, 4:5], t_dc[:, 2:3], 2.0),
       rd=[(DC, None)], wr=[(DC, None)])
    op("vector", lambda e: e.memset(t_dc[:, 11:12], 10000.0), wr=[(DC, None)])
    op("vector", lambda e: e.memset(t_dc[:, 10:11], 1e-5), wr=[(DC, None)])
    op("scalar", lambda e: e.activation(t_dc[:, 3:4], t_dc[:, 11:12], AF.Ln),
       rd=[(DC, None)], wr=[(DC, None)])

    dma(sVRC, VRC.t[0:32, 0:1024], w_l0vr, wr=[(VRC, None)], tag="vrt")
    dma(sVRC, VRC.t[0:32, 1024:2048], w_l0c, wr=[(VRC, None)], tag="ct")
    seal(VRC, None, sVRC)
    for c in range(NCH):
        sl = c % 2
        dma(sWT[sl], WTS.t[:, sl * 1024:(sl + 1) * 1024],
            w_l0f[128 * c:128 * (c + 1), :], wr=[(WTS, sl)], tag="wt")
        for h in range(2):
            bk = banks.take()
            mm(bk, bk[0][:, :], VRC.t[0:32, c * 128:(c + 1) * 128],
               VRC.t[0:32, 1024 + 512 * h:1024 + 512 * (h + 1)],
               start=True, stop=True, rd=[(VRC, None)], tag="weff.mm")
            tok = op("vector", lambda e, _c=c, _h=h, _sl=sl, _b=bk[0]:
                     e.tensor_tensor(
                         WE0.t[:, _c * D + 512 * _h:_c * D + 512 * (_h + 1)],
                         WTS.t[:, _sl * 1024 + 512 * _h:_sl * 1024 + 512 * (_h + 1)],
                         _b[:, :], ALU.add),
                     rd=[(WTS, sl)], wr=[(WE0, None)], extra=bk[1].rd(),
                     tag="weff.add")
            bk[1].did_read(tok)

    def ln_stats_apply(x_rd, x_chunk_ap, dst_wr, dst_ap, sq, sq_ap, tag="ln"):
        bkA = banks.take(); bkB = banks.take()
        for c in range(NCH):
            sl = c % 2
            op("scalar", lambda e, _s=sq_ap(sl), _x=x_chunk_ap(c): e.activation(
                _s, _x, AF.Square), rd=[x_rd(c)], wr=[(sq, sl)], tag=tag + ".sq")
            mm(bkA, bkA[0][0:1, :], ONES_F.t[:, 0:1], x_chunk_ap(c),
               start=(c == 0), stop=(c == NCH - 1), rd=[x_rd(c)], tag=tag + ".sx")
            mm(bkB, bkB[0][0:1, :], ONES_F.t[:, 0:1], sq_ap(sl),
               start=(c == 0), stop=(c == NCH - 1), rd=[(sq, sl)], tag=tag + ".sx2")
        A = STAT.t[0:1, 0:512]
        B = STAT.t[0:1, 512:1024]
        tmp = sq.t[0:1, 0:512]
        t1 = op("vector", lambda e, _b=bkA[0]: e.tensor_scalar_mul(
                A, _b[0:1, :], -1.0 / 1024.0), wr=[(STAT, None)],
                extra=bkA[1].rd(), tag=tag + ".m")
        bkA[1].did_read(t1)
        t2 = op("vector", lambda e, _b=bkB[0]: e.tensor_scalar_mul(
                B, _b[0:1, :], 1.0 / 1024.0), wr=[(STAT, None)],
                extra=bkB[1].rd(), tag=tag + ".e2")
        bkB[1].did_read(t2)
        op("vector", lambda e: e.tensor_tensor(tmp, A, A, ALU.mult),
           rd=[(STAT, None)], wr=[(sq, 0)], tag=tag + ".msq")
        op("vector", lambda e: e.tensor_tensor(B, B, tmp, ALU.subtract),
           rd=[(STAT, None), (sq, 0)], wr=[(STAT, None)], tag=tag + ".var")
        op("scalar", lambda e: e.activation(B, B, AF.Sqrt,
           bias=DC.t[0:1, 10:11]),
           rd=[(STAT, None), (DC, None)], wr=[(STAT, None)], tag=tag + ".sd")
        op("vector", lambda e: e.reciprocal(B, B),
           rd=[(STAT, None)], wr=[(STAT, None)], tag=tag + ".rstd")
        bkR = banks.take(); bkM = banks.take()
        mm(bkR, bkR[0][:, :], ONES_F.t[0:1, 0:128], B, start=True, stop=True,
           rd=[(STAT, None)], tag=tag + ".bcr")
        mm(bkM, bkM[0][:, :], ONES_F.t[0:1, 0:128], A, start=True, stop=True,
           rd=[(STAT, None)], tag=tag + ".bcm")
        for c in range(NCH):
            sl = c % 2
            tk1 = op("vector", lambda e, _t=sq_ap(sl), _x=x_chunk_ap(c), _b=bkM[0]:
                     e.tensor_tensor(_t, _x, _b[:, :], ALU.add),
                     rd=[x_rd(c)], wr=[(sq, sl)], extra=bkM[1].rd(), tag=tag + ".a1")
            bkM[1].did_read(tk1)
            tk2 = op("vector", lambda e, _t=sq_ap(sl), _d=dst_ap(c), _b=bkR[0]:
                     e.tensor_tensor(_d, _t, _b[:, :], ALU.mult),
                     rd=[(sq, sl)], wr=[dst_wr(c)], extra=bkR[1].rd(),
                     tag=tag + ".a2")
            bkR[1].did_read(tk2)

    alias_from(XSUP, WTS, VRC)
    for s in range(NSUP):
        for c in range(NCH):
            dma(sXS, XSUP.t[:, c * 512:(c + 1) * 512],
                x_in[128 * c:128 * (c + 1), SUP * s:SUP * (s + 1)],
                wr=[(XSUP, 0)], tag="x.ld")
        seal(XSUP, None, sXS)
        ln_stats_apply(
            x_rd=lambda c, _s=s: (XSUP, 0),
            x_chunk_ap=lambda c, _s=s: XSUP.t[:, c * 512:(c + 1) * 512],
            dst_wr=lambda c, _s=s: (XN, _s),
            dst_ap=lambda c, _s=s: XN.t[:, c * T + SUP * _s:c * T + SUP * (_s + 1)],
            sq=SQT, sq_ap=lambda sl: SQT.t[:, sl * 512:(sl + 1) * 512],
            tag=f"ln0.{s}")

    for c in range(NCH):
        dma(sWV, WKV0.t[:, c * D:(c + 1) * D], w_l0k[128 * c:128 * (c + 1), :],
            wr=[(WKV0, None)], tag="wk0")
    seal(WKV0, None, sWV)
    for s in range(NSUP):
        for m in range(NCH):
            bk = banks.take()
            for c in range(NCH):
                mm(bk, bk[0][:, :],
                   WKV0.t[:, c * D + 128 * m:c * D + 128 * (m + 1)],
                   XN.t[:, c * T + SUP * s:c * T + SUP * (s + 1)],
                   start=(c == 0), stop=(c == NCH - 1),
                   rd=[(WKV0, None), (XN, s)], tag="k.mm")
            kt = op("vector", lambda e, _b=bk[0], _m=m, _s=s: e.tensor_copy(
                KFM.t[:, _m * T + SUP * _s:_m * T + SUP * (_s + 1)], _b[:, :]),
                wr=[(KFM, [4 * s + j for j in range(4)])], extra=bk[1].rd(),
                tag="k.ev")
            bk[1].did_read(kt)
    for c in range(NCH):
        dma(sWV, WKV0.t[:, c * D:(c + 1) * D], w_l0v[128 * c:128 * (c + 1), :],
            wr=[(WKV0, None)], tag="wv0")
    seal(WKV0, None, sWV)
    for tt in range(NTOK):
        s = tt // 4
        for h in range(2):
            bk = banks.take()
            for c in range(NCH):
                mm(bk, bk[0][:, :],
                   XN.t[:, c * T + TILE * tt:c * T + TILE * (tt + 1)],
                   WKV0.t[:, c * D + 512 * h:c * D + 512 * (h + 1)],
                   start=(c == 0), stop=(c == NCH - 1),
                   rd=[(WKV0, None), (XN, s)], tag="v.mm")
            vt = op("vector", lambda e, _b=bk[0], _t=tt, _h=h: e.tensor_scalar(
                VTM.t[:, _t * D + 512 * _h:_t * D + 512 * (_h + 1)], _b[:, :],
                DC.t[:, 1:2], None, ALU.mult),
                rd=[(DC, None)], wr=[(VTM, tt)], extra=bk[1].rd(), tag="v.ev")
            bk[1].did_read(vt)

    alias_from(WQR, WKV0)
    alias_from(XG, SQT)
    alias_from(SSUP, XSUP)
    for c in range(NCH):
        dma(sWV, WQR.t[:, c * D:(c + 1) * D], w_l0q[128 * c:128 * (c + 1), :],
            wr=[(WQR, None)], tag="wq.res")
    seal(WQR, None, sWV)
    cs_banks = banks.reserve(2)
    cs_started = [False, False]
    for s in range(NSUP):
        for mh in range(2):
            bks = banks.take(4)
            for c in range(NCH):
                for mi in range(4):
                    m = mh * 4 + mi
                    mm(bks[mi], bks[mi][0][:, :],
                       WQR.t[:, c * D + 128 * m:c * D + 128 * (m + 1)],
                       XN.t[:, c * T + SUP * s:c * T + SUP * (s + 1)],
                       start=(c == 0), stop=(c == NCH - 1),
                       rd=[(WQR, None), (XN, s)], tag="q.mm")
            for mi in range(4):
                m = mh * 4 + mi
                tk = op("vector", lambda e, _b=bks[mi][0], _m=m: e.tensor_scalar_mul(
                    QY.t[:, _m * 512:(_m + 1) * 512], _b[:, :], 0.125),
                    wr=[(QY, m)], extra=bks[mi][1].rd(), tag="q.ev")
                bks[mi][1].did_read(tk)
        for m in range(NCH):
            bk = banks.take()
            for c in range(NCH):
                mm(bk, bk[0][:, :], WE0.t[:, c * D + 128 * m:c * D + 128 * (m + 1)],
                   XN.t[:, c * T + SUP * s:c * T + SUP * (s + 1)],
                   start=(c == 0), stop=(c == NCH - 1),
                   rd=[(WE0, None), (XN, s)], tag="g.mm")
            gt = op("scalar", lambda e, _b=bk[0], _m=m, _s=s: e.activation(
                GSUP.t[:, _m * 512:(_m + 1) * 512], _b[:, :], AF.Silu,
                bias=VEC.t[:, _m:_m + 1]),
                rd=[(VEC, None)], wr=[(GSUP, 0)], extra=bk[1].rd(), tag="g.silu")
            bk[1].did_read(gt)
        if s > 0:
            for kt in range(4 * (s - 1), 4 * s):
                for hp in range(NCH):
                    bi = hp // 4
                    ex = list(cs_banks[bi][1].r[0]) if kt == 4 * (s - 1) else []
                    mm(cs_banks[bi],
                       cs_banks[bi][0][0:1, 128 * (hp % 4):128 * (hp % 4 + 1)],
                       ONES_BF.t[:, 0:1],
                       VTM.t[:, kt * D + 128 * hp:kt * D + 128 * (hp + 1)],
                       start=(not cs_started[bi]), stop=False,
                       rd=[(VTM, kt), (ONES_BF, None)], tag="cs.mm",
                       kwargs_record_write=True, skip_gc=True, extra=ex)
                    cs_started[bi] = True
            for bi in range(2):
                tk = op("vector", lambda e, _b=cs_banks[bi][0], _bi=bi:
                        e.tensor_copy(SPFX.t[0:1, 512 * _bi:512 * (_bi + 1)],
                                      _b[0:1, :]),
                        wr=[(SPFX, None)], extra=cs_banks[bi][1].rd(), tag="cs.ev")
                cs_banks[bi][1].did_read(tk)
        for hp in range(NCH):
            ybks = banks.take(2)
            ystarted = set()
            def ymm(h, lhsT_ap, rhs_ap, rd, tag, stop=False, ldw=True):
                ybk = ybks[h]
                start = h not in ystarted
                oy = ybk[0][64 * h:64 * (h + 1), :]
                mm(ybk, oy, lhsT_ap, rhs_ap, start=start, stop=stop, rd=rd,
                   tag=tag, kwargs_record_write=stop,
                   tile_position=(0, 64 * h), ldw=ldw)
                ystarted.add(h)
            if s > 0:
                for h in range(2):
                    ymm(h, SPFX.t[0:1, 128 * hp + 64 * h:128 * hp + 64 * (h + 1)],
                        ONES_BF.t[0:1, 0:512],
                        [(SPFX, None), (ONES_BF, None)], "y.ones")
            sc_pair = [banks.take(2), banks.take(2)]
            nkt = 4 * s + 4

            def scores(kt, _s=s, _hp=hp, _sp=sc_pair):
                pr = _sp[kt % 2]
                for h in range(2):
                    mm(pr[h], pr[h][0][:, :],
                       KFM.t[64 * h:64 * (h + 1),
                             _hp * T + 128 * kt:_hp * T + 128 * (kt + 1)],
                       QY.t[64 * h:64 * (h + 1), _hp * 512:(_hp + 1) * 512],
                       start=True, stop=True, rd=[(KFM, kt), (QY, _hp)],
                       tile_position=(64 * h, 0), ldw=True, tag="sc.mm")

            def poly_av(kt, _s=s, _hp=hp, _sp=sc_pair, _ybk=None, _fm=None):
                pr = _sp[kt % 2]
                j = kt - 4 * _s
                last = (kt == 4 * _s + 3)
                for h in range(2):
                    sb, sreg = pr[h]
                    r2_ap = PB16.t[:, h * 512:(h + 1) * 512]
                    u_ap = PB16B.t[:, h * 512:(h + 1) * 512]
                    t1 = op("scalar", lambda e, _sb=sb: e.activation(
                        PR.t[:], _sb[:, :], AF.Relu), wr=[(PR, None)],
                        extra=sreg.rd(), tag="poly.r")
                    sreg.did_read(t1)
                    op("scalar", lambda e, _d=r2_ap: e.activation(
                        _d, PR.t[:], AF.Square), rd=[(PR, None)],
                        wr=[(PB16, h)], tag="poly.r2")
                    t2 = op("scalar", lambda e, _sb=sb, _d=u_ap: e.activation(
                        _d, _sb[:, :], AF.Relu, scale=DC.t[:, 0:1]),
                        rd=[(DC, None)], wr=[(PB16B, h)], extra=sreg.rd(),
                        tag="poly.u")
                    sreg.did_read(t2)
                    if j >= 0:
                        op("vector", lambda e, _d=r2_ap, _j=j: e.tensor_tensor(
                            _d, _d, TRIJ.t[:, _j * 512:(_j + 1) * 512], ALU.mult),
                            rd=[(PB16, h), (TRIJ, None)], wr=[(PB16, h)],
                            tag="poly.mr2")
                        op("vector", lambda e, _d=u_ap, _j=j: e.tensor_tensor(
                            _d, _d, TRIJ.t[:, _j * 512:(_j + 1) * 512], ALU.mult),
                            rd=[(PB16B, h), (TRIJ, None)], wr=[(PB16B, h)],
                            tag="poly.mu")
                    vh_ap = VTM.t[:, kt * D + 128 * _hp + 64 * h:
                                  kt * D + 128 * _hp + 64 * (h + 1)]
                    ymm(h, vh_ap, r2_ap, [(VTM, kt), (PB16, h)], "y.r2")
                    ymm(h, vh_ap, u_ap, [(VTM, kt), (PB16B, h)], "y.u")
                if j >= 0:
                    for h in range(2):
                        vh_ap = VTM.t[:, kt * D + 128 * _hp + 64 * h:
                                      kt * D + 128 * _hp + 64 * (h + 1)]
                        ymm(h, vh_ap, TRIJ.t[:, j * 512:(j + 1) * 512],
                            [(VTM, kt), (TRIJ, None)], "y.tri", stop=last)


            scores(0)
            for kt in range(nkt):
                if kt + 1 < nkt:
                    scores(kt + 1)
                poly_av(kt)
            for h in range(2):
                tk = op("vector", lambda e, _hp=hp, _h=h, _b=ybks[h][0]:
                        e.tensor_copy(
                            QY.t[64 * _h:64 * (_h + 1),
                                 _hp * 512:(_hp + 1) * 512],
                            _b[64 * _h:64 * (_h + 1), :]),
                        wr=[(QY, hp)], extra=ybks[h][1].rd(), tag="y.ev")
                ybks[h][1].did_read(tk)
        for mh in range(2):
            bks = banks.take(4)
            for c in range(NCH):
                sl = c % 2
                dma(sWSA[sl], WSA.t[:, sl * 1024:(sl + 1) * 1024],
                    w_l0p[128 * c:128 * (c + 1), :], wr=[(WSA, sl)], tag="wp.st")
                for mi in range(4):
                    m = mh * 4 + mi
                    mm(bks[mi], bks[mi][0][:, :],
                       WSA.t[:, sl * 1024 + 128 * m:sl * 1024 + 128 * (m + 1)],
                       QY.t[:, c * 512:(c + 1) * 512],
                       start=(c == 0), stop=(c == NCH - 1),
                       rd=[(WSA, sl), (QY, c)], tag="s.mm")
            for mi in range(4):
                m = mh * 4 + mi
                tk = op("vector", lambda e, _b=bks[mi][0], _m=m: e.tensor_copy(
                    SSUP.t[:, _m * 512:(_m + 1) * 512], _b[:, :]),
                    wr=[(SSUP, 0)], extra=bks[mi][1].rd(), tag="s.ev")
                bks[mi][1].did_read(tk)
        for c in range(NCH):
            dma(sXG, XG.t[:, c * 512:(c + 1) * 512],
                x_in[128 * c:128 * (c + 1), SUP * s:SUP * (s + 1)],
                wr=[(XG, 0)], tag="xg.ld")
        seal(XG, None, sXG)
        for c in range(NCH):
            op("vector", lambda e, _c=c: e.tensor_tensor(
                PR.t[:], SSUP.t[:, _c * 512:(_c + 1) * 512],
                GSUP.t[:, _c * 512:(_c + 1) * 512], ALU.mult),
                rd=[(SSUP, 0), (GSUP, 0)], wr=[(PR, None)], tag="gate.sg")
            op("vector", lambda e, _c=c: e.tensor_tensor(
                SSUP.t[:, _c * 512:(_c + 1) * 512], PR.t[:],
                XG.t[:, _c * 512:(_c + 1) * 512], ALU.add),
                rd=[(PR, None), (XG, 0)], wr=[(SSUP, 0)], tag="gate.add")
            tgt = out_t if PHASES == "l0" else hbm_x1
            dma(sOUT, tgt[128 * c:128 * (c + 1), SUP * s:SUP * (s + 1)],
                SSUP.t[:, c * 512:(c + 1) * 512], rd=[(SSUP, 0)],
                wr=[(HX1, s)], tag="x1.st")


    banks.release_all()

    if PHASES != "l0":
        sX1 = p.stream("sync", "x1l")
        sG = p.stream("sync", "g")
        sQKV = [p.stream("gpsimd", f"qkv{i}") for i in range(2)]
        sW1 = [p.stream("gpsimd", f"w1_{i}") for i in range(2)]
        sW2b = [p.stream("gpsimd", f"w2_{i}") for i in range(2)]
        sWO = [p.stream("gpsimd", f"wo{i}") for i in range(2)]
        sWB = p.stream("gpsimd", "wb")

        XF = mem.at("l1_xf", oXN, [128, 16384], F32, ("l1f",), nslots=NTOK)
        VL = mem.at("l1_vl", oVR, [128, 8 * 512], BF16, ("l1a",), nslots=8)
        HBR = mem.at("l1_hb", oVR + 8192, [128, 8 * 512], BF16, ("l1a",))
        PREQ = mem.at("l1_preq", oVR + 16384, [128, 4 * 512], BF16, ("l1a",), nslots=4)
        PREK = mem.at("l1_prek", oVR + 20480, [128, 4 * 512], BF16, ("l1a",), nslots=4)
        QTOK = mem.at("l1_qtok", oVR + 24576, [128, 512], BF16, ("l1a",))
        KTOK = mem.at("l1_ktok", oVR + 25600, [128, 512], BF16, ("l1a",))
        OF64 = mem.at("l1_of64", oVR + 16384, [64, 8 * 512], BF16, ("l1ab",))
        STR = mem.at("l1_str", oVR + 8192, [128, 8 * 512], BF16, ("l1ab",))
        R2S = mem.at("l1_r2s", oVR + 30720, [128, 512], F32, ("l1a",))
        QP = mem.at("l1_qp", oWR, [128, 8 * 512], BF16, ("l1a",), nslots=8)
        KP = mem.at("l1_kp", oWR + 8192, [128, 8 * 512], BF16, ("l1a",), nslots=8)
        WQKVS = mem.at("l1_wqkvs", oWR + 16384, [128, 2 * 1536], BF16, ("l1a",), nslots=2)
        WOS = mem.at("l1_wos", oWR + 22528, [128, 2 * 1024], BF16, ("l1a",), nslots=2)
        W1S = mem.at("l1_w1s", oWR + 26624, [128, 2 * 2048], BF16, ("l1a",), nslots=2)
        W2S = mem.at("l1_w2s", oWR + 34816, [128, 2 * 1024], BF16, ("l1a",), nslots=2)
        XTB = mem.at("l1_xtb", oFA, [128, 2048], F32, ("l1a",))
        ABT = mem.at("l1_abt", oFA + 8192, [128, 4 * 1024], F16, ("l1a",), nslots=4)
        H2R = mem.at("l1_h2", oFA + 16384, [128, 22 * 512], BF16, ("l1a",))
        A16 = mem.at("l1_a16", oFA + 38912, [128, 512], F16, ("l1a",))
        G16 = mem.at("l1_g16", oFA + 39936, [128, 512], F16, ("l1a",))
        ROT1 = mem.at("l1_rot1", oWS + 4096, [128, 512], F32, ("l1a",))
        ROT2 = mem.at("l1_rot2", oWS + 6144, [128, 512], F32, ("l1a",))
        ROT3 = mem.at("l1_rot3", oWS + 8192, [128, 512], F32, ("l1a",))
        RSR = mem.at("l1_rs", oWS + 10240, [1, 1024], F32, ("l1a",))

        alias_from(XF, XN, KFM)
        alias_from(VL, VTM)
        alias_from(HBR, VTM)
        alias_from(PREQ, VTM)
        alias_from(PREK, VTM)
        alias_from(QTOK, VTM)
        alias_from(KTOK, VTM)
        alias_from(OF64, VTM)
        alias_from(STR, VTM)
        alias_from(R2S, VTM)
        alias_from(QP, WE0)
        alias_from(KP, WE0, QY)
        alias_from(WQKVS, QY)
        alias_from(WOS, QY, WQR)
        alias_from(W1S, WQR)
        alias_from(W2S, WQR)
        alias_from(XTB, XSUP, SSUP)
        alias_from(ABT, XSUP, SSUP)
        alias_from(H2R, GSUP, XG)
        alias_from(A16, XG)
        alias_from(G16, XG)
        alias_from(ROT1, SPFX)
        alias_from(ROT2, TRIJ)
        alias_from(ROT3, TRIJ)
        alias_from(RSR, PB16, PB16B)

        st_dc2 = op("vector", lambda e: e.memset(t_dc[:, 8:9], -50.0), wr=[(DC, None)])
        op("vector", lambda e: e.memset(t_dc[:, 9:10], -20.0), wr=[(DC, None)])

        X1L = mem.at("l1_x1l", oFA + 16384, [128, 8 * 512], F32, ("l1ln",))
        alias_from(X1L, GSUP, XG)
        SQ2 = mem.at("l1_sq2", oFA, [128, 1024], F32, ("l1ln2",), nslots=2)
        alias_from(SQ2, XSUP, SSUP)

        for s in range(NSUP):
            for c in range(NCH):
                dma(sX1, X1L.t[:, c * 512:(c + 1) * 512],
                    hbm_x1[128 * c:128 * (c + 1), SUP * s:SUP * (s + 1)],
                    rd=[(HX1, s)], wr=[(X1L, 0)], tag="x1.ld")
            seal(X1L, None, sX1)
            ln_stats_apply(
                x_rd=lambda c, _s=s: (X1L, 0),
                x_chunk_ap=lambda c, _s=s: X1L.t[:, c * 512:(c + 1) * 512],
                dst_wr=lambda c, _s=s: (XF, [4 * _s + j for j in range(4)]),
                dst_ap=lambda c, _s=s: XF.t[:, c * T + SUP * _s:c * T + SUP * (_s + 1)],
                sq=SQ2, sq_ap=lambda sl: SQ2.t[:, sl * 512:(sl + 1) * 512],
                tag=f"ln1.{s}")
        alias_from(XTB, SQ2)
        alias_from(ABT, SQ2)
        alias_from(H2R, X1L)

        cosb_ap = XTB.t[:, 0:512]; sinb_ap = XTB.t[:, 512:1024]
        scrb_ap = XTB.t[:, 1024:1536]; xscr_ap = XTB.t[:, 1536:2048]
        r_invf = RSR.t[0:1, 0:32]
        r_tmp = RSR.t[0:1, 32:64]
        r_lnb = RSR.t[0:1, 64:96]
        r_qs = RSR.t[0:1, 96:160]
        r_ks = RSR.t[0:1, 160:224]
        op("scalar", lambda e: e.activation(r_invf, XROW.t[0:1, 0:32], AF.Exp,
           scale=DC.t[0:1, 3:4]), rd=[(XROW, None), (DC, None)], wr=[(RSR, None)],
           tag="xt.invf")
        op("vector", lambda e: e.memset(t_dc[:, 7:8], 448.0), wr=[(DC, None)])
        op("vector", lambda e: e.reciprocal(t_dc[:, 6:7], t_dc[:, 7:8]),
           rd=[(DC, None)], wr=[(DC, None)])
        op("scalar", lambda e: e.activation(r_lnb, XROW.t[0:1, 32:64], AF.Ln,
           scale=DC.t[0:1, 6:7]), rd=[(XROW, None), (DC, None)], wr=[(RSR, None)],
           tag="xt.lnb")

        def cw_sincos(bank_ang, breg, dst_sin, dst_cos, wr_reg, tag):
            def one_pass(bias_col, kq, dst, ptag):
                t1 = op("scalar", lambda e, _bc=bias_col: e.activation(
                        ROT1.t[:], bank_ang[:, :], AF.Identity,
                        scale=DC.t[:, 4:5], bias=_bc),
                        rd=[(DC, None), (CC, None)], wr=[(ROT1, None)],
                        extra=breg.rd(), tag=ptag + ".y")
                breg.did_read(t1)
                op("vector", lambda e: e.tensor_copy(ROT2.t[:], ROT1.t[:]),
                   rd=[(ROT1, None)], wr=[(ROT2, None)], tag=ptag + ".fc")
                for _b in (512.0, 256.0, 128.0, 64.0, 32.0, 16.0, 8.0, 4.0,
                           2.0, 1.0):
                    op("vector", lambda e, _bb=_b: e.tensor_scalar(
                        ROT3.t[:], ROT2.t[:], _bb, None, ALU.is_ge),
                        rd=[(ROT2, None)], wr=[(ROT3, None)], tag=ptag + ".fg")
                    op("vector", lambda e, _bb=_b: e.tensor_scalar_mul(
                        PR.t[:], ROT3.t[:], -_bb),
                        rd=[(ROT3, None)], wr=[(PR, None)], tag=ptag + ".fm")
                    op("vector", lambda e: e.tensor_tensor(
                        ROT2.t[:], ROT2.t[:], PR.t[:], ALU.add),
                        rd=[(ROT2, None), (PR, None)], wr=[(ROT2, None)],
                        tag=ptag + ".fs")
                op("vector", lambda e: e.tensor_scalar_mul(
                    PR.t[:], ROT3.t[:], -2.0),
                    rd=[(ROT3, None)], wr=[(PR, None)], tag=ptag + ".sg1")
                op("vector", lambda e: e.tensor_scalar_add(
                    PR.t[:], PR.t[:], 1.0),
                    rd=[(PR, None)], wr=[(PR, None)], tag=ptag + ".sg2")
                op("vector", lambda e: e.tensor_tensor(ROT1.t[:], ROT1.t[:],
                   ROT2.t[:], ALU.subtract),
                   rd=[(ROT1, None), (ROT2, None)], wr=[(ROT1, None)],
                   tag=ptag + ".k")
                if kq:
                    op("vector", lambda e: e.tensor_scalar_add(
                        ROT1.t[:], ROT1.t[:], kq),
                        rd=[(ROT1, None)], wr=[(ROT1, None)], tag=ptag + ".kq")
                op("vector", lambda e: e.tensor_scalar_mul(ROT3.t[:], ROT1.t[:],
                   -_C1H), rd=[(ROT1, None)], wr=[(ROT3, None)],
                   tag=ptag + ".r1m")
                t2 = op("vector", lambda e: e.tensor_tensor(ROT2.t[:],
                        ROT3.t[:], bank_ang[:, :], ALU.add),
                        rd=[(ROT3, None)], wr=[(ROT2, None)],
                        extra=breg.rd(), tag=ptag + ".r1")
                breg.did_read(t2)
                for cv, nm in ((_C2H, ".r2"), (_C3H, ".r3")):
                    op("vector", lambda e, _cv=cv: e.tensor_scalar_mul(
                        ROT3.t[:], ROT1.t[:], -_cv),
                        rd=[(ROT1, None)], wr=[(ROT3, None)], tag=ptag + nm + "m")
                    op("vector", lambda e: e.tensor_tensor(ROT2.t[:], ROT2.t[:],
                       ROT3.t[:], ALU.add),
                       rd=[(ROT2, None), (ROT3, None)], wr=[(ROT2, None)],
                       tag=ptag + nm)
                op("vector", lambda e: e.tensor_single_scalar(ROT2.t[:],
                   ROT2.t[:], _PI_LO, ALU.min),
                   rd=[(ROT2, None)], wr=[(ROT2, None)], tag=ptag + ".clA")
                op("vector", lambda e: e.tensor_single_scalar(ROT2.t[:],
                   ROT2.t[:], -_PI_LO, ALU.max),
                   rd=[(ROT2, None)], wr=[(ROT2, None)], tag=ptag + ".cl")
                op("scalar", lambda e: e.activation(ROT2.t[:], ROT2.t[:],
                   AF.Sin), rd=[(ROT2, None)], wr=[(ROT2, None)],
                   tag=ptag + ".sin")
                op("vector", lambda e, _d=dst: e.tensor_tensor(_d, ROT2.t[:],
                   PR.t[:], ALU.mult),
                   rd=[(ROT2, None), (PR, None)], wr=[wr_reg],
                   tag=ptag + ".ap")
            one_pass(CC.t[:, 5:6], None, dst_sin, tag + ".s")
            one_pass(CC.t[:, 6:7], -0.5, dst_cos, tag + ".c")

        dma(sCONST, cosb_ap, c_xtcos, wr=[(XTB, None)], tag="xt.ldc")
        dma(sCONST, sinb_ap, c_xtsin, wr=[(XTB, None)], tag="xt.lds")
        dma(sCONST, scrb_ap, c_xtscl, wr=[(XTB, None)], tag="xt.ldr")
        seal(XTB, None, sCONST)

        def build_ab(l, kind, a_slot, b_slot, tag):
            if kind == "q":
                op("vector", lambda e: e.tensor_tensor(xscr_ap, cosb_ap, scrb_ap,
                   ALU.mult), rd=[(XTB, None)], wr=[(XTB, None)], tag=tag + ".cq")
            else:
                op("vector", lambda e: e.tensor_tensor(ROT1.t[:], scrb_ap, scrb_ap,
                   ALU.bypass), rd=[(XTB, None)], wr=[(ROT1, None)], tag=tag + ".cp")
                op("vector", lambda e: e.reciprocal(ROT1.t[:], ROT1.t[:]),
                   rd=[(ROT1, None)], wr=[(ROT1, None)], tag=tag + ".ci")
                op("vector", lambda e: e.tensor_tensor(xscr_ap, cosb_ap, ROT1.t[:],
                   ALU.mult), rd=[(XTB, None), (ROT1, None)], wr=[(XTB, None)],
                   tag=tag + ".ck")
            qrow = RROW.t[0:1, 128 * l + (0 if kind == "q" else 64):
                          128 * l + (64 if kind == "q" else 128)]
            qsc = 8.0 if kind == "q" else 1.0
            bkq = banks.take()
            opx = op("vector", lambda e, _q=qrow, _s=qsc: e.tensor_scalar_mul(
                RSR.t[0:1, 224:288], _q, _s), rd=[(RROW, None)], wr=[(RSR, None)],
                tag=tag + ".qs")
            mm(bkq, bkq[0][:, 0:64], ONES_F.t[0:1, 0:128], RSR.t[0:1, 224:288],
               start=True, stop=True, rd=[(RSR, None), (ONES_F, None)],
               tag=tag + ".qb")
            xrep = bass.AP(XTB.t, 1536, [[2048, 128], [32, 16], [0, 2], [1, 32]])
            qrep = None
            def fa(e, _sl=a_slot, _x=xrep, _b=bkq[0]):
                qr = bass.AP(_b, 0, [[512, 128], [0, 16], [1, 64]])
                return e.tensor_tensor(ABT.t[:, _sl * 1024:(_sl + 1) * 1024],
                                       _x, qr, ALU.mult)
            ta = op("vector", fa, rd=[(XTB, None)], wr=[(ABT, a_slot)],
                    extra=bkq[1].rd(), tag=tag + ".A")
            bkq[1].did_read(ta)
            if kind == "q":
                op("vector", lambda e: e.tensor_tensor(xscr_ap, sinb_ap, scrb_ap,
                   ALU.mult), rd=[(XTB, None)], wr=[(XTB, None)], tag=tag + ".sq")
            else:
                op("vector", lambda e: e.tensor_tensor(xscr_ap, sinb_ap, ROT1.t[:],
                   ALU.mult), rd=[(XTB, None), (ROT1, None)], wr=[(XTB, None)],
                   tag=tag + ".sk")
            def fb(e, _sl=b_slot, _x=xrep, _b=bkq[0]):
                qr = bass.AP(_b, 0, [[512, 128], [0, 16], [1, 64]])
                return e.tensor_tensor(ABT.t[:, _sl * 1024:(_sl + 1) * 1024],
                                       _x, qr, ALU.mult)
            tb = op("vector", fb, rd=[(XTB, None)], wr=[(ABT, b_slot)],
                    extra=bkq[1].rd(), tag=tag + ".B")
            bkq[1].did_read(tb)

        for l in range(2):
            build_ab(l, "q", 0, 1, f"ab{l}q")
            build_ab(l, "k", 2, 3, f"ab{l}k")
            sqr = [ROT1, ROT2]

            for s in range(NSUP):
                alias_from(HBR, STR)
                alias_from(PREQ, OF64)
                alias_from(PREK, OF64)
                xsl = [4 * s + j for j in range(4)]
                for c in range(NCH):
                    if c < 4:
                        op("vector", lambda e, _c=c, _s=s: e.tensor_copy(
                            HBR.t[:, _c * 512:(_c + 1) * 512],
                            XF.t[:, _c * T + SUP * _s:_c * T + SUP * (_s + 1)]),
                            rd=[(XF, xsl)], wr=[(HBR, None)], tag="hb.c")
                    elif s > 0:
                        op("vector", lambda e, _c=c: e.tensor_copy(
                            HBR.t[:, _c * 512:_c * 512 + 1],
                            R2S.t[:, _c - 4:_c - 3]),
                            rd=[(R2S, None)], wr=[(HBR, None)], tag="hb.sb")
                        op("vector", lambda e, _c=c, _s=s: e.tensor_copy(
                            HBR.t[:, _c * 512 + 1:(_c + 1) * 512],
                            XF.t[:, _c * T + SUP * _s:_c * T + SUP * (_s + 1) - 1]),
                            rd=[(XF, xsl)], wr=[(HBR, None)], tag="hb.s")
                    else:
                        op("vector", lambda e, _c=c: e.memset(
                            HBR.t[:, _c * 512:_c * 512 + 1], 0.0),
                            wr=[(HBR, None)], tag="hb.z")
                        op("vector", lambda e, _c=c: e.tensor_copy(
                            HBR.t[:, _c * 512 + 1:(_c + 1) * 512],
                            XF.t[:, _c * T:_c * T + 511]),
                            rd=[(XF, xsl)], wr=[(HBR, None)], tag="hb.s0")
                for ki, (wsrc, dst, dsl) in enumerate(
                        ((w_fq[l], PREQ, None), (w_fk[l], PREK, None),
                         (w_fv[l], VL, None))):
                    bks = banks.take(4)
                    for c in range(NCH):
                        sl = c % 2
                        dma(sQKV[sl], WQKVS.t[:, sl * 1536 + 512 * 0:sl * 1536 + 512],
                            wsrc[128 * c:128 * (c + 1), :], wr=[(WQKVS, sl)],
                            tag="wqkv.ld")
                        for t in range(4):
                            mm(bks[t], bks[t][0][:, :],
                               HBR.t[:, c * 512 + 128 * t:c * 512 + 128 * (t + 1)],
                               WQKVS.t[:, sl * 1536:sl * 1536 + 512],
                               start=(c == 0), stop=(c == NCH - 1),
                               rd=[(WQKVS, sl), (HBR, None)], tag=f"qkv{ki}.mm")
                    for t in range(4):
                        tt = 4 * s + t
                        if ki == 2:
                            tk = op("vector", lambda e, _b=bks[t][0], _tt=tt:
                                    e.tensor_copy(
                                        VL.t[:, (_tt % 8) * 512:(_tt % 8 + 1) * 512],
                                        _b[:, :]),
                                    wr=[(VL, tt % 8)], extra=bks[t][1].rd(),
                                    tag="v.ev")
                        else:
                            dstr = PREQ if ki == 0 else PREK
                            tk = op("vector", lambda e, _b=bks[t][0], _t=t, _d=dstr:
                                    e.tensor_copy(
                                        _d.t[:, _t * 512:(_t + 1) * 512], _b[:, :]),
                                    wr=[(dstr, t)], extra=bks[t][1].rd(),
                                    tag="pre.ev")
                        bks[t][1].did_read(tk)
                for ki in range(2):
                    pre = PREQ if ki == 0 else PREK
                    tokd = QTOK if ki == 0 else KTOK
                    ring = QP if ki == 0 else KP
                    asl, bsl = (0, 1) if ki == 0 else (2, 3)
                    for t in range(4):
                        tt = 4 * s + t
                        pap = pre.t[:, t * 512:(t + 1) * 512]
                        op("scalar", lambda e, _p=pap: e.activation(
                            PR.t[:], _p, AF.Square), rd=[(pre, t)],
                            wr=[(PR, None)], tag="l2.sq")
                        for h in range(8):
                            op("vector", lambda e, _h=h: e.reduce_sum(
                                STAT.t[:, 1008 + _h:1009 + _h],
                                PR.t[:, 64 * _h:64 * (_h + 1)], AX.X),
                                rd=[(PR, None)], wr=[(STAT, None)], tag="l2.red")
                        op("scalar", lambda e: e.activation(
                            STAT.t[:, 1000:1008], STAT.t[:, 1008:1016], AF.Sqrt),
                            rd=[(STAT, None)], wr=[(STAT, None)], tag="l2.sd")
                        op("vector", lambda e: e.tensor_scalar_max(
                            STAT.t[:, 1000:1008], STAT.t[:, 1000:1008], 1e-12),
                            rd=[(STAT, None)], wr=[(STAT, None)], tag="l2.mx")
                        op("vector", lambda e: e.reciprocal(
                            STAT.t[:, 992:1000], STAT.t[:, 1000:1008]),
                            rd=[(STAT, None)], wr=[(STAT, None)], tag="l2.rc")
                        def f_qr(e, _p=pap):
                            rr = bass.AP(STAT.t, 992, [[1024, 128], [1, 8], [0, 64]])
                            return e.tensor_tensor(ROT1.t[:], _p, rr, ALU.mult)
                        op("vector", f_qr, rd=[(pre, t), (STAT, None)],
                           wr=[(ROT1, None)], tag="rot.qr")
                        def f_t1(e, _sl=asl, _t=t, _s=s):
                            arep = bass.AP(ABT.t, _sl * 1024 + (4 * _s + _t) * 64,
                                           [[4096, 128], [0, 8], [1, 64]])
                            return e.tensor_tensor(ROT2.t[:], ROT1.t[:], arep,
                                                   ALU.mult)
                        op("vector", f_t1, rd=[(ROT1, None), (ABT, asl)],
                           wr=[(ROT2, None)], tag="rot.t1")
                        def f_t2(e, _sl=bsl, _t=t, _s=s):
                            brep = bass.AP(ABT.t, _sl * 1024 + (4 * _s + _t) * 64,
                                           [[4096, 128], [0, 8], [1, 64]])
                            return e.tensor_tensor(ROT3.t[:], ROT1.t[:], brep,
                                                   ALU.mult)
                        op("vector", f_t2, rd=[(ROT1, None), (ABT, bsl)],
                           wr=[(ROT3, None)], tag="rot.t2")
                        def f_lo(e):
                            i0 = bass.AP(ROT2.t, 0, [[512, 128], [64, 8], [1, 32]])
                            i1 = bass.AP(ROT3.t, 32, [[512, 128], [64, 8], [1, 32]])
                            o0 = bass.AP(ROT1.t, 0, [[512, 128], [64, 8], [1, 32]])
                            return e.tensor_tensor(o0, i0, i1, ALU.subtract)
                        op("vector", f_lo, rd=[(ROT2, None), (ROT3, None)],
                           wr=[(ROT1, None)], tag="rot.lo")
                        def f_hi(e):
                            i0 = bass.AP(ROT2.t, 32, [[512, 128], [64, 8], [1, 32]])
                            i1 = bass.AP(ROT3.t, 0, [[512, 128], [64, 8], [1, 32]])
                            o0 = bass.AP(ROT1.t, 32, [[512, 128], [64, 8], [1, 32]])
                            return e.tensor_tensor(o0, i0, i1, ALU.add)
                        op("vector", f_hi, rd=[(ROT2, None), (ROT3, None)],
                           wr=[(ROT1, None)], tag="rot.hi")
                        bkt = banks.take()
                        pw = bkt[1].wr()
                        bkt[1].w[0] = []; bkt[1].r[0] = []
                        for cq in range(4):
                            mm_t = op("tensor", lambda e, _b=bkt[0], _c=cq:
                                      e.transpose(_b[:, _c * 128:(_c + 1) * 128],
                                                  ROT1.t[:, _c * 128:(_c + 1) * 128],
                                                  ID_F.t[:]),
                                      rd=[(ROT1, None), (ID_F, None)],
                                      extra=(pw if cq == 0 else ()),
                                      tag="rot.tr")
                            bkt[1].did_write(mm_t)
                        tk = op("vector", lambda e, _b=bkt[0], _sl=tt % 8, _r=ring:
                                e.tensor_copy(
                                    _r.t[:, _sl * 512:(_sl + 1) * 512], _b[:, :]),
                                wr=[(ring, tt % 8)], extra=bkt[1].rd(),
                                tag="rot.ev")
                        bkt[1].did_read(tk)
                op("vector", lambda e, _s=s: e.tensor_copy(
                    R2S.t[:, 0:4],
                    bass.AP(XF.t, 4 * T + SUP * (_s + 1) - 1,
                            [[16384, 128], [T, 4]])),
                    rd=[(XF, 4 * s + 3)], wr=[(R2S, None)], tag="sv1")

                w = s
                alias_from(OF64, PREQ, PREK)
                alias_from(STR, HBR)
                for h in range(8):
                    for j in range(4):
                        if w > 0:
                            runs = [list(range(4 * (w - 1) + j, 4 * w)),
                                    list(range(4 * w, 4 * w + j + 1))]
                        else:
                            runs = [list(range(0, j + 1))]
                        nr = len(runs)
                        lhsq = QP.t[64 * (h % 2):64 * (h % 2) + 64,
                                    ((4 * w + j) % 8) * 512 + (h // 2) * 128:
                                    ((4 * w + j) % 8) * 512 + (h // 2) * 128 + 128]
                        scbs = []
                        for ri, run in enumerate(runs):
                            scb = banks.take()
                            scbs.append(scb)
                            krh = bass.AP(
                                KP.t, 64 * (h % 2) * 4096 +
                                (run[0] % 8) * 512 + (h // 2) * 128,
                                [[4096, 64], [512, len(run)], [1, 128]])
                            mm(scb, scb[0][:, 0:128 * len(run)], lhsq, krh,
                               start=True, stop=True,
                               rd=[(KP, [kt % 8 for kt in run]),
                                   (QP, (4 * w + j) % 8)],
                               tile_position=(64 * (h % 2), 0), ldw=True,
                               tag="la.sc")

                        def addmask(scb, col, mreg):
                            tmk = op("vector", lambda e, _b=scb[0], _c=col,
                                     _m=mreg: e.tensor_tensor(
                                         _b[:, _c * 128:(_c + 1) * 128],
                                         _b[:, _c * 128:(_c + 1) * 128],
                                         _m.t[:], ALU.add),
                                     rd=[(mreg, None)], extra=scb[1].rd(),
                                     tag="la.mask")
                            scb[1].did_read(tmk)
                            scb[1].did_write(tmk)
                        if w > 0:
                            addmask(scbs[0], 0, MASKU)
                        addmask(scbs[-1], len(runs[-1]) - 1, MASKL)
                        for ri, run in enumerate(runs):
                            t_r = op("vector", lambda e, _b=scbs[ri][0], _ri=ri,
                                     _n=128 * len(run): e.reduce_max(
                                         STAT.t[:, 944 + _ri:945 + _ri],
                                         _b[:, 0:_n], AX.X),
                                     wr=[(STAT, None)], extra=scbs[ri][1].rd(),
                                     tag="la.max")
                            scbs[ri][1].did_read(t_r)
                        if nr == 2:
                            op("vector", lambda e: e.tensor_tensor(
                                STAT.t[:, 944:945], STAT.t[:, 944:945],
                                STAT.t[:, 945:946], ALU.max),
                                rd=[(STAT, None)], wr=[(STAT, None)],
                                tag="la.max2")
                        op("vector", lambda e: e.tensor_scalar_mul(
                            STAT.t[:, 946:947], STAT.t[:, 944:945], -1.0),
                            rd=[(STAT, None)], wr=[(STAT, None)], tag="la.neg")
                        for ri, run in enumerate(runs):
                            phd = ROT1 if ri == 0 else ROT2
                            tex = op("scalar", lambda e, _b=scbs[ri][0], _d=phd,
                                     _n=128 * len(run), _ri=ri: e.activation(
                                         _d.t[:, 0:_n], _b[:, 0:_n], AF.Exp,
                                         bias=STAT.t[:, 946:947],
                                         accum_out=STAT.t[:, 948 + _ri:
                                                          949 + _ri]),
                                     rd=[(STAT, None)],
                                     wr=[(phd, None), (STAT, None)],
                                     extra=scbs[ri][1].rd(), tag="la.exp")
                            scbs[ri][1].did_read(tex)
                        if nr == 2:
                            op("vector", lambda e: e.tensor_tensor(
                                STAT.t[:, 948:949], STAT.t[:, 948:949],
                                STAT.t[:, 949:950], ALU.add),
                                rd=[(STAT, None)], wr=[(STAT, None)],
                                tag="la.rs2")
                        op("vector", lambda e: e.reciprocal(
                            STAT.t[:, 950:951], STAT.t[:, 948:949]),
                            rd=[(STAT, None)], wr=[(STAT, None)], tag="la.rc")
                        for ri, run in enumerate(runs):
                            phd = ROT1 if ri == 0 else ROT2
                            op("scalar", lambda e, _d=phd, _n=128 * len(run):
                               e.activation(_d.t[:, 0:_n], _d.t[:, 0:_n],
                                            AF.Identity,
                                            scale=STAT.t[:, 950:951]),
                               rd=[(phd, None), (STAT, None)], wr=[(phd, None)],
                               tag="la.nrm")
                            bkt = banks.take()
                            pw = bkt[1].wr()
                            bkt[1].w[0] = []
                            bkt[1].r[0] = []
                            for ii in range(len(run)):
                                tmt = op("tensor", lambda e, _b=bkt[0], _i=ii,
                                         _d=phd: e.transpose(
                                             _b[:, _i * 128:(_i + 1) * 128],
                                             _d.t[:, _i * 128:(_i + 1) * 128],
                                             ID_F.t[:]),
                                         rd=[(phd, None), (ID_F, None)],
                                         extra=(pw if ii == 0 else ()),
                                         tag="la.tr")
                                bkt[1].did_write(tmt)
                            tev = op("vector", lambda e, _b=bkt[0],
                                     _n=len(run), _k0=run[0], _j=j:
                                     e.tensor_copy(
                                         bass.AP(STR.t,
                                                 (_k0 % 8) * 512 + _j * 128,
                                                 [[4096, 128], [512, _n],
                                                  [1, 128]]),
                                         _b[:, 0:128 * _n]),
                                     wr=[(STR, None)], extra=bkt[1].rd(),
                                     tag="la.trev")
                            bkt[1].did_read(tev)
                    ybk = banks.take()
                    kts_av = list(range(4 * w, 4 * w + 4))
                    if w > 0:
                        kts_av += list(range(4 * w - 4, 4 * w))
                    for ki, kt in enumerate(kts_av):
                        if kt < 4 * w:
                            pp = kt - (4 * w - 4)
                            q0, q1 = 0, 128 * (pp + 1)
                        else:
                            ko = kt - 4 * w
                            q0, q1 = 128 * ko, 512
                        mm(ybk, ybk[0][0:64, q0:q1],
                           VL.t[:, (kt % 8) * 512 + 64 * h:
                                (kt % 8) * 512 + 64 * h + 64],
                           bass.AP(STR.t, (kt % 8) * 512 + q0,
                                   [[4096, 128], [1, q1 - q0]]),
                           start=(ki == 0), stop=(ki == len(kts_av) - 1),
                           rd=[(VL, kt % 8), (STR, None)],
                           tile_position=(0, 0), ldw=True,
                           kwargs_record_write=(ki == len(kts_av) - 1),
                           tag="la.av")
                    tof = op("vector", lambda e, _b=ybk[0], _h=h:
                             e.tensor_copy(
                                 OF64.t[0:64, _h * 512:(_h + 1) * 512],
                                 _b[0:64, :]),
                             wr=[(OF64, None)], extra=ybk[1].rd(), tag="la.oev")
                    ybk[1].did_read(tof)
                bks8 = banks.take(8)
                for h in range(8):
                    sl = h % 2
                    dma(sWO[sl], WOS.t[0:64, sl * 1024:(sl + 1) * 1024],
                        w_o[l][0:64, D * h:D * (h + 1)], wr=[(WOS, sl)],
                        tag="wo.ld")
                    for m in range(NCH):
                        mm(bks8[m], bks8[m][0][:, :],
                           WOS.t[0:64, sl * 1024 + 128 * m:
                                 sl * 1024 + 128 * (m + 1)],
                           OF64.t[0:64, h * 512:(h + 1) * 512],
                           start=(h == 0), stop=(h == 7),
                           rd=[(WOS, sl), (OF64, None)],
                           skip_gc=(h not in (0, 7)),
                           kwargs_record_write=(h == 7), tag="wo.mm")
                for m in range(NCH):
                    wsl = [4 * w + jj for jj in range(4)]
                    tup = op("vector", lambda e, _b=bks8[m][0], _m=m, _w=w:
                             e.tensor_tensor(
                                 XF.t[:, _m * T + SUP * _w:
                                      _m * T + SUP * (_w + 1)],
                                 XF.t[:, _m * T + SUP * _w:
                                      _m * T + SUP * (_w + 1)],
                                 _b[:, :], ALU.add),
                             rd=[(XF, wsl)], wr=[(XF, wsl)],
                             extra=bks8[m][1].rd(), tag="wo.up")
                    bks8[m][1].did_read(tup)

                op("vector", lambda e, _s=s, _d=4 + 4 * (s % 2): e.tensor_copy(
                    R2S.t[:, _d:_d + 4],
                    bass.AP(XF.t, 4 * T + SUP * (_s + 1) - 1,
                            [[16384, 128], [T, 4]])),
                    rd=[(XF, 4 * s + 3)], wr=[(R2S, None)], tag="sv2")
                if os.environ.get("KDBG"):
                    for c in range(NCH):
                        dma(sDBG, dbg_mid[l][128 * c:128 * (c + 1),
                                             SUP * s:SUP * (s + 1)],
                            XF.t[:, c * T + SUP * s:c * T + SUP * (s + 1)],
                            rd=[(XF, [4 * s + jj for jj in range(4)])],
                            tag="dbg.mid")

                alias_from(HBR, STR)
                xsl = [4 * s + j for j in range(4)]
                sv2rd = 4 + 4 * ((s + 1) % 2)
                bkA = banks.take(); bkB = banks.take()
                for c in range(NCH):
                    sl = c % 2
                    bsrc = None
                    if c < 4:
                        xap = XF.t[:, c * T + SUP * s:c * T + SUP * (s + 1)]
                        sqs, sqn = 0, 512
                    elif s > 0:
                        xap = XF.t[:, c * T + SUP * s:c * T + SUP * (s + 1) - 1]
                        sqs, sqn = 1, 511
                        bsrc = R2S.t[:, sv2rd + c - 4:sv2rd + c - 3]
                    else:
                        xap = XF.t[:, c * T:c * T + 511]
                        sqs, sqn = 1, 511
                    op("scalar", lambda e, _x=xap, _sl=sl, _s0=sqs, _n=sqn:
                       e.activation(sqr[_sl].t[:, _s0:_s0 + _n],
                                    _x, AF.Square),
                       rd=[(XF, xsl)], wr=[(sqr[sl], None)], tag="fh.sq")
                    if bsrc is not None:
                        op("scalar", lambda e, _b=bsrc, _sl=sl: e.activation(
                            sqr[_sl].t[:, 0:1], _b, AF.Square),
                            rd=[(R2S, None)], wr=[(sqr[sl], None)], tag="fh.sqb")
                        mm(bkA, bkA[0][0:1, 0:1], ONES_F.t[:, 0:1], bsrc,
                           start=False, stop=False, rd=[(R2S, None)],
                           skip_gc=True, tag="fh.sxb")
                    mm(bkA, bkA[0][0:1, sqs:sqs + sqn], ONES_F.t[:, 0:1], xap,
                       start=(c == 0), stop=(c == NCH - 1), rd=[(XF, xsl)],
                       skip_gc=(c not in (0, NCH - 1)),
                       tag="fh.sx")
                    sq0, sqN = (1, 511) if (c >= 4 and s == 0) else (0, 512)
                    mm(bkB, bkB[0][0:1, sq0:sq0 + sqN], ONES_F.t[:, 0:1],
                       sqr[sl].t[:, sq0:sq0 + sqN],
                       start=(c == 0), stop=(c == NCH - 1), rd=[(sqr[sl], None)],
                       skip_gc=(c not in (0, NCH - 1)),
                       tag="fh.sx2")
                A = STAT.t[0:1, 0:512]
                B = STAT.t[0:1, 512:1024]
                tmp = sqr[0].t[0:1, 0:512]
                t1 = op("vector", lambda e, _b=bkA[0]: e.tensor_scalar_mul(
                        A, _b[0:1, :], -1.0 / 1024.0), wr=[(STAT, None)],
                        extra=bkA[1].rd(), tag="fh.m")
                bkA[1].did_read(t1)
                t2 = op("vector", lambda e, _b=bkB[0]: e.tensor_scalar_mul(
                        B, _b[0:1, :], 1.0 / 1024.0), wr=[(STAT, None)],
                        extra=bkB[1].rd(), tag="fh.e2")
                bkB[1].did_read(t2)
                op("vector", lambda e: e.tensor_tensor(tmp, A, A,
                   ALU.mult), rd=[(STAT, None)], wr=[(sqr[0], None)],
                   tag="fh.msq")
                op("vector", lambda e: e.tensor_tensor(B, B, tmp,
                   ALU.subtract), rd=[(STAT, None), (sqr[0], None)],
                   wr=[(STAT, None)], tag="fh.var")
                op("scalar", lambda e: e.activation(B, B, AF.Sqrt,
                   bias=DC.t[0:1, 10:11]), rd=[(STAT, None), (DC, None)],
                   wr=[(STAT, None)], tag="fh.sd")
                op("vector", lambda e: e.reciprocal(B, B),
                   rd=[(STAT, None)], wr=[(STAT, None)], tag="fh.rstd")
                bkR = banks.take(); bkM = banks.take()
                mm(bkR, bkR[0][:, :], ONES_F.t[0:1, 0:128], B,
                   start=True, stop=True, rd=[(STAT, None)], tag="fh.bcr")
                mm(bkM, bkM[0][:, :], ONES_F.t[0:1, 0:128], A,
                   start=True, stop=True, rd=[(STAT, None)], tag="fh.bcm")
                for c in range(NCH):
                    sl = c % 2
                    bsrc = None
                    if c < 4:
                        hap = XF.t[:, c * T + SUP * s:c * T + SUP * (s + 1)]
                        c0 = 0
                    elif s > 0:
                        hap = XF.t[:, c * T + SUP * s:c * T + SUP * (s + 1) - 1]
                        c0 = 1
                        bsrc = R2S.t[:, sv2rd + c - 4:sv2rd + c - 3]
                    else:
                        hap = XF.t[:, c * T:c * T + 511]
                        c0 = 1
                    n = 512 - c0
                    tk1 = op("vector", lambda e, _h=hap, _sl=sl, _c0=c0, _n=n,
                             _b=bkM[0]: e.tensor_tensor(
                                 sqr[_sl].t[:, _c0:_c0 + _n],
                                 _h, _b[:, _c0:_c0 + _n], ALU.add),
                             rd=[(XF, xsl)], wr=[(sqr[sl], None)],
                             extra=bkM[1].rd(), tag="fh.a1")
                    bkM[1].did_read(tk1)
                    if bsrc is not None:
                        tkb = op("vector", lambda e, _b2=bsrc, _sl=sl,
                                 _b=bkM[0]: e.tensor_tensor(
                                     sqr[_sl].t[:, 0:1], _b2,
                                     _b[:, 0:1], ALU.add),
                                 rd=[(R2S, None)], wr=[(sqr[sl], None)],
                                 extra=bkM[1].rd(), tag="fh.a1b")
                        bkM[1].did_read(tkb)
                    elif c0 == 1:
                        tkz = op("vector", lambda e, _sl=sl, _b=bkM[0]:
                                 e.tensor_copy(sqr[_sl].t[:, 0:1], _b[:, 0:1]),
                                 wr=[(sqr[sl], None)], extra=bkM[1].rd(),
                                 tag="fh.a1z")
                        bkM[1].did_read(tkz)
                    tk2 = op("vector", lambda e, _sl=sl, _b=bkR[0]:
                             e.tensor_tensor(
                                 sqr[_sl].t[:, :],
                                 sqr[_sl].t[:, :],
                                 _b[:, :], ALU.mult),
                             rd=[(sqr[sl], None)], wr=[(sqr[sl], None)],
                             extra=bkR[1].rd(), tag="fh.a2")
                    bkR[1].did_read(tk2)
                    op("scalar", lambda e, _c=c, _sl=sl, _l=l:
                       e.activation(
                           HBR.t[:, _c * 512:(_c + 1) * 512],
                           sqr[_sl].t[:, :], AF.Identity,
                           scale=VEC.t[:, 8 + 16 * _l + _c:9 + 16 * _l + _c],
                           bias=VEC.t[:, 16 + 16 * _l + _c:17 + 16 * _l + _c]),
                       rd=[(sqr[sl], None), (VEC, None)], wr=[(HBR, None)],
                       tag="fh.a3")
                if os.environ.get("KDBG") and s == 0 and l == 0:
                    dma(sDBG, dbg_hn[:, :], HBR.t[:, :], rd=[(HBR, None)],
                        tag="dbg.hn")
                for j in range(22):
                    sl = j % 2
                    dma(sW1[sl], W1S.t[:, sl * 2048:(sl + 1) * 2048],
                        w_1[l][128 * j:128 * (j + 1), :],
                        wr=[(W1S, sl)], tag="w1.ld")
                    bkA2 = banks.take(); bkG2 = banks.take()
                    for c in range(NCH):
                        mm(bkA2, bkA2[0][:, :],
                           W1S.t[:, sl * 2048 + c * 256:sl * 2048 + c * 256 + 128],
                           HBR.t[:, c * 512:(c + 1) * 512],
                           start=(c == 0), stop=(c == NCH - 1),
                           rd=[(W1S, sl), (HBR, None)], tag="w1.a")
                        mm(bkG2, bkG2[0][:, :],
                           W1S.t[:, sl * 2048 + c * 256 + 128:
                                 sl * 2048 + c * 256 + 256],
                           HBR.t[:, c * 512:(c + 1) * 512],
                           start=(c == 0), stop=(c == NCH - 1),
                           rd=[(W1S, sl), (HBR, None)], tag="w1.g")
                    ta = op("scalar", lambda e, _b=bkA2[0], _j=j, _l=l:
                        e.activation(
                            A16.t[:], _b[:, :], AF.Identity,
                            bias=B1V.t[:, 44 * _l + _j:44 * _l + _j + 1]),
                        rd=[(B1V, None)], wr=[(A16, None)],
                        extra=bkA2[1].rd(), tag="w1.aev")
                    bkA2[1].did_read(ta)
                    tg = op("scalar", lambda e, _b=bkG2[0], _j=j, _l=l:
                        e.activation(
                            G16.t[:], _b[:, :], AF.Gelu,
                            bias=B1V.t[:, 44 * _l + 22 + _j:44 * _l + 23 + _j]),
                        rd=[(B1V, None)], wr=[(G16, None)],
                        extra=bkG2[1].rd(), tag="w1.gev")
                    bkG2[1].did_read(tg)
                    op("vector", lambda e, _j=j: e.tensor_tensor(
                        H2R.t[:, _j * 512:(_j + 1) * 512], A16.t[:], G16.t[:],
                        ALU.mult), rd=[(A16, None), (G16, None)],
                        wr=[(H2R, None)], tag="w1.h2")
                if os.environ.get("KDBG") and s == 0 and l == 0:
                    dma(sDBG, dbg_h2[:, :], H2R.t[:, :], rd=[(H2R, None)],
                        tag="dbg.h2")
                bks8b = banks.take(8)
                for j in range(22):
                    sl = j % 2
                    dma(sW2b[sl], W2S.t[:, sl * 1024:(sl + 1) * 1024],
                        w_2[l][128 * j:128 * (j + 1), :], wr=[(W2S, sl)],
                        tag="w2.ld")
                    for m in range(NCH):
                        mm(bks8b[m], bks8b[m][0][:, :],
                           W2S.t[:, sl * 1024 + 128 * m:sl * 1024 + 128 * (m + 1)],
                           H2R.t[:, j * 512:(j + 1) * 512],
                           start=(j == 0), stop=(j == 21),
                           rd=[(W2S, sl), (H2R, None)], tag="w2.mm")
                for m in range(NCH):
                    ssl = [4 * s + jj for jj in range(4)]
                    tpp = op("scalar", lambda e, _b=bks8b[m][0], _m=m, _l=l:
                             e.activation(
                                 PR.t[:], _b[:, :], AF.Identity,
                                 bias=VEC.t[:, 40 + 8 * _l + _m:
                                            41 + 8 * _l + _m]),
                             rd=[(VEC, None)], wr=[(PR, None)],
                             extra=bks8b[m][1].rd(), tag="w2.upb")
                    bks8b[m][1].did_read(tpp)
                    op("vector", lambda e, _m=m, _s=s: e.tensor_tensor(
                        XF.t[:, _m * T + SUP * _s:_m * T + SUP * (_s + 1)],
                        XF.t[:, _m * T + SUP * _s:_m * T + SUP * (_s + 1)],
                        PR.t[:], ALU.add),
                        rd=[(XF, ssl), (PR, None)], wr=[(XF, ssl)],
                        tag="w2.up")
                if os.environ.get("KDBG"):
                    for c in range(NCH):
                        dma(sDBG, dbg_out[l][128 * c:128 * (c + 1),
                                             SUP * s:SUP * (s + 1)],
                            XF.t[:, c * T + SUP * s:c * T + SUP * (s + 1)],
                            rd=[(XF, [4 * s + jj for jj in range(4)])],
                            tag="dbg.out")

        sGL = p.stream("sync", "gl")
        sX1B = p.stream("sync", "x1b")
        XN1 = mem.at("b_xn1", oXN, [128, 8 * T], BF16, ("l1b",), nslots=NSUP)
        KROT = mem.at("b_krot", oKR, [128, 8 * T], BF16, ("l1b",), nslots=NSUP)
        VTB = mem.at("b_vtb", oVR, [128, NTOK * D], BF16, ("l1b",), nslots=NTOK)
        WVR = mem.at("b_wvr", oWR, [128, 8 * D], BF16, ("l1bw",))
        WQR1 = mem.at("b_wqr", oWR + 16384, [128, 8 * D], BF16, ("l1b",))
        PT1 = mem.at("b_pt", oWR + 32768, [128, 16 * 256], BF16, ("l1b",))
        X1S2 = mem.at("b_x1s", oFA, [128, 8 * 512], F32, ("l1bln",))
        QROT = mem.at("b_qrot", oFA, [128, 4096], BF16, ("l1batt",))
        PRE1 = mem.at("b_pre", oFA + 8192, [128, 4096], BF16, ("l1bq",))
        SQB = mem.at("b_sqb", oFA + 32768, [128, 1024], F32, ("l1bln2",), nslots=2)
        PHAT = mem.at("b_phat", oFA + 8192, [128, 2048], BF16, ("l1batt2",))
        GRB = mem.at("b_grb", oFA + 12288, [128, 512], F32, ("l1batt2",))
        X1RB = mem.at("b_x1rb", oFA + 14336, [128, 512], F32, ("l1batt2",))
        COSB1 = mem.at("b_cosb", oFA + 16384, [128, 2048], F32, ("l1b",))
        SINB1 = mem.at("b_sinb", oFA + 24576, [128, 2048], F32, ("l1b",))
        CS1 = mem.at("b_cs", oFA + 32768, [128, 2048], F16, ("l1b",))
        SS1 = mem.at("b_ss", oFA + 36864, [128, 2048], F16, ("l1b",))
        ROTA = mem.at("b_rota", oWS + 8192, [128, 512], F16, ("l1b",))
        ROTB = mem.at("b_rotb", oWS + 9216, [128, 512], F16, ("l1b",))
        THR = mem.at("b_thr", oWS + 12288, [1, 512], F32, ("l1b",))
        OUTS = mem.at("b_outs", oWS + 10240, [128, 512], F32, ("l1b",))

        alias_from(XN1, XF)
        alias_from(KROT, XF)
        alias_from(VTB, VL, HBR, PREQ, PREK, QTOK, KTOK, OF64, STR, R2S)
        alias_from(WVR, QP, KP, WQKVS)
        alias_from(WQR1, WQKVS, WOS, W1S)
        alias_from(PT1, W1S, W2S)
        alias_from(X1S2, XTB, ABT, SQ2)
        alias_from(PRE1, ABT, H2R)
        alias_from(SQB, ABT, H2R)
        alias_from(GRB, H2R)
        alias_from(X1RB, H2R)
        alias_from(COSB1, H2R)
        alias_from(SINB1, H2R, A16)
        alias_from(CS1, G16)
        alias_from(SS1, G16)
        alias_from(ROTA, ROT3)
        alias_from(ROTB, ROT3)
        alias_from(THR, RSR)
        alias_from(OUTS, RSR)

        for s in range(NSUP):
            for c in range(NCH):
                tg = op("scalar", lambda e, _c=c, _s=s: e.activation(
                    OUTS.t[:], XF.t[:, _c * T + SUP * _s:_c * T + SUP * (_s + 1)],
                    AF.Silu), rd=[(XF, [4 * s + j for j in range(4)])],
                    wr=[(OUTS, None)], tag="g.silu")
                dma(sG, hbm_g[128 * c:128 * (c + 1), SUP * s:SUP * (s + 1)],
                    OUTS.t[:], rd=[(OUTS, None)], wr=[(HG, s)], tag="g.st")

        op("scalar", lambda e: e.activation(THR.t[:], ETHR.t[:], AF.Exp,
           scale=DC.t[0:1, 3:4]), rd=[(ETHR, None), (DC, None)], wr=[(THR, None)],
           tag="rb.th")
        for c in range(4):
            bka = banks.take()
            mm(bka, bka[0][:, :], THR.t[0:1, 128 * c:128 * (c + 1)],
               POSR.t[0:1, :], start=True, stop=True,
               rd=[(THR, None), (POSR, None)], tag="rb.ang")
            cw_sincos(bka[0], bka[1],
                      SINB1.t[:, 512 * c:512 * (c + 1)],
                      COSB1.t[:, 512 * c:512 * (c + 1)],
                      (SINB1, None) if True else None, f"rb.{c}")
            COSB1.did_write(SINB1.w[0][-1])

        def cw_col(src_ap, dst_sin_ap, dst_cos_ap, tag):
            A1 = STAT.t[:, 980:984]; A2 = STAT.t[:, 984:988]
            G = STAT.t[:, 944:948]; J = STAT.t[:, 948:952]
            def one_pass(bias_col, kq, dst, ptag):
                op("scalar", lambda e, _bc=bias_col: e.activation(
                   A1, src_ap, AF.Identity, scale=DC.t[:, 4:5], bias=_bc),
                   rd=[(STAT, None), (DC, None), (CC, None)],
                   wr=[(STAT, None)], tag=ptag + ".y")
                op("vector", lambda e: e.tensor_copy(A2, A1),
                   rd=[(STAT, None)], wr=[(STAT, None)], tag=ptag + ".fc")
                for _b in (512.0, 256.0, 128.0, 64.0, 32.0, 16.0, 8.0, 4.0,
                           2.0, 1.0):
                    op("vector", lambda e, _bb=_b: e.tensor_scalar(
                        G, A2, _bb, None, ALU.is_ge),
                        rd=[(STAT, None)], wr=[(STAT, None)], tag=ptag + ".fg")
                    op("vector", lambda e, _bb=_b: e.tensor_scalar_mul(
                        J, G, -_bb),
                        rd=[(STAT, None)], wr=[(STAT, None)], tag=ptag + ".fm")
                    op("vector", lambda e: e.tensor_tensor(A2, A2, J, ALU.add),
                        rd=[(STAT, None)], wr=[(STAT, None)], tag=ptag + ".fs")
                op("vector", lambda e: e.tensor_scalar_mul(J, G, -2.0),
                   rd=[(STAT, None)], wr=[(STAT, None)], tag=ptag + ".sg1")
                op("vector", lambda e: e.tensor_scalar_add(J, J, 1.0),
                   rd=[(STAT, None)], wr=[(STAT, None)], tag=ptag + ".sg2")
                op("vector", lambda e: e.tensor_tensor(A1, A1, A2, ALU.subtract),
                   rd=[(STAT, None)], wr=[(STAT, None)], tag=ptag + ".k")
                if kq:
                    op("vector", lambda e: e.tensor_scalar_add(A1, A1, kq),
                       rd=[(STAT, None)], wr=[(STAT, None)], tag=ptag + ".kq")
                op("vector", lambda e: e.tensor_scalar_mul(G, A1, -_C1H),
                   rd=[(STAT, None)], wr=[(STAT, None)], tag=ptag + ".r1m")
                op("vector", lambda e: e.tensor_tensor(A2, src_ap, G, ALU.add),
                   rd=[(STAT, None)], wr=[(STAT, None)], tag=ptag + ".r1")
                for cv, nm in ((_C2H, ".r2"), (_C3H, ".r3")):
                    op("vector", lambda e, _cv=cv: e.tensor_scalar_mul(
                        G, A1, -_cv),
                        rd=[(STAT, None)], wr=[(STAT, None)], tag=ptag + nm + "m")
                    op("vector", lambda e: e.tensor_tensor(A2, A2, G, ALU.add),
                       rd=[(STAT, None)], wr=[(STAT, None)], tag=ptag + nm)
                op("vector", lambda e: e.tensor_single_scalar(A2, A2, _PI_LO,
                   ALU.min), rd=[(STAT, None)], wr=[(STAT, None)],
                   tag=ptag + ".clA")
                op("vector", lambda e: e.tensor_single_scalar(A2, A2, -_PI_LO,
                   ALU.max), rd=[(STAT, None)], wr=[(STAT, None)],
                   tag=ptag + ".cl")
                op("scalar", lambda e: e.activation(A2, A2, AF.Sin),
                   rd=[(STAT, None)], wr=[(STAT, None)], tag=ptag + ".sin")
                op("vector", lambda e, _d=dst: e.tensor_tensor(_d, A2, J,
                   ALU.mult), rd=[(STAT, None)], wr=[(STAT, None)],
                   tag=ptag + ".ap")
            one_pass(CC.t[:, 5:6], None, dst_sin_ap, tag + ".s")
            one_pass(CC.t[:, 6:7], -0.5, dst_cos_ap, tag + ".c")

        for s in range(NSUP):
            for c in range(NCH):
                dma(sX1B, X1S2.t[:, c * 512:(c + 1) * 512],
                    hbm_x1[128 * c:128 * (c + 1), SUP * s:SUP * (s + 1)],
                    rd=[(HX1, s)], wr=[(X1S2, 0)], tag="b.x1ld")
            seal(X1S2, None, sX1B)
            ln_stats_apply(
                x_rd=lambda c, _s=s: (X1S2, 0),
                x_chunk_ap=lambda c, _s=s: X1S2.t[:, c * 512:(c + 1) * 512],
                dst_wr=lambda c, _s=s: (XN1, _s),
                dst_ap=lambda c, _s=s: XN1.t[:, c * T + SUP * _s:
                                            c * T + SUP * (_s + 1)],
                sq=SQB, sq_ap=lambda sl: SQB.t[:, sl * 512:(sl + 1) * 512],
                tag=f"bln.{s}")

        if os.environ.get("KDBG"):
            dma(sDBG, dbg_xn1[:, :], XN1.t[:, :],
                rd=[(XN1, list(range(NSUP)))], tag="dbg.xn1")
        alias_from(ROTA, ROT3)
        alias_from(ROTB, ROT3)
        op("scalar", lambda e: e.activation(STAT.t[:, 976:980], ETHC.t[:], AF.Exp,
           scale=DC.t[:, 3:4]), rd=[(ETHC, None), (DC, None)], wr=[(STAT, None)],
           tag="rb.thc")
        for c in range(NCH):
            dma(sWB, WVR.t[:, c * D:(c + 1) * D], w_l1v[128 * c:128 * (c + 1), :],
                wr=[(WVR, None)], tag="b.wv")
        seal(WVR, None, sWB)
        for tt in range(NTOK):
            s = tt // 4
            for h in range(2):
                bk = banks.take()
                for c in range(NCH):
                    mm(bk, bk[0][:, :],
                       XN1.t[:, c * T + TILE * tt:c * T + TILE * (tt + 1)],
                       WVR.t[:, c * D + 512 * h:c * D + 512 * (h + 1)],
                       start=(c == 0), stop=(c == NCH - 1),
                       rd=[(WVR, None), (XN1, s)], tag="b.vmm")
                vt = op("vector", lambda e, _b=bk[0], _t=tt, _h=h: e.tensor_copy(
                    VTB.t[:, _t * D + 512 * _h:_t * D + 512 * (_h + 1)], _b[:, :]),
                    wr=[(VTB, tt)], extra=bk[1].rd(), tag="b.vev")
                bk[1].did_read(vt)
        if os.environ.get("KDBG"):
            dma(sDBG, dbg_vtb[:, :], VTB.t[:, :],
                rd=[(VTB, list(range(NTOK)))], tag="dbg.vtb")
        alias_from(CS1, SQB, G16)
        for c in range(NCH):
            dma(sWB, WVR.t[:, c * D:(c + 1) * D], w_l1k[128 * c:128 * (c + 1), :],
                wr=[(WVR, None)], tag="b.wk")
        seal(WVR, None, sWB)
        for s in range(NSUP):
            op("vector", lambda e, _s=s: e.tensor_scalar_mul(
                STAT.t[:, 988:992], STAT.t[:, 976:980], float(512 * _s)),
                rd=[(STAT, None)], wr=[(STAT, None)], tag="b.angs")
            cw_col(STAT.t[:, 988:992], STAT.t[:, 960:964], STAT.t[:, 964:968],
                   f"b.cw{s}")
            for c in range(4):
                op("vector", lambda e, _c=c: e.tensor_scalar_mul(
                    ROT1.t[:], SINB1.t[:, 512 * _c:512 * (_c + 1)],
                    STAT.t[:, 960 + _c:961 + _c]),
                    rd=[(SINB1, None), (STAT, None)],
                    wr=[(ROT1, None)], tag="b.ts")
                op("scalar", lambda e, _c=c: e.activation(
                    ROT2.t[:], COSB1.t[:, 512 * _c:512 * (_c + 1)], AF.Identity,
                    scale=STAT.t[:, 964 + _c:965 + _c]),
                    rd=[(COSB1, None), (STAT, None)], wr=[(ROT2, None)],
                    tag="b.csa")
                op("vector", lambda e, _c=c: e.tensor_tensor(
                    CS1.t[:, 512 * _c:512 * (_c + 1)],
                    ROT2.t[:], ROT1.t[:], ALU.subtract),
                    rd=[(ROT2, None), (ROT1, None)],
                    wr=[(CS1, None)], tag="b.cs")
                op("vector", lambda e, _c=c: e.tensor_scalar_mul(
                    ROT1.t[:], COSB1.t[:, 512 * _c:512 * (_c + 1)],
                    STAT.t[:, 960 + _c:961 + _c]),
                    rd=[(COSB1, None), (STAT, None)],
                    wr=[(ROT1, None)], tag="b.tc")
                op("scalar", lambda e, _c=c: e.activation(
                    ROT2.t[:], SINB1.t[:, 512 * _c:512 * (_c + 1)], AF.Identity,
                    scale=STAT.t[:, 964 + _c:965 + _c]),
                    rd=[(SINB1, None), (STAT, None)], wr=[(ROT2, None)],
                    tag="b.ssa")
                op("vector", lambda e, _c=c: e.tensor_tensor(
                    SS1.t[:, 512 * _c:512 * (_c + 1)],
                    ROT2.t[:], ROT1.t[:], ALU.add),
                    rd=[(ROT2, None), (ROT1, None)],
                    wr=[(SS1, None)], tag="b.ss")
            for m in range(NCH):
                bk = banks.take()
                for c in range(NCH):
                    mm(bk, bk[0][:, :],
                       WVR.t[:, c * D + 128 * m:c * D + 128 * (m + 1)],
                       XN1.t[:, c * T + SUP * s:c * T + SUP * (s + 1)],
                       start=(c == 0), stop=(c == NCH - 1),
                       rd=[(WVR, None), (XN1, s)], tag="b.kmm")
                tk = op("vector", lambda e, _b=bk[0], _m=m: e.tensor_copy(
                    PRE1.t[:, _m * 512:(_m + 1) * 512], _b[:, :]),
                    wr=[(PRE1, None)], extra=bk[1].rd(), tag="b.kev")
                bk[1].did_read(tk)
            for c in range(4):
                pe = PRE1.t[:, c * 512:(c + 1) * 512]
                po = PRE1.t[:, (c + 4) * 512:(c + 5) * 512]
                cs_ap = CS1.t[:, 512 * c:512 * (c + 1)]
                ss_ap = SS1.t[:, 512 * c:512 * (c + 1)]
                op("vector", lambda e, _pe=pe, _cs=cs_ap: e.tensor_tensor(
                    ROTA.t[:], _pe, _cs, ALU.mult),
                    rd=[(PRE1, None), (CS1, None)], wr=[(ROTA, None)], tag="b.k1")
                op("vector", lambda e, _po=po, _ss=ss_ap: e.tensor_tensor(
                    ROTB.t[:], _po, _ss, ALU.mult),
                    rd=[(PRE1, None), (SS1, None)], wr=[(ROTB, None)], tag="b.k2")
                op("vector", lambda e, _c=c, _s=s: e.tensor_tensor(
                    KROT.t[:, _c * T + SUP * _s:_c * T + SUP * (_s + 1)],
                    ROTA.t[:], ROTB.t[:], ALU.add),
                    rd=[(ROTA, None), (ROTB, None)], wr=[(KROT, s)], tag="b.k3")
                op("vector", lambda e, _po=po, _cs=cs_ap: e.tensor_tensor(
                    ROTA.t[:], _po, _cs, ALU.mult),
                    rd=[(PRE1, None), (CS1, None)], wr=[(ROTA, None)], tag="b.k4")
                op("vector", lambda e, _pe=pe, _ss=ss_ap: e.tensor_tensor(
                    ROTB.t[:], _pe, _ss, ALU.mult),
                    rd=[(PRE1, None), (SS1, None)], wr=[(ROTB, None)], tag="b.k5")
                op("vector", lambda e, _c=c, _s=s: e.tensor_tensor(
                    KROT.t[:, (_c + 4) * T + SUP * _s:(_c + 4) * T + SUP * (_s + 1)],
                    ROTA.t[:], ROTB.t[:], ALU.subtract),
                    rd=[(ROTA, None), (ROTB, None)], wr=[(KROT, s)], tag="b.k6")
        if os.environ.get("KDBG"):
            dma(sDBG, dbg_kr[:, :], KROT.t[:, :],
                rd=[(KROT, list(range(NSUP)))], tag="dbg.kr")
        for c in range(NCH):
            dma(sWB, WQR1.t[:, c * D:(c + 1) * D], w_l1q[128 * c:128 * (c + 1), :],
                wr=[(WQR1, None)], tag="b.wq")
        seal(WQR1, None, sWB)
        alias_from(GRB, X1S2, H2R)
        alias_from(X1RB, X1S2, H2R)
        alias_from(QROT, X1S2, XTB)
        for s in range(NSUP):
            op("vector", lambda e, _s=s: e.tensor_scalar_mul(
                STAT.t[:, 988:992], STAT.t[:, 976:980], float(512 * _s)),
                rd=[(STAT, None)], wr=[(STAT, None)], tag="b.angs2")
            cw_col(STAT.t[:, 988:992], STAT.t[:, 960:964], STAT.t[:, 964:968],
                   f"b.cw2{s}")
            for c in range(4):
                op("vector", lambda e, _c=c: e.tensor_scalar_mul(
                    ROT1.t[:], SINB1.t[:, 512 * _c:512 * (_c + 1)],
                    STAT.t[:, 960 + _c:961 + _c]),
                    rd=[(SINB1, None), (STAT, None)],
                    wr=[(ROT1, None)], tag="b.ts2")
                op("scalar", lambda e, _c=c: e.activation(
                    ROT2.t[:], COSB1.t[:, 512 * _c:512 * (_c + 1)], AF.Identity,
                    scale=STAT.t[:, 964 + _c:965 + _c]),
                    rd=[(COSB1, None), (STAT, None)], wr=[(ROT2, None)],
                    tag="b.cs2a")
                op("vector", lambda e, _c=c: e.tensor_tensor(
                    CS1.t[:, 512 * _c:512 * (_c + 1)],
                    ROT2.t[:], ROT1.t[:], ALU.subtract),
                    rd=[(ROT2, None), (ROT1, None)],
                    wr=[(CS1, None)], tag="b.cs2")
                op("vector", lambda e, _c=c: e.tensor_scalar_mul(
                    ROT1.t[:], COSB1.t[:, 512 * _c:512 * (_c + 1)],
                    STAT.t[:, 960 + _c:961 + _c]),
                    rd=[(COSB1, None), (STAT, None)],
                    wr=[(ROT1, None)], tag="b.tc2")
                op("scalar", lambda e, _c=c: e.activation(
                    ROT2.t[:], SINB1.t[:, 512 * _c:512 * (_c + 1)], AF.Identity,
                    scale=STAT.t[:, 964 + _c:965 + _c]),
                    rd=[(SINB1, None), (STAT, None)], wr=[(ROT2, None)],
                    tag="b.ss2a")
                op("vector", lambda e, _c=c: e.tensor_tensor(
                    SS1.t[:, 512 * _c:512 * (_c + 1)],
                    ROT2.t[:], ROT1.t[:], ALU.add),
                    rd=[(ROT2, None), (ROT1, None)],
                    wr=[(SS1, None)], tag="b.ss2")
            alias_from(PRE1, PHAT, GRB, X1RB)
            for mh in range(2):
                bks = banks.take(4)
                for c in range(NCH):
                    for mi in range(4):
                        m = mh * 4 + mi
                        mm(bks[mi], bks[mi][0][:, :],
                           WQR1.t[:, c * D + 128 * m:c * D + 128 * (m + 1)],
                           XN1.t[:, c * T + SUP * s:c * T + SUP * (s + 1)],
                           start=(c == 0), stop=(c == NCH - 1),
                           rd=[(WQR1, None), (XN1, s)], tag="b.qmm")
                for mi in range(4):
                    m = mh * 4 + mi
                    tk = op("vector", lambda e, _b=bks[mi][0], _m=m:
                            e.tensor_scalar_mul(
                                PRE1.t[:, _m * 512:(_m + 1) * 512], _b[:, :],
                                0.03125),
                            wr=[(PRE1, None)], extra=bks[mi][1].rd(), tag="b.qev")
                    bks[mi][1].did_read(tk)
            for c in range(4):
                pe = PRE1.t[:, c * 512:(c + 1) * 512]
                po = PRE1.t[:, (c + 4) * 512:(c + 5) * 512]
                cs_ap = CS1.t[:, 512 * c:512 * (c + 1)]
                ss_ap = SS1.t[:, 512 * c:512 * (c + 1)]
                op("vector", lambda e, _pe=pe, _cs=cs_ap: e.tensor_tensor(
                    ROTA.t[:], _pe, _cs, ALU.mult),
                    rd=[(PRE1, None), (CS1, None)], wr=[(ROTA, None)], tag="b.q1")
                op("vector", lambda e, _po=po, _ss=ss_ap: e.tensor_tensor(
                    ROTB.t[:], _po, _ss, ALU.mult),
                    rd=[(PRE1, None), (SS1, None)], wr=[(ROTB, None)], tag="b.q2")
                op("vector", lambda e, _c=c: e.tensor_tensor(
                    QROT.t[:, _c * 512:(_c + 1) * 512],
                    ROTA.t[:], ROTB.t[:], ALU.add),
                    rd=[(ROTA, None), (ROTB, None)], wr=[(QROT, None)], tag="b.q3")
                op("vector", lambda e, _po=po, _cs=cs_ap: e.tensor_tensor(
                    ROTA.t[:], _po, _cs, ALU.mult),
                    rd=[(PRE1, None), (CS1, None)], wr=[(ROTA, None)], tag="b.q4")
                op("vector", lambda e, _pe=pe, _ss=ss_ap: e.tensor_tensor(
                    ROTB.t[:], _pe, _ss, ALU.mult),
                    rd=[(PRE1, None), (SS1, None)], wr=[(ROTB, None)], tag="b.q5")
                op("vector", lambda e, _c=c: e.tensor_tensor(
                    QROT.t[:, (_c + 4) * 512:(_c + 5) * 512],
                    ROTA.t[:], ROTB.t[:], ALU.subtract),
                    rd=[(ROTA, None), (ROTB, None)], wr=[(QROT, None)], tag="b.q6")
            if os.environ.get("KDBG") and s == 0:
                dma(sDBG, dbg_qr[:, :], QROT.t[:, :],
                    rd=[(QROT, None)], tag="dbg.qr")
            alias_from(PHAT, PRE1)
            alias_from(GRB, PRE1)
            alias_from(X1RB, PRE1)
            for h2 in range(2):
                for qt in range(2):
                    t_glob = 4 * s + 2 * h2 + qt
                    nkt = t_glob + 1
                    nkb = (nkt + 3) // 4
                    scbs = banks.take(nkb)
                    scbs = [scbs] if nkb == 1 else scbs
                    for kb in range(nkb):
                        n_in_kb = min(512, nkt * 128 - kb * 512)
                        for c in range(NCH):
                            mm(scbs[kb], scbs[kb][0][:, 0:n_in_kb],
                               QROT.t[:, c * 512 + 128 * (2 * h2 + qt):
                                      c * 512 + 128 * (2 * h2 + qt + 1)],
                               KROT.t[:, c * T + 512 * kb:c * T + 512 * kb + n_in_kb],
                               start=(c == 0), stop=(c == NCH - 1),
                               rd=[(QROT, None),
                                   (KROT, [min(3, (512 * kb) // 512 + jj)
                                           for jj in range(2)] if False else
                                    list(range(NSUP)))],
                               tag="b.sc")
                        if kb == nkb - 1:
                            d0 = nkt * 128 - 128 - kb * 512
                            tmk = op("vector", lambda e, _b=scbs[kb][0], _d0=d0:
                                     e.tensor_tensor(_b[:, _d0:_d0 + 128],
                                                     _b[:, _d0:_d0 + 128],
                                                     MASKL.t[:], ALU.add),
                                     rd=[(MASKL, None)], extra=scbs[kb][1].rd(),
                                     tag="b.mask")
                            scbs[kb][1].did_read(tmk)
                            scbs[kb][1].did_write(tmk)
                        n_in_kb2 = min(512, nkt * 128 - kb * 512)
                        tex = op("scalar", lambda e, _b=scbs[kb][0], _kb=kb,
                                 _n=n_in_kb2: e.activation(
                                     PHAT.t[:, _kb * 512:_kb * 512 + _n],
                                     _b[:, 0:_n], AF.Exp, bias=DC.t[:, 9:10],
                                     accum_out=STAT.t[:, 968 + _kb:969 + _kb]),
                                 rd=[(DC, None)], wr=[(PHAT, None), (STAT, None)],
                                 extra=scbs[kb][1].rd(), tag="b.exp")
                        scbs[kb][1].did_read(tex)
                    for kb in range(1, nkb):
                        op("vector", lambda e, _kb=kb: e.tensor_tensor(
                            STAT.t[:, 968:969], STAT.t[:, 968:969],
                            STAT.t[:, 968 + _kb:969 + _kb], ALU.add),
                            rd=[(STAT, None)], wr=[(STAT, None)], tag="b.rs")
                    op("vector", lambda e: e.reciprocal(STAT.t[:, 972:973],
                       STAT.t[:, 968:969]), rd=[(STAT, None)], wr=[(STAT, None)],
                       tag="b.rc")
                    for kb in range(nkb):
                        n_in_kb2 = min(512, nkt * 128 - kb * 512)
                        op("scalar", lambda e, _kb=kb, _n=n_in_kb2:
                           e.activation(
                               ROT1.t[:, 0:_n],
                               PHAT.t[:, _kb * 512:_kb * 512 + _n],
                               AF.Identity,
                               scale=STAT.t[:, 972:973]),
                           rd=[(PHAT, None), (STAT, None)], wr=[(ROT1, None)],
                           tag="b.nrm")
                        ntile = min(4, nkt - kb * 4)
                        bkt = banks.take()
                        pw = bkt[1].wr()
                        bkt[1].w[0] = []; bkt[1].r[0] = []
                        for i in range(ntile):
                            tmt = op("tensor", lambda e, _b=bkt[0], _i=i:
                                     e.transpose(_b[:, _i * 128:(_i + 1) * 128],
                                                 ROT1.t[:, _i * 128:(_i + 1) * 128],
                                                 ID_F.t[:]),
                                     rd=[(ROT1, None), (ID_F, None)],
                                     extra=(pw if i == 0 else ()), tag="b.tr")
                            bkt[1].did_write(tmt)
                        tev = op("vector", lambda e, _b=bkt[0], _kb=kb, _nt=ntile,
                                 _qt=qt: e.tensor_copy(
                                     bass.AP(PT1.t, (_kb * 4) * 256 + _qt * 128,
                                             [[4096, 128], [256, _nt], [1, 128]]),
                                     _b[:, 0:_nt * 128]),
                                 wr=[(PT1, None)], extra=bkt[1].rd(), tag="b.trev")
                        bkt[1].did_read(tev)
                _ktl = 4 * s + 2 * h2 + 1
                op("vector", lambda e, _k=_ktl: e.memset(
                    PT1.t[:, _k * 256:_k * 256 + 128], 0.0),
                    wr=[(PT1, None)], tag="b.ptz")
                for m in range(NCH):
                    dma(sGL, GRB.t[:, 0:256],
                        hbm_g[128 * m:128 * (m + 1),
                              SUP * s + 256 * h2:SUP * s + 256 * (h2 + 1)],
                        rd=[(HG, s)], wr=[(GRB, None)], tag="b.gld")
                    dma(sX1B, X1RB.t[:, 0:256],
                        hbm_x1[128 * m:128 * (m + 1),
                               SUP * s + 256 * h2:SUP * s + 256 * (h2 + 1)],
                        rd=[(HX1, s)], wr=[(X1RB, None)], tag="b.x1ld2")
                    ybk = banks.take()
                    nkt_half = 4 * s + 2 * h2 + 2
                    for kt in range(nkt_half):
                        mm(ybk, ybk[0][:, 0:256],
                           VTB.t[:, kt * D + 128 * m:kt * D + 128 * (m + 1)],
                           PT1.t[:, kt * 256:(kt + 1) * 256],
                           start=(kt == 0), stop=(kt == nkt_half - 1),
                           rd=[(VTB, kt), (PT1, None)], tag="b.av")
                    t1g = op("vector", lambda e, _b=ybk[0]:
                             e.tensor_tensor(PR.t[:, 0:256], _b[:, 0:256],
                                             GRB.t[:, 0:256],
                                             ALU.mult),
                             rd=[(GRB, None)], wr=[(PR, None)],
                             extra=ybk[1].rd(), tag="b.g1")
                    ybk[1].did_read(t1g)
                    op("vector", lambda e: e.tensor_tensor(
                        OUTS.t[:, 0:256], PR.t[:, 0:256],
                        X1RB.t[:, 0:256], ALU.add),
                        rd=[(PR, None), (X1RB, None)], wr=[(OUTS, None)],
                        tag="b.g2")
                    dma(sOUT, out_t[128 * m:128 * (m + 1),
                                    SUP * s + 256 * h2:SUP * s + 256 * (h2 + 1)],
                        OUTS.t[:, 0:256], rd=[(OUTS, None)], tag="b.out")

    final_toks = [("d", sOUT, sOUT.n)]
    p.emit("vector", lambda e: e.engine_nop(), waits=final_toks, tag="fin")
    p.build(nc, es)
    es.close()
    return nc


def kernel(**inputs):
    from concourse.bass_utils import run_bass_kernel_spmd
    shared, xs = host_prepare(inputs)
    nc = build_program()
    in_maps = []
    for b in range(NCORES):
        m = dict(shared)
        m["x"] = xs[b]
        in_maps.append(m)
    res = run_bass_kernel_spmd(nc, in_maps, core_ids=list(range(NCORES)))
    out = np.stack([np.asarray(r["out"], dtype=np.float32).T
                    for r in res.results], axis=0)
    return out
